# version 03
# baseline (speedup 1.0000x reference)
import numpy as np
import ml_dtypes
from contextlib import ExitStack
import concourse.bass as bass
import concourse.mybir as mybir
from concourse.bass_utils import run_bass_kernel_spmd
from concourse.alu_op_type import AluOpType as OP

F32 = mybir.dt.float32
BF16 = mybir.dt.bfloat16
I32 = mybir.dt.int32
AF = mybir.ActivationFunctionType
AX = mybir.AxisListType.X
BFNP = ml_dtypes.bfloat16

ENGINES = ("sync", "tensor", "vector", "scalar", "gpsimd")
NCORES = 8
T = 2048
D = 2048
NT = T // 128
NC = D // 128
HFF = 5632
NHC = HFF // 128
INVD = 1.0 / D
EPS = 1e-5
ASCALE = 0.08838834764831845
NEG = -1.0e30


class Sem:
    __slots__ = ("h", "n")

    def __init__(self, h):
        self.h = h
        self.n = 0


class Bld:

    def __init__(self, nc, es, tag, sems=None, sems_es=None):
        self.nc, self.es, self.tag = nc, es, tag
        self.q = {e: [] for e in ENGINES}
        self._sems = sems if sems is not None else {}
        self._sems_es = sems_es
        self._uid = 0

    def fresh(self):
        self._uid += 1
        return self._uid

    def sem(self, name):
        if name not in self._sems:
            es = self._sems_es or self.es
            self._sems[name] = Sem(es.enter_context(self.nc.semaphore(f"sem_{name}_{self.tag}")))
        return self._sems[name]

    def sb(self, name, shape, dt, es=None):
        return (es or self.es).enter_context(self.nc.sbuf_tensor(f"{self.tag}_{name}", shape, dt))

    def ps(self, name, shape, dt, es=None):
        return (es or self.es).enter_context(self.nc.psum_tensor(f"{self.tag}_{name}", shape, dt))

    def op(self, eng, fn, sem=None, k=1):
        if sem is None:
            self.q[eng].append(fn)
            return None
        sem.n += k
        v = sem.n
        self.q[eng].append(lambda e, fn=fn, s=sem.h, k=k: fn(e).then_inc(s, k))
        return v

    def dma(self, eng, dst, src, sem):
        return self.op(eng, lambda e, d=dst, s=src: e.dma_start(d, s), sem, 16)

    def wait(self, eng, sem, v):
        if v:
            self.q[eng].append(lambda e, s=sem.h, v=v: e.wait_ge(s, v))

    def drain(self, eng):
        self.q[eng].append(lambda e: e.drain())

    def flush(self):
        q = self.q
        self.q = {e: [] for e in ENGINES}
        with self.nc.Block() as blk:
            for name in ENGINES:
                ops = q[name]
                if ops:
                    getattr(blk, name)(lambda eng, ops=ops: [f(eng) for f in ops])


def emit_consts(b, A):
    c = {}
    c["idb"] = b.sb("idb", [128, 128], BF16)
    c["idf"] = b.sb("idf", [128, 128], F32)
    c["ones1"] = b.sb("ones1", [1, 128], F32)
    c["epsc"] = b.sb("epsc", [128, 1], F32)
    sDC = b.sem("dc")
    b.dma("sync", c["idb"][:], A["ident_bf"], sDC)
    b.dma("sync", c["idf"][:], A["ident_f32"], sDC)
    b.op("vector", lambda e: e.memset(c["ones1"][:], 1.0))
    b.op("vector", lambda e: e.memset(c["epsc"][:], EPS))
    b.wait("sync", sDC, sDC.n)
    return c


def emit_bcast_row(b, c, row_sb, dst_sb, pa, width=D):
    sT, sV = b.sem("t"), b.sem("v")
    for ob in range(width // 512):
        vprev = sV.n
        b.wait("tensor", sV, vprev)
        b.op("tensor", lambda e, ob=ob: e.matmul(
            pa[:], c["ones1"][:], row_sb[:, ob * 512:(ob + 1) * 512], start=True, stop=True), sT)
        b.wait("vector", sT, sT.n)
        b.op("vector", lambda e, ob=ob: e.tensor_copy(dst_sb[:, ob * 512:(ob + 1) * 512], pa[:]), sV)


def emit_ln_stats(b, c, x_ap, bn6, mv, sc, nb, double_ln):
    sV, sS = b.sem("v"), b.sem("s")
    for i in range(4):
        b.op("vector", lambda e, i=i: e.bn_stats(bn6[:, i * 6:(i + 1) * 6], x_ap[:, i * 512:(i + 1) * 512]))
    b.drain("vector")
    vs = b.op("vector", lambda e: e.bn_aggr(mv[:], bn6[:]), sV)
    b.wait("scalar", sV, vs)
    ss = b.op("scalar", lambda e: e.activation(sc[:, 2:3], mv[:, 1:2], AF.Sqrt, bias=c["epsc"][:], scale=1.0), sS)
    b.wait("vector", sS, ss)
    b.op("vector", lambda e: e.reciprocal(sc[:, 3:4], sc[:, 2:3]))
    b.drain("vector")
    if double_ln:
        b.op("vector", lambda e: e.tensor_scalar(sc[:, 4:5], sc[:, 3:4], sc[:, 3:4], None, OP.mult))
        b.drain("vector")
        vv = b.op("vector", lambda e: e.tensor_scalar(sc[:, 5:6], mv[:, 1:2], sc[:, 4:5], None, OP.mult), sV)
        b.wait("scalar", sV, vv)
        ss = b.op("scalar", lambda e: e.activation(sc[:, 6:7], sc[:, 5:6], AF.Sqrt, bias=c["epsc"][:], scale=1.0), sS)
        b.wait("vector", sS, ss)
        b.op("vector", lambda e: e.reciprocal(sc[:, 7:8], sc[:, 6:7]))
        b.drain("vector")
        b.op("vector", lambda e: e.tensor_scalar(sc[:, 0:1], sc[:, 3:4], sc[:, 7:8], None, OP.mult))
    else:
        b.op("vector", lambda e: e.tensor_copy(sc[:, 0:1], sc[:, 3:4]))
    b.drain("vector")
    vv = b.op("vector", lambda e: e.scalar_tensor_tensor(nb[:], mv[:, 0:1], -1.0, sc[:, 0:1], OP.mult, OP.mult), sV)
    return vv


def emit_p2_l0ffn(nc, es, A, xin, xout, tag="p2", dbg=None, ntiles=NT, sems=None, sems_es=None):
    b = Bld(nc, es, tag, sems=sems, sems_es=sems_es)
    sV, sS, sT, sG = b.sem("v"), b.sem("s"), b.sem("t"), b.sem("g")
    sDC, sDW, sDX, sDO = b.sem("dc"), b.sem("dw"), b.sem("dx"), b.sem("do")

    c = emit_consts(b, A)
    gam = b.sb("gam", [128, 16], F32)
    bet = b.sb("bet", [128, 16], F32)
    brow = b.sb("brow", [1, D], F32)
    bias_b = b.sb("bias_b", [128, D], F32)
    wq = b.sb("wq", [128, NC * D], BF16)
    pa = b.ps("pa", [128, 512], F32)
    pb = b.ps("pb", [128, 512], F32)
    pta = b.ps("pta", [128, 512], BF16)
    ptb = b.ps("ptb", [128, 512], BF16)

    b.dma("sync", gam[:], A["l0f_gam"], sDC)
    b.dma("sync", bet[:], A["l0f_bet"], sDC)
    b.dma("sync", brow[:], A["l0f_bias"], sDC)
    b.wait("sync", sDC, sDC.n)
    b.flush()

    emit_bcast_row(b, c, brow, bias_b, pa)
    b.flush()

    with ExitStack() as wes:
        stg = [b.sb(f"stg{i}", [128, D], F32, wes) for i in range(2)]
        acc = b.sb("acc", [128, 16], F32, wes)
        dump = b.sb("dump", [128, D], F32, wes)
        small = b.sb("small", [1, 4], F32, wes)
        rinv = b.sb("rinv", [128, 1], F32, wes)
        tmp_col = b.sb("tmpcol", [128, 1], F32, wes)
        t1 = b.sb("t1", [128, D], BF16, wes)
        t2 = b.sb("t2", [128, D], BF16, wes)
        for cc in range(NC):
            if cc >= 2:
                b.wait("sync", sS, sS.n)
            dv = b.dma("sync", stg[cc % 2][:], A["l0f_w"][cc * 128:(cc + 1) * 128, :], sDW)
            b.wait("scalar", sDW, dv)
            b.op("scalar", lambda e, cc=cc: e.activation(
                dump[:], stg[cc % 2][:], AF.Abs, accum_out=acc[:, cc:cc + 1]), sS)
        b.flush()
        b.op("vector", lambda e: e.reduce_sum(tmp_col[:], acc[:], axis=AX))
        b.drain("vector")
        vv = b.op("vector", lambda e: e.engine_nop(), sV)
        b.wait("tensor", sV, vv)
        tv = b.op("tensor", lambda e: e.transpose(pa[0:1, 0:128], tmp_col[:], c["idf"][:]), sT)
        b.wait("vector", sT, tv)
        b.op("vector", lambda e: e.reduce_sum(small[:, 0:1], pa[0:1, 0:128], axis=AX))
        b.drain("vector")
        b.op("vector", lambda e: e.tensor_scalar(small[:, 1:2], small[:, 0:1], 1.0 / (D * D), EPS, OP.mult, OP.add))
        b.drain("vector")
        vv = b.op("vector", lambda e: e.reciprocal(small[:, 2:3], small[:, 1:2]), sV)
        b.wait("tensor", sV, vv)
        tv = b.op("tensor", lambda e: e.matmul(pb[:, 0:1], c["ones1"][:], small[:, 2:3], start=True, stop=True), sT)
        b.wait("vector", sT, tv)
        b.op("vector", lambda e: e.tensor_copy(rinv[:], pb[:, 0:1]))
        b.drain("vector")
        b.flush()
        for cc in range(NC):
            if cc >= 2:
                b.wait("sync", sV, sV.n)
            dv = b.dma("sync", stg[cc % 2][:], A["l0f_w"][cc * 128:(cc + 1) * 128, :], sDW)
            b.wait("vector", sDW, dv)
            b.op("vector", lambda e, cc=cc: e.tensor_scalar(
                t1[:], stg[cc % 2][:], rinv[:], 0.5, OP.mult, OP.is_gt))
            b.drain("vector")
            b.op("vector", lambda e, cc=cc: e.tensor_scalar(
                t2[:], stg[cc % 2][:], rinv[:], -0.5, OP.mult, OP.is_lt))
            b.drain("vector")
            b.op("vector", lambda e, cc=cc: e.tensor_tensor(
                wq[:, cc * D:(cc + 1) * D], t1[:], t2[:], OP.subtract), sV)
        b.flush()
        if dbg is not None:
            b.dma("scalar", dbg["rinv"], rinv[:], sDO)
            b.dma("scalar", dbg["wq0"], wq[:, 0:D], sDO)
            b.wait("scalar", sDO, sDO.n)
            b.flush()

    xt = [b.sb(f"xt{i}", [128, D], F32) for i in range(2)]
    ub = [b.sb(f"u{i}", [128, D], BF16) for i in range(2)]
    xnt = [b.sb(f"xnt{i}", [128, D], BF16) for i in range(2)]
    xb = [b.sb(f"xb{i}", [128, D], F32) for i in range(2)]
    x2t = [b.sb(f"x2t{i}", [128, D], F32) for i in range(2)]
    bn6 = b.sb("bn6", [128, 24], F32)
    mv = b.sb("mv", [128, 2], F32)
    scc = b.sb("scc", [128, 8], F32)
    nb = b.sb("nb", [128, 1], F32)
    xt_free = [(0, 0, 0), (0, 0, 0)]
    u_free = [0, 0]
    xnt_free = [0, 0]
    x2_free = [0, 0]
    xb_free = [0, 0]
    pmm = [pa, pb]
    pmm_free = [0, 0]
    ptr = [pta, ptb]
    ptr_free = [0, 0]
    last_uv = 0

    for tt in range(ntiles):
        ib = tt % 2
        b.wait("sync", sG, xt_free[ib][0])
        b.wait("sync", sV, xt_free[ib][1])
        b.wait("sync", sS, xt_free[ib][2])
        dv = b.dma("sync", xt[ib][:], xin[tt * 128:(tt + 1) * 128, :], sDX)
        b.wait("vector", sDX, dv)
        b.wait("vector", sS, last_uv)
        coeff_v = emit_ln_stats(b, c, xt[ib], bn6, mv, scc, nb, double_ln=True)
        b.wait("scalar", sDX, dv)
        b.wait("scalar", sV, coeff_v)
        b.wait("scalar", sT, u_free[ib])
        uv = b.op("scalar", lambda e, ib=ib: e.activation(
            ub[ib][:], xt[ib][:], AF.Identity, bias=nb[:], scale=scc[:, 0:1]), sS)
        last_uv = uv
        b.wait("gpsimd", sDX, dv)
        b.wait("gpsimd", sV, xb_free[ib])
        gv = b.op("gpsimd", lambda e, ib=ib: e.tensor_tensor(
            xb[ib][:], xt[ib][:], bias_b[:], OP.add), sG)
        xt_free[ib] = (gv, coeff_v, uv)
        b.wait("tensor", sS, uv)
        b.wait("scalar", sT, xnt_free[ib])
        for grp in range(4):
            pg = grp % 2
            b.wait("tensor", sS, ptr_free[pg])
            for j in range(4):
                cc = grp * 4 + j
                tv = b.op("tensor", lambda e, cc=cc, pg=pg, j=j, ib=ib: e.transpose(
                    ptr[pg][:, j * 128:(j + 1) * 128], ub[ib][:, cc * 128:(cc + 1) * 128], c["idb"][:]),
                    sT if j == 3 else None)
            if grp == 3:
                u_free[ib] = sT.n
            b.wait("scalar", sT, sT.n)
            for j in range(4):
                cc = grp * 4 + j
                ev = b.op("scalar", lambda e, cc=cc, pg=pg, j=j, ib=ib: e.activation(
                    xnt[ib][:, cc * 128:(cc + 1) * 128], ptr[pg][:, j * 128:(j + 1) * 128],
                    AF.Identity, bias=bet[:, cc:cc + 1], scale=gam[:, cc:cc + 1]), sS)
            ptr_free[pg] = sS.n
        xnt_ready = sS.n
        b.wait("tensor", sS, xnt_ready)
        for ob in range(4):
            pg = ob % 2
            b.wait("tensor", sV, pmm_free[pg])
            for cc in range(NC):
                mmv = b.op("tensor", lambda e, cc=cc, ob=ob, pg=pg, ib=ib: e.matmul(
                    pmm[pg][:], xnt[ib][:, cc * 128:(cc + 1) * 128],
                    wq[:, cc * D + ob * 512: cc * D + (ob + 1) * 512],
                    start=(cc == 0), stop=(cc == NC - 1)), sT if cc == NC - 1 else None)
            b.wait("vector", sT, mmv)
            b.wait("vector", sG, xt_free[ib][0])
            b.wait("vector", sDO, x2_free[ib])
            pmm_free[pg] = b.op("vector", lambda e, ob=ob, pg=pg, ib=ib: e.tensor_tensor(
                x2t[ib][:, ob * 512:(ob + 1) * 512], pmm[pg][:], xb[ib][:, ob * 512:(ob + 1) * 512], OP.add), sV)
        xb_free[ib] = sV.n
        xnt_free[ib] = sT.n
        b.wait("scalar", sV, sV.n)
        x2_free[ib] = b.dma("scalar", xout[tt * 128:(tt + 1) * 128, :], x2t[ib][:], sDO)
        if dbg is not None and tt == 0:
            b.dma("scalar", dbg["u0"], ub[0][:], sDO)
            b.dma("scalar", dbg["xnt0"], xnt[0][:], sDO)
            b.dma("scalar", dbg["biasb"], bias_b[:], sDO)
            b.dma("scalar", dbg["coef"], scc[:], sDO)
            b.dma("scalar", dbg["nb"], nb[:], sDO)
    b.wait("scalar", sDO, sDO.n)
    b.flush()


def emit_ln_phase(b, c, xin_ap, col_of, xnt_sb, gamcol, betcol, double_ln, tiles, les, ptpair):
    sV, sS, sT, sDX = b.sem("v"), b.sem("s"), b.sem("t"), b.sem("dx")
    u_ = b.fresh()
    xt = [b.sb(f"lnxt{i}_{u_}", [128, D], F32, les) for i in range(2)]
    ub = [b.sb(f"lnu{i}_{u_}", [128, D], BF16, les) for i in range(2)]
    bn6 = b.sb(f"lnbn6_{u_}", [128, 24], F32, les)
    mv = b.sb(f"lnmv_{u_}", [128, 2], F32, les)
    scc = b.sb(f"lnscc_{u_}", [128, 8], F32, les)
    nb = b.sb(f"lnnb_{u_}", [128, 1], F32, les)
    ptr = ptpair
    ptr_free = [0, 0]
    xt_free = [(0, 0), (0, 0)]
    u_free = [0, 0]
    last_uv = 0
    for ti, tt in enumerate(tiles):
        ib = ti % 2
        b.wait("sync", sV, xt_free[ib][0])
        b.wait("sync", sS, xt_free[ib][1])
        dv = b.dma("sync", xt[ib][:], xin_ap[tt * 128:(tt + 1) * 128, :], sDX)
        b.wait("vector", sDX, dv)
        b.wait("vector", sS, last_uv)
        coeff_v = emit_ln_stats(b, c, xt[ib], bn6, mv, scc, nb, double_ln=double_ln)
        b.wait("scalar", sDX, dv)
        b.wait("scalar", sV, coeff_v)
        b.wait("scalar", sT, u_free[ib])
        uv = b.op("scalar", lambda e, ib=ib: e.activation(
            ub[ib][:], xt[ib][:], AF.Identity, bias=nb[:], scale=scc[:, 0:1]), sS)
        last_uv = uv
        xt_free[ib] = (coeff_v, uv)
        b.wait("tensor", sS, uv)
        for grp in range(4):
            pg = grp % 2
            b.wait("tensor", sS, ptr_free[pg])
            for j in range(4):
                cc = grp * 4 + j
                b.op("tensor", lambda e, cc=cc, pg=pg, j=j, ib=ib: e.transpose(
                    ptr[pg][:, j * 128:(j + 1) * 128], ub[ib][:, cc * 128:(cc + 1) * 128], c["idb"][:]),
                    sT if j == 3 else None)
            if grp == 3:
                u_free[ib] = sT.n
            b.wait("scalar", sT, sT.n)
            for j in range(4):
                cc = grp * 4 + j
                if gamcol is not None:
                    b.op("scalar", lambda e, cc=cc, pg=pg, j=j, tt=tt: e.activation(
                        xnt_sb[:, col_of(cc, tt):col_of(cc, tt) + 128], ptr[pg][:, j * 128:(j + 1) * 128],
                        AF.Identity, bias=betcol(cc), scale=gamcol(cc)), sS)
                else:
                    b.op("scalar", lambda e, cc=cc, pg=pg, j=j, tt=tt: e.activation(
                        xnt_sb[:, col_of(cc, tt):col_of(cc, tt) + 128], ptr[pg][:, j * 128:(j + 1) * 128],
                        AF.Copy), sS)
            ptr_free[pg] = sS.n
    return sS.n


def emit_attn_head(b, c, st, qh, kh, vtile, avt_slice, bufs, ready):
    sV, sS, sT = b.sem("v"), b.sem("s"), b.sem("t")
    ps, pta, ptb, pbc, pav = bufs["ps"], bufs["pta"], bufs["ptb"], bufs["pbc"], bufs["pav"]
    prow, ls, mx, nm, lc = bufs["prow"], bufs["ls"], bufs["mx"], bufs["nm"], bufs["lc"]
    lrow, invl, invlb, msk, PT = bufs["lrow"], bufs["invl"], bufs["invlb"], bufs["msk"], bufs["PT"]
    b.wait("tensor", sS, ready)
    for qb in range(4):
        nkb = qb + 1
        for t4 in range(4):
            tau = qb * 4 + t4
            for kb in range(nkb):
                b.wait("tensor", sS, st["psfree"][kb])
                b.wait("tensor", sV, st["lrow_ev"])
                st["mm"][kb] = b.op("tensor", lambda e, kb=kb, tau=tau: e.matmul(
                    ps[kb][:], qh[:, tau * 128:(tau + 1) * 128], kh[:, kb * 512:(kb + 1) * 512],
                    start=True, stop=True), sT)
            for kb in range(nkb):
                b.wait("vector", sT, st["mm"][kb])
                if kb == qb:
                    b.op("vector", lambda e, kb=kb, t4=t4: e.tensor_tensor(
                        ps[kb][:], ps[kb][:], msk[:, (3 - t4) * 128:(3 - t4) * 128 + 512], OP.add))
                    b.drain("vector")
                b.op("vector", lambda e, kb=kb: e.reduce_max(mx[:, kb:kb + 1], ps[kb][:], axis=AX))
            b.drain("vector")
            b.op("vector", lambda e, nkb=nkb: e.reduce_max(nm[:, 0:1], mx[:, 0:nkb], axis=AX))
            b.drain("vector")
            mv_ = b.op("vector", lambda e: e.tensor_scalar(nm[:, 1:2], nm[:, 0:1], -ASCALE, None, OP.mult), sV)
            b.wait("scalar", sV, mv_)
            b.wait("scalar", sT, st["prow_free"][t4])
            for kb in range(nkb):
                st["psfree"][kb] = b.op("scalar", lambda e, kb=kb, t4=t4: e.activation(
                    prow[t4][:, kb * 512:(kb + 1) * 512], ps[kb][:], AF.Exp,
                    bias=nm[:, 1:2], scale=ASCALE, accum_out=ls[:, kb:kb + 1]), sS)
            b.wait("vector", sS, st["psfree"][nkb - 1])
            lcv = b.op("vector", lambda e, t4=t4, nkb=nkb: e.reduce_sum(
                lc[:, t4:t4 + 1], ls[:, 0:nkb], axis=AX), sV if t4 == 3 else None)
            b.drain("vector")
        b.wait("tensor", sV, lcv)
        b.wait("tensor", sS, st["psfree"][0])
        for t4 in range(4):
            b.op("tensor", lambda e, t4=t4: e.transpose(
                ps[0][0:1, t4 * 128:(t4 + 1) * 128], lc[:, t4:t4 + 1], c["idf"][:]), sT if t4 == 3 else None)
        ltr = sT.n
        b.wait("vector", sT, ltr)
        b.op("vector", lambda e: e.tensor_copy(lrow[:], ps[0][0:1, 0:512]))
        b.drain("vector")
        st["lrow_ev"] = b.op("vector", lambda e: e.reciprocal(invl[:], lrow[:]), sV)
        b.wait("tensor", sV, st["lrow_ev"])
        b.wait("tensor", sV, st["invlb_ev"])
        bcv = b.op("tensor", lambda e: e.matmul(pbc[:], c["ones1"][:], invl[:], start=True, stop=True), sT)
        b.wait("vector", sT, bcv)
        st["invlb_ev"] = b.op("vector", lambda e: e.tensor_copy(invlb[:], pbc[:]), sV)
        nk = 4 * nkb
        ptp = [pta, ptb]
        for kk in range(nk):
            pg = kk % 2
            b.wait("tensor", sS, st["ptevac"][pg])
            for t4 in range(4):
                b.op("tensor", lambda e, kk=kk, t4=t4, pg=pg: e.transpose(
                    ptp[pg][:, t4 * 128:(t4 + 1) * 128], prow[t4][:, kk * 128:(kk + 1) * 128], c["idb"][:]),
                    sT if t4 == 3 else None)
            trv = sT.n
            if kk >= nk - 2:
                for t4 in range(4):
                    st["prow_free"][t4] = max(st["prow_free"][t4], trv)
            b.wait("scalar", sT, trv)
            b.wait("scalar", sT, st["PTmm"][pg])
            st["ptevac"][pg] = b.op("scalar", lambda e, pg=pg, kk=kk: e.activation(
                PT[pg][:], ptp[pg][:], AF.Copy), sS)
            b.wait("tensor", sS, st["ptevac"][pg])
            if kk == 0:
                b.wait("tensor", sV, st["pav_ev"])
            st["PTmm"][pg] = b.op("tensor", lambda e, kk=kk, pg=pg, nk=nk: e.matmul(
                pav[:], vtile(kk), PT[pg][:], start=(kk == 0), stop=(kk == nk - 1)), sT)
        b.wait("vector", sT, st["PTmm"][(nk - 1) % 2])
        st["pav_ev"] = b.op("vector", lambda e, qb=qb: e.tensor_tensor(
            avt_slice(qb), pav[:], invlb[:], OP.mult), sV)


def emit_p1_l0attn(nc, es, A, xin, xout, tag="p1", ntiles=NT, sems=None, sems_es=None):
    b = Bld(nc, es, tag, sems=sems, sems_es=sems_es)
    sV, sS, sT, sG = b.sem("v"), b.sem("s"), b.sem("t"), b.sem("g")
    sDC, sDX, sDO = b.sem("dc"), b.sem("dx"), b.sem("do")
    sWA = b.sem("wa")

    c = emit_consts(b, A)
    lng = b.sb("lng", [128, 16], F32)
    lnb = b.sb("lnb", [128, 16], F32)
    bqc = b.sb("bqc", [128, 16], F32)
    bkc = b.sb("bkc", [128, 16], F32)
    bvb = b.sb("bvb", [128, 16], BF16)
    msk = b.sb("msk", [128, 896], F32)
    for dst, src_ in [(lng, A["l0_lng"]), (lnb, A["l0_lnb"]), (bqc, A["l0_bq"]),
                      (bkc, A["l0_bk"]), (msk, A["masks"])]:
        b.dma("sync", dst[:], src_, sDC)
    b.dma("gpsimd", bvb[:], A["l0_bv"], sDC)
    b.wait("sync", sDC, sDC.n)
    b.flush()

    ps = [b.ps(f"ps{i}", [128, 512], F32) for i in range(4)]
    pta = b.ps("pta", [128, 512], BF16)
    ptb = b.ps("ptb", [128, 512], BF16)
    pbc = b.ps("pbc", [128, 512], F32)
    pav = b.ps("pav", [128, 512], F32)

    AVT = b.sb("avt", [128, NC * T], BF16)

    st = dict(psfree=[0, 0, 0, 0], mm=[0, 0, 0, 0], prow_free=[0, 0, 0, 0],
              ptevac=[0, 0], PTmm=[0, 0], pav_ev=0, lrow_ev=0, invlb_ev=0)

    with ExitStack() as xes:
        xnt = b.sb("xnt", [128, NC * T], BF16, xes)
        with ExitStack() as les:
            xnt_done = emit_ln_phase(
                b, c, xin, lambda cc, tt: cc * T + tt * 128, xnt,
                lambda cc: lng[:, cc:cc + 1], lambda cc: lnb[:, cc:cc + 1],
                True, range(ntiles), les, (pta, ptb))
            b.flush()

        with ExitStack() as aes:
            prow = [b.sb(f"prow{i}", [128, T], BF16, aes) for i in range(4)]
            ls = b.sb("ls", [128, 4], F32, aes)
            mx = b.sb("mx", [128, 4], F32, aes)
            nm = b.sb("nm", [128, 2], F32, aes)
            lc = b.sb("lc", [128, 4], F32, aes)
            lrowb = b.sb("lrowb", [1, 1024], F32, aes)
            lrow, invl = lrowb[:, 0:512], lrowb[:, 512:1024]
            invlb = b.sb("invlb", [128, 512], F32, aes)
            PT = [b.sb(f"PT{i}", [128, 512], BF16, aes) for i in range(2)]
            V_sb = b.sb("vsb", [128, 16 * 512], BF16, aes)
            qh = b.sb("qh", [128, T], BF16, aes)
            kh = b.sb("kh", [128, T], BF16, aes)
            wq_b = b.sb("wqb", [128, D], BF16, aes)
            wk_b = b.sb("wkb", [128, D], BF16, aes)
            bufs = dict(ps=ps, pta=pta, ptb=ptb, pbc=pbc, pav=pav, prow=prow, ls=ls,
                        mx=mx, nm=nm, lc=lc, lrow=lrow, invl=invl, invlb=invlb,
                        msk=msk, PT=PT)

            qk_done = 0
            for g in range(4):
                with ExitStack() as ges:
                    wvg = b.sb(f"wvg{g}", [128, 8192], BF16, ges)
                    dvv = b.dma("gpsimd", wvg[:], A["l0_wv"][g * 128:(g + 1) * 128, :], sWA)
                    b.wait("tensor", sWA, dvv)
                    b.wait("tensor", sS, xnt_done)
                    b.wait("tensor", sV, st["lrow_ev"])
                    vev = [st["psfree"][0], st["psfree"][1]]
                    for tt in range(16):
                        pg = tt % 2
                        b.wait("tensor", sS, vev[pg])
                        mmv = None
                        for cc in range(NC):
                            mmv = b.op("tensor", lambda e, cc=cc, tt=tt, pg=pg, wvg=wvg: e.matmul(
                                ps[pg][:], xnt[:, cc * T + tt * 128: cc * T + (tt + 1) * 128],
                                wvg[:, cc * 512:(cc + 1) * 512],
                                start=(cc == 0), stop=(cc == NC - 1)), sT if cc == NC - 1 else None)
                        b.wait("scalar", sT, mmv)
                        vev[pg] = b.op("scalar", lambda e, tt=tt, pg=pg: e.activation(
                            V_sb[:, tt * 512:(tt + 1) * 512], ps[pg][:], AF.Copy), sS)
                    st["psfree"][0], st["psfree"][1] = vev
                    vprod_ev = sS.n
                    for hh in range(4):
                        h = g * 4 + hh
                        b.wait("gpsimd", sT, qk_done)
                        dq = b.dma("gpsimd", wq_b[:], A["l0_wq"][h * 128:(h + 1) * 128, :], sWA)
                        dk = b.dma("gpsimd", wk_b[:], A["l0_wk"][h * 128:(h + 1) * 128, :], sWA)
                        b.wait("tensor", sWA, dq)
                        kv = None
                        for (which, wbuf, dst, bias) in (("q", wq_b, qh, bqc), ("k", wk_b, kh, bkc)):
                            if which == "k":
                                b.wait("tensor", sWA, dk)
                            for tb in range(4):
                                pg = tb % 2
                                b.wait("tensor", sS, st["psfree"][pg])
                                mmv = None
                                for cc in range(NC):
                                    mmv = b.op("tensor", lambda e, cc=cc, tb=tb, pg=pg, wbuf=wbuf: e.matmul(
                                        ps[pg][:], wbuf[:, cc * 128:(cc + 1) * 128],
                                        xnt[:, cc * T + tb * 512: cc * T + tb * 512 + 512],
                                        start=(cc == 0), stop=(cc == NC - 1)), sT if cc == NC - 1 else None)
                                b.wait("scalar", sT, mmv)
                                st["psfree"][pg] = b.op("scalar", lambda e, tb=tb, pg=pg, dst=dst, bias=bias, h=h:
                                    e.activation(dst[:, tb * 512:(tb + 1) * 512], ps[pg][:], AF.Identity,
                                                 bias=bias[:, h:h + 1], scale=1.0), sS)
                            if which == "k":
                                kv = sS.n
                        qk_done = sT.n
                        emit_attn_head(
                            b, c, st, qh, kh,
                            lambda kk, hh=hh, V_sb=V_sb: V_sb[:, kk * 512 + hh * 128: kk * 512 + hh * 128 + 128],
                            lambda qb, h=h: AVT[:, h * T + qb * 512: h * T + (qb + 1) * 512],
                            bufs, max(kv, vprod_ev))
                    b.flush()

    with ExitStack() as oes:
        wo = b.sb("wo", [128, NC * D], BF16, oes)
        borow = b.sb("borow", [1, D], F32, oes)
        boeff = b.sb("boeff", [1, D], F32, oes)
        bob = b.sb("bob", [128, D], F32, oes)
        xt = [b.sb(f"oxt{i}", [128, D], F32, oes) for i in range(2)]
        xbb = [b.sb(f"oxb{i}", [128, D], F32, oes) for i in range(2)]
        x1t = [b.sb(f"ox1{i}", [128, D], F32, oes) for i in range(2)]
        b.dma("sync", borow[:], A["l0_bo"], sDC)
        b.wait("sync", sDC, sDC.n)
        for cc in range(NC):
            b.dma("gpsimd", wo[:, cc * D:(cc + 1) * D], A["l0_wo"][cc * 128:(cc + 1) * 128, :], sWA)
        b.wait("tensor", sWA, sWA.n)
        b.flush()
        for ob in range(4):
            mmv = None
            for cc in range(NC):
                mmv = b.op("tensor", lambda e, cc=cc, ob=ob: e.matmul(
                    pbc[0:1, 0:512], bvb[:, cc:cc + 1], wo[:, cc * D + ob * 512: cc * D + (ob + 1) * 512],
                    start=(cc == 0), stop=(cc == NC - 1)), sT if cc == NC - 1 else None)
            b.wait("vector", sT, mmv)
            ev = b.op("vector", lambda e, ob=ob: e.tensor_tensor(
                boeff[:, ob * 512:(ob + 1) * 512], pbc[0:1, 0:512], borow[:, ob * 512:(ob + 1) * 512],
                OP.add), sV)
            b.wait("tensor", sV, ev)
        b.flush()
        emit_bcast_row(b, c, boeff, bob, pbc)
        b.flush()
        xt_free = [0, 0]
        x1_free = [0, 0]
        xb_free = [0, 0]
        pmm_free = [0, 0]
        for tt in range(ntiles):
            ib = tt % 2
            b.wait("sync", sG, xt_free[ib])
            dv = b.dma("sync", xt[ib][:], xin[tt * 128:(tt + 1) * 128, :], sDX)
            b.wait("gpsimd", sDX, dv)
            b.wait("gpsimd", sV, xb_free[ib])
            gv = b.op("gpsimd", lambda e, ib=ib: e.tensor_tensor(
                xbb[ib][:], xt[ib][:], bob[:], OP.add), sG)
            xt_free[ib] = gv
            for ob in range(4):
                pg = ob % 2
                b.wait("tensor", sV, pmm_free[pg])
                mmv = None
                for cc in range(NC):
                    mmv = b.op("tensor", lambda e, cc=cc, tt=tt, ob=ob, pg=pg: e.matmul(
                        ps[pg][:], AVT[:, cc * T + tt * 128: cc * T + (tt + 1) * 128],
                        wo[:, cc * D + ob * 512: cc * D + (ob + 1) * 512],
                        start=(cc == 0), stop=(cc == NC - 1)), sT if cc == NC - 1 else None)
                b.wait("vector", sT, mmv)
                b.wait("vector", sG, gv)
                b.wait("vector", sDO, x1_free[ib])
                pmm_free[pg] = b.op("vector", lambda e, ob=ob, pg=pg, ib=ib: e.tensor_tensor(
                    x1t[ib][:, ob * 512:(ob + 1) * 512], ps[pg][:], xbb[ib][:, ob * 512:(ob + 1) * 512],
                    OP.add), sV)
            xb_free[ib] = sV.n
            b.wait("scalar", sV, sV.n)
            x1_free[ib] = b.dma("scalar", xout[tt * 128:(tt + 1) * 128, :], x1t[ib][:], sDO)
        b.wait("scalar", sDO, sDO.n)
        b.flush()


def emit_p3_l1attn(nc, es, A, xin, xout, tag="p3", ntiles=NT, sems=None, sems_es=None):
    b = Bld(nc, es, tag, sems=sems, sems_es=sems_es)
    sV, sS, sT, sG = b.sem("v"), b.sem("s"), b.sem("t"), b.sem("g")
    sDC, sDX, sDO = b.sem("dc"), b.sem("dx"), b.sem("do")
    sWA = b.sem("wa")

    c = emit_consts(b, A)
    msk = b.sb("msk", [128, 896], F32)
    b.dma("sync", msk[:], A["masks"], sDC)
    b.wait("sync", sDC, sDC.n)
    b.flush()

    ps = [b.ps(f"ps{i}", [128, 512], F32) for i in range(4)]
    pta = b.ps("pta", [128, 512], BF16)
    ptb = b.ps("ptb", [128, 512], BF16)
    pbc = b.ps("pbc", [128, 512], F32)
    pav = b.ps("pav", [128, 512], F32)

    AVT = b.sb("avt", [128, NC * T], BF16)

    st = dict(psfree=[0, 0, 0, 0], mm=[0, 0, 0, 0], prow_free=[0, 0, 0, 0],
              ptevac=[0, 0], PTmm=[0, 0], pav_ev=0, lrow_ev=0, invlb_ev=0)

    with ExitStack() as aes:
        prow = [b.sb(f"prow{i}", [128, T], BF16, aes) for i in range(4)]
        ls = b.sb("ls", [128, 4], F32, aes)
        mx = b.sb("mx", [128, 4], F32, aes)
        nm = b.sb("nm", [128, 2], F32, aes)
        lc = b.sb("lc", [128, 4], F32, aes)
        lrowb = b.sb("lrowb", [1, 1024], F32, aes)
        lrow, invl = lrowb[:, 0:512], lrowb[:, 512:1024]
        invlb = b.sb("invlb", [128, 512], F32, aes)
        PT = [b.sb(f"PT{i}", [128, 512], BF16, aes) for i in range(2)]
        bufs = dict(ps=ps, pta=pta, ptb=ptb, pbc=pbc, pav=pav, prow=prow, ls=ls,
                    mx=mx, nm=nm, lc=lc, lrow=lrow, invl=invl, invlb=invlb,
                    msk=msk, PT=PT)
        with ExitStack() as xes:
            xnt = b.sb("xnt", [128, NC * T], BF16, xes)
            with ExitStack() as les:
                xnt_done = emit_ln_phase(
                    b, c, xin, lambda cc, tt: cc * T + tt * 128, xnt,
                    None, None, False, range(ntiles), les, (pta, ptb))
                b.flush()

            wdone = 0
            for g in range(4):
                with ExitStack() as ges:
                    Qg = b.sb(f"qg{g}", [128, 4 * T], BF16, ges)
                    Kg = b.sb(f"kg{g}", [128, T], BF16, ges)
                    Vt = b.sb(f"vt{g}", [128, T], BF16, ges)
                    Vg = b.sb(f"vg{g}", [128, T], BF16, ges)
                    wb = [b.sb(f"wb{g}{i}", [128, D], BF16, ges) for i in range(2)]
                    vtv = 0
                    for r in range(6):
                        pc = 6 * g + r
                        ib = r % 2
                        b.wait("gpsimd", sT, wdone if r < 2 else wdone)
                        dq = b.dma("gpsimd", wb[ib][:], A["l1_wqkv"][pc * 128:(pc + 1) * 128, :], sWA)
                        b.wait("tensor", sWA, dq)
                        dst = (Qg[:, r * T:(r + 1) * T] if r < 4 else (Kg[:] if r == 4 else Vt[:]))
                        for tb in range(4):
                            pg = tb % 2
                            b.wait("tensor", sS, st["psfree"][pg])
                            b.wait("tensor", sV, st["lrow_ev"])
                            mmv = None
                            for cc in range(NC):
                                mmv = b.op("tensor", lambda e, cc=cc, tb=tb, pg=pg, ib=ib: e.matmul(
                                    ps[pg][:], wb[ib][:, cc * 128:(cc + 1) * 128],
                                    xnt[:, cc * T + tb * 512: cc * T + tb * 512 + 512],
                                    start=(cc == 0), stop=(cc == NC - 1)), sT if cc == NC - 1 else None)
                            wdone = mmv
                            b.wait("scalar", sT, mmv)
                            st["psfree"][pg] = b.op("scalar", lambda e, tb=tb, pg=pg, dst=dst: e.activation(
                                dst[:, tb * 512:(tb + 1) * 512], ps[pg][:], AF.Copy), sS)
                        if r == 5:
                            vtv = sS.n
                    b.wait("tensor", sS, vtv)
                    ptp = [pta, ptb]
                    for g4 in range(4):
                        pg = g4 % 2
                        b.wait("tensor", sS, st["ptevac"][pg])
                        for j in range(4):
                            kk = g4 * 4 + j
                            b.op("tensor", lambda e, kk=kk, j=j, pg=pg: e.transpose(
                                ptp[pg][:, j * 128:(j + 1) * 128], Vt[:, kk * 128:(kk + 1) * 128],
                                c["idb"][:]), sT if j == 3 else None)
                        b.wait("scalar", sT, sT.n)
                        st["ptevac"][pg] = b.op("scalar", lambda e, g4=g4, pg=pg: e.activation(
                            Vg[:, g4 * 512:(g4 + 1) * 512], ptp[pg][:], AF.Copy), sS)
                    vgv = sS.n
                    for j in range(4):
                        h = g * 4 + j
                        emit_attn_head(
                            b, c, st, Qg[:, j * T:(j + 1) * T], Kg,
                            lambda kk, Vg=Vg: Vg[:, kk * 128:(kk + 1) * 128],
                            lambda qb, h=h: AVT[:, h * T + qb * 512: h * T + (qb + 1) * 512],
                            bufs, vgv)
                    b.flush()

    with ExitStack() as oes:
        wo = b.sb("wo", [128, NC * D], BF16, oes)
        xt = [b.sb(f"oxt{i}", [128, D], F32, oes) for i in range(2)]
        x1t = [b.sb(f"ox1{i}", [128, D], F32, oes) for i in range(2)]
        for cc in range(NC):
            b.dma("gpsimd", wo[:, cc * D:(cc + 1) * D], A["l1_wproj"][cc * 128:(cc + 1) * 128, :], sWA)
        b.wait("tensor", sWA, sWA.n)
        b.flush()
        xt_free = [0, 0]
        x1_free = [0, 0]
        pmm_free = [0, 0]
        for tt in range(ntiles):
            ib = tt % 2
            b.wait("sync", sV, xt_free[ib])
            dv = b.dma("sync", xt[ib][:], xin[tt * 128:(tt + 1) * 128, :], sDX)
            for ob in range(4):
                pg = ob % 2
                b.wait("tensor", sV, pmm_free[pg])
                mmv = None
                for cc in range(NC):
                    mmv = b.op("tensor", lambda e, cc=cc, tt=tt, ob=ob, pg=pg: e.matmul(
                        ps[pg][:], AVT[:, cc * T + tt * 128: cc * T + (tt + 1) * 128],
                        wo[:, cc * D + ob * 512: cc * D + (ob + 1) * 512],
                        start=(cc == 0), stop=(cc == NC - 1)), sT if cc == NC - 1 else None)
                b.wait("vector", sT, mmv)
                b.wait("vector", sDX, dv)
                b.wait("vector", sDO, x1_free[ib])
                pmm_free[pg] = b.op("vector", lambda e, ob=ob, pg=pg, ib=ib: e.tensor_tensor(
                    x1t[ib][:, ob * 512:(ob + 1) * 512], ps[pg][:], xt[ib][:, ob * 512:(ob + 1) * 512],
                    OP.add), sV)
            xt_free[ib] = sV.n
            b.wait("scalar", sV, sV.n)
            x1_free[ib] = b.dma("scalar", xout[tt * 128:(tt + 1) * 128, :], x1t[ib][:], sDO)
        b.wait("scalar", sDO, sDO.n)
        b.flush()


def emit_p4_l1ffn(nc, es, A, xin, xout, ddram, tag="p4", sems=None, sems_es=None):
    b = Bld(nc, es, tag, sems=sems, sems_es=sems_es)
    sV, sS, sT, sG = b.sem("v"), b.sem("s"), b.sem("t"), b.sem("g")
    sDC, sDX, sDO, sWA, sWB = b.sem("dc"), b.sem("dx"), b.sem("do"), b.sem("wa"), b.sem("wb")
    TH = 1024

    c = emit_consts(b, A)
    sgc = b.sb("sgc", [128, 44], F32)
    suc = b.sb("suc", [128, 44], F32)
    sdrow = b.sb("sdrow", [1, D], F32)
    sdb = b.sb("sdb", [128, D], F32)
    onesc = b.sb("onesc", [128, 1], BF16)
    b.dma("sync", sgc[:], A["l1_sg"], sDC)
    b.dma("sync", suc[:], A["l1_su"], sDC)
    b.dma("sync", sdrow[:], A["l1_sd"], sDC)
    b.op("vector", lambda e: e.memset(onesc[:], 1.0))
    b.wait("sync", sDC, sDC.n)
    b.flush()

    pta = b.ps("pta", [128, 512], BF16)
    ptb = b.ps("ptb", [128, 512], BF16)
    ps = [b.ps(f"ps{i}", [128, 512], F32) for i in range(4)]
    pgu = [b.ps(f"pgu{i}", [1, 512], F32) for i in range(2)]

    emit_bcast_row(b, c, sdrow, sdb, ps[0])
    b.flush()

    for half in range(2):
        off = half * 8
        with ExitStack() as hes:
            gu = b.sb(f"gu{half}", [128, 44 * TH], BF16, hes)
            gurow = b.sb(f"gurow{half}", [1, TH], F32, hes)
            m128g = b.sb(f"m128g{half}", [128, 8], F32, hes)
            with ExitStack() as xes:
                xnt = b.sb(f"xnt{half}", [128, NC * TH], BF16, xes)
                with ExitStack() as les:
                    emit_ln_phase(
                        b, c, xin, lambda cc, tt: cc * TH + (tt - off) * 128, xnt,
                        None, None, False, range(off, off + 8), les, (pta, ptb))
                    b.flush()
                with ExitStack() as ges:
                    wstg = [b.sb(f"wsg{half}{i}", [128, D], BF16, ges) for i in range(2)]
                    wstu = [b.sb(f"wsu{half}{i}", [128, D], BF16, ges) for i in range(2)]
                    wgb = [b.sb(f"wgb{half}{i}", [128, D], BF16, ges) for i in range(2)]
                    wub = [b.sb(f"wub{half}{i}", [128, D], BF16, ges) for i in range(2)]
                    grow = b.sb(f"grow{half}", [128, TH], BF16, ges)
                    urow = b.sb(f"urow{half}", [128, TH], BF16, ges)
                    stg_free = [(0, 0), (0, 0)]
                    wgb_free = [0, 0]
                    ev_free = [0, 0, 0, 0]
                    psfree = [0, 0, 0, 0]
                    growv = [0, 0]
                    for hc in range(NHC):
                        ib = hc % 2
                        b.wait("gpsimd", sV, stg_free[ib][0])
                        dg = b.dma("gpsimd", wstg[ib][:], A["l1_wg"][hc * 128:(hc + 1) * 128, :], sWA)
                        du = b.dma("gpsimd", wstu[ib][:], A["l1_wu"][hc * 128:(hc + 1) * 128, :], sWA)
                        b.wait("vector", sWA, dg)
                        b.wait("vector", sT, wgb_free[ib])
                        b.op("vector", lambda e, ib=ib: e.tensor_scalar(
                            wgb[ib][:], wstg[ib][:], 128.0, None, OP.subtract))
                        b.drain("vector")
                        b.wait("vector", sWA, du)
                        uq = b.op("vector", lambda e, ib=ib: e.tensor_scalar(
                            wub[ib][:], wstu[ib][:], 128.0, None, OP.subtract), sV)
                        stg_free[ib] = (uq, uq)
                        b.wait("tensor", sV, uq)
                        for which, base, wsrc, scol, func in (
                                ("g", 0, wgb[ib], sgc, AF.Silu), ("u", 2, wub[ib], suc, AF.Identity)):
                            for tb in range(2):
                                pg = base + tb
                                b.wait("tensor", sS, psfree[pg])
                                mmv = None
                                for cc in range(NC):
                                    mmv = b.op("tensor", lambda e, cc=cc, tb=tb, pg=pg, wsrc=wsrc: e.matmul(
                                        ps[pg][:], wsrc[:, cc * 128:(cc + 1) * 128],
                                        xnt[:, cc * TH + tb * 512: cc * TH + tb * 512 + 512],
                                        start=(cc == 0), stop=(cc == NC - 1)), sT if cc == NC - 1 else None)
                                b.wait("scalar", sT, mmv)
                                dst = grow if which == "g" else urow
                                b.wait("scalar", sV, growv[tb])
                                psfree[pg] = b.op("scalar", lambda e, tb=tb, pg=pg, dst=dst, scol=scol, hc=hc, func=func:
                                    e.activation(dst[:, tb * 512:(tb + 1) * 512], ps[pg][:], func,
                                                 bias=0.0, scale=scol[:, hc:hc + 1]), sS)
                        wgb_free[ib] = sT.n
                        for tb in range(2):
                            b.wait("vector", sS, psfree[2 + tb])
                            growv[tb] = b.op("vector", lambda e, hc=hc, tb=tb: e.tensor_tensor(
                                gu[:, hc * TH + tb * 512: hc * TH + tb * 512 + 512],
                                grow[:, tb * 512:(tb + 1) * 512], urow[:, tb * 512:(tb + 1) * 512],
                                OP.mult), sV)
                            b.wait("tensor", sV, growv[tb])
                            b.op("tensor", lambda e, hc=hc, tb=tb: e.matmul(
                                pgu[tb][:], onesc[:], gu[:, hc * TH + tb * 512: hc * TH + tb * 512 + 512],
                                start=(hc == 0), stop=(hc == NHC - 1)), sT if hc == NHC - 1 else None)
                    gusum_mm = sT.n
                    b.flush()
            b.wait("vector", sT, gusum_mm)
            for tb in range(2):
                b.op("vector", lambda e, tb=tb: e.tensor_copy(gurow[:, tb * 512:(tb + 1) * 512], pgu[tb][:]))
            b.drain("vector")
            grv = b.op("vector", lambda e: e.engine_nop(), sV)
            b.wait("tensor", sV, grv)
            for j in range(8):
                b.op("tensor", lambda e, j=j: e.matmul(
                    ps[0][:, j:j + 1], gurow[:, j * 128:(j + 1) * 128], c["ones1"][0:1, 0:1],
                    start=True, stop=True), sT if j == 7 else None)
            gtv = sT.n
            b.wait("vector", sT, gtv)
            b.op("vector", lambda e: e.tensor_scalar(m128g[:], ps[0][:, 0:8], -128.0, None, OP.mult))
            b.drain("vector")
            m128v = b.op("vector", lambda e: e.engine_nop(), sV)
            b.flush()

            with ExitStack() as des:
                wdb = [b.sb(f"wdb{half}{i}", [128, 44 * 512], BF16, des) for i in range(2)]
                tmp = [b.sb(f"dtmp{half}{i}", [128, 512], F32, des) for i in range(2)]
                wd_used = [0, 0]
                tmp_free = [0, 0]
                dmad = []
                for db in range(4):
                    ibw = db % 2
                    b.wait("gpsimd", sT, wd_used[ibw])
                    dws = []
                    for q in range(4):
                        lo = q * 11
                        dws.append(b.dma(
                            "gpsimd",
                            wdb[ibw][:].rearrange("p (hc j) -> p hc j", j=512)[:, lo:lo + 11, :],
                            A["l1_wd"][db * 5632 + lo * 128: db * 5632 + (lo + 11) * 128, :]
                            .rearrange("(hc p) j -> p hc j", p=128),
                            sWB))
                    for tt in range(off, off + 8):
                        ti = tt - off
                        pg = 2 + ti % 2
                        b.wait("tensor", sWB, dws[-1])
                        if len(dmad) >= 2:
                            b.wait("tensor", sV, dmad[-2][1])
                        b.wait("tensor", sV, grv)
                        mmv = None
                        for hc in range(NHC):
                            mmv = b.op("tensor", lambda e, hc=hc, ti=ti, pg=pg, ibw=ibw: e.matmul(
                                ps[pg][:], gu[:, hc * TH + ti * 128: hc * TH + (ti + 1) * 128],
                                wdb[ibw][:, hc * 512:(hc + 1) * 512],
                                start=(hc == 0), stop=(hc == NHC - 1)), sT if hc == NHC - 1 else None)
                        wd_used[ibw] = mmv
                        ibt = ti % 2
                        b.wait("vector", sT, mmv)
                        b.wait("vector", sV, m128v)
                        b.wait("vector", sDO, tmp_free[ibt])
                        vv = b.op("vector", lambda e, ti=ti, pg=pg, ibt=ibt, db=db: e.scalar_tensor_tensor(
                            tmp[ibt][:], ps[pg][:], m128g[:, ti:ti + 1], sdb[:, db * 512:(db + 1) * 512],
                            OP.add, OP.mult), sV)
                        dmad.append((None, vv))
                        b.wait("scalar", sV, vv)
                        tmp_free[ibt] = b.dma(
                            "scalar", ddram[tt * 128:(tt + 1) * 128, db * 512:(db + 1) * 512],
                            tmp[ibt][:], sDO)
                b.flush()

        with ExitStack() as fes:
            dxt = [b.sb(f"fx{half}{i}", [128, D], F32, fes) for i in range(2)]
            ddt = [b.sb(f"fd{half}{i}", [128, D], F32, fes) for i in range(2)]
            x4t = [b.sb(f"fo{half}{i}", [128, D], F32, fes) for i in range(2)]
            bfree = [0, 0]
            ofree = [0, 0]
            b.wait("sync", sDO, sDO.n)
            for tt in range(off, off + 8):
                ib = tt % 2
                b.wait("sync", sV, bfree[ib])
                d1 = b.dma("sync", dxt[ib][:], xin[tt * 128:(tt + 1) * 128, :], sDX)
                d2 = b.dma("sync", ddt[ib][:], ddram[tt * 128:(tt + 1) * 128, :], sDX)
                b.wait("vector", sDX, d2)
                b.wait("vector", sDO, ofree[ib])
                bfree[ib] = b.op("vector", lambda e, ib=ib: e.tensor_tensor(
                    x4t[ib][:], dxt[ib][:], ddt[ib][:], OP.add), sV)
                b.wait("scalar", sV, bfree[ib])
                ofree[ib] = b.dma("scalar", xout[tt * 128:(tt + 1) * 128, :], x4t[ib][:], sDO)
            b.wait("scalar", sDO, sDO.n)
            b.flush()


P4_INPUTS = ["ident_bf", "ident_f32", "l1_wg", "l1_wu", "l1_wd", "l1_sg", "l1_su", "l1_sd"]


def build_p4_standalone():
    nc = bass.Bass("TRN2", target_bir_lowering=False)
    A = declare_shared(nc, P4_INPUTS)
    xin = nc.dram_tensor("xin", [T, D], F32, kind="ExternalInput").ap()
    xout = nc.dram_tensor("xout", [T, D], F32, kind="ExternalOutput").ap()
    ddram = nc.dram_tensor("ddram", [T, D], F32, kind="Internal").ap()
    with ExitStack() as es:
        emit_p4_l1ffn(nc, es, A, xin, xout, ddram)
    return nc


P3_INPUTS = ["ident_bf", "ident_f32", "masks", "l1_wqkv", "l1_wproj"]


def build_p3_standalone(ntiles=NT):
    nc = bass.Bass("TRN2", target_bir_lowering=False)
    A = declare_shared(nc, P3_INPUTS)
    xin = nc.dram_tensor("xin", [T, D], F32, kind="ExternalInput").ap()
    xout = nc.dram_tensor("xout", [T, D], F32, kind="ExternalOutput").ap()
    with ExitStack() as es:
        emit_p3_l1attn(nc, es, A, xin, xout, ntiles=ntiles)
    return nc


P1_INPUTS = ["ident_bf", "ident_f32", "masks", "l0_wq", "l0_wk", "l0_wv", "l0_wo",
             "l0_lng", "l0_lnb", "l0_bq", "l0_bk", "l0_bv", "l0_bo"]


def build_p1_standalone(ntiles=NT):
    nc = bass.Bass("TRN2", target_bir_lowering=False)
    A = declare_shared(nc, P1_INPUTS)
    xin = nc.dram_tensor("xin", [T, D], F32, kind="ExternalInput").ap()
    xout = nc.dram_tensor("xout", [T, D], F32, kind="ExternalOutput").ap()
    with ExitStack() as es:
        emit_p1_l0attn(nc, es, A, xin, xout, ntiles=ntiles)
    return nc



def _prep_shared(inp):
    A = {}
    A["ident_bf"] = np.eye(128, dtype=BFNP)
    A["ident_f32"] = np.eye(128, dtype=np.float32)
    p = np.arange(128)[:, None]
    mcol = np.arange(896)[None, :]
    A["masks"] = np.where(mcol <= p + 384, 0.0, NEG).astype(np.float32)

    def colchunks(v):
        return np.ascontiguousarray(v.reshape(-1, 128).T)

    A["l0_wq"] = np.ascontiguousarray(
        inp["L0_attn_Wq"].reshape(16, 128, 16, 128).transpose(2, 1, 0, 3).reshape(2048, 2048))
    A["l0_wk"] = np.ascontiguousarray(
        inp["L0_attn_Wk"].reshape(16, 128, 16, 128).transpose(2, 1, 0, 3).reshape(2048, 2048))
    A["l0_wv"] = np.ascontiguousarray(
        inp["L0_attn_Wv"].reshape(16, 128, 4, 512).transpose(2, 1, 0, 3).reshape(512, 8192))
    A["l0_wo"] = np.ascontiguousarray(inp["L0_attn_Wo"])
    A["l0_lng"] = colchunks(inp["L0_attn_ln_g"])
    A["l0_lnb"] = colchunks(inp["L0_attn_ln_b"])
    A["l0_bq"] = colchunks(inp["L0_attn_bq"])
    A["l0_bk"] = colchunks(inp["L0_attn_bk"])
    A["l0_bv"] = colchunks(inp["L0_attn_bv"])
    A["l0_bo"] = np.ascontiguousarray(inp["L0_attn_bo"].reshape(1, 2048))
    A["l0f_w"] = np.ascontiguousarray(inp["L0_ffn_weight"].T)
    A["l0f_gam"] = colchunks(inp["L0_ffn_ln_gamma"])
    A["l0f_bet"] = colchunks(inp["L0_ffn_ln_beta"])
    A["l0f_bias"] = np.ascontiguousarray(inp["L0_ffn_bias"].reshape(1, 2048))
    A["l1_wqkv"] = np.ascontiguousarray(
        inp["L1_attn_weight_qkv"].T.reshape(16, 128, 24, 128).transpose(2, 1, 0, 3).reshape(3072, 2048))
    A["l1_wproj"] = np.ascontiguousarray(inp["L1_attn_weight_proj"].T)
    A["l1_wg"] = np.ascontiguousarray(
        inp["L1_ffn_wq_gate"].T.reshape(16, 128, 44, 128).transpose(2, 1, 0, 3).reshape(5632, 2048))
    A["l1_wu"] = np.ascontiguousarray(
        inp["L1_ffn_wq_up"].T.reshape(16, 128, 44, 128).transpose(2, 1, 0, 3).reshape(5632, 2048))
    A["l1_wd"] = np.ascontiguousarray(
        inp["L1_ffn_wq_down"].T.reshape(44, 128, 4, 512).transpose(2, 0, 1, 3).reshape(22528, 512))
    A["l1_sg"] = colchunks(inp["L1_ffn_s_gate"])
    A["l1_su"] = colchunks(inp["L1_ffn_s_up"])
    A["l1_sd"] = np.ascontiguousarray(inp["L1_ffn_s_down"].reshape(1, 2048))
    return A


SHARED_SPECS = {
    "ident_bf": ([128, 128], BF16), "ident_f32": ([128, 128], F32),
    "masks": ([128, 896], F32),
    "l0_wq": ([2048, 2048], F32), "l0_wk": ([2048, 2048], F32),
    "l0_wv": ([512, 8192], F32), "l0_wo": ([2048, 2048], F32),
    "l0_lng": ([128, 16], F32), "l0_lnb": ([128, 16], F32),
    "l0_bq": ([128, 16], F32), "l0_bk": ([128, 16], F32), "l0_bv": ([128, 16], F32),
    "l0_bo": ([1, 2048], F32),
    "l0f_w": ([2048, 2048], F32), "l0f_gam": ([128, 16], F32),
    "l0f_bet": ([128, 16], F32), "l0f_bias": ([1, 2048], F32),
    "l1_wqkv": ([3072, 2048], F32), "l1_wproj": ([2048, 2048], F32),
    "l1_wg": ([5632, 2048], I32), "l1_wu": ([5632, 2048], I32), "l1_wd": ([22528, 512], I32),
    "l1_sg": ([128, 44], F32), "l1_su": ([128, 44], F32), "l1_sd": ([1, 2048], F32),
}


def declare_shared(nc, names):
    aps = {}
    for n in names:
        shape, dt = SHARED_SPECS[n]
        aps[n] = nc.dram_tensor(n, shape, dt, kind="ExternalInput").ap()
    return aps


P2_INPUTS = ["ident_bf", "ident_f32", "l0f_w", "l0f_gam", "l0f_bet", "l0f_bias"]


def build_p2_standalone(debug=False, ntiles=NT):
    nc = bass.Bass("TRN2", target_bir_lowering=False)
    A = declare_shared(nc, P2_INPUTS)
    xin = nc.dram_tensor("xin", [T, D], F32, kind="ExternalInput").ap()
    xout = nc.dram_tensor("xout", [T, D], F32, kind="ExternalOutput").ap()
    dbg = None
    if debug:
        dbg = {
            "rinv": nc.dram_tensor("d_rinv", [128, 1], F32, kind="ExternalOutput").ap(),
            "wq0": nc.dram_tensor("d_wq0", [128, D], BF16, kind="ExternalOutput").ap(),
            "u0": nc.dram_tensor("d_u0", [128, D], BF16, kind="ExternalOutput").ap(),
            "xnt0": nc.dram_tensor("d_xnt0", [128, D], BF16, kind="ExternalOutput").ap(),
            "biasb": nc.dram_tensor("d_biasb", [128, D], F32, kind="ExternalOutput").ap(),
            "coef": nc.dram_tensor("d_coef", [128, 8], F32, kind="ExternalOutput").ap(),
            "nb": nc.dram_tensor("d_nb", [128, 1], F32, kind="ExternalOutput").ap(),
        }
    with ExitStack() as es:
        emit_p2_l0ffn(nc, es, A, xin, xout, dbg=dbg, ntiles=ntiles)
    return nc


def build_fused():
    nc = bass.Bass("TRN2", target_bir_lowering=False)
    names = sorted(set(P1_INPUTS) | set(P2_INPUTS) | set(P3_INPUTS) | set(P4_INPUTS))
    A = declare_shared(nc, names)
    x0 = nc.dram_tensor("xin", [T, D], F32, kind="ExternalInput").ap()
    x4 = nc.dram_tensor("xout", [T, D], F32, kind="ExternalOutput").ap()
    x1 = nc.dram_tensor("x1s", [T, D], F32, kind="Internal").ap()
    x2 = nc.dram_tensor("x2s", [T, D], F32, kind="Internal").ap()
    x3 = nc.dram_tensor("x3s", [T, D], F32, kind="Internal").ap()
    dd = nc.dram_tensor("ddram", [T, D], F32, kind="Internal").ap()
    sems = {}
    with ExitStack() as ses:
        with ExitStack() as es:
            emit_p1_l0attn(nc, es, A, x0, x1, tag="p1", sems=sems, sems_es=ses)
        with ExitStack() as es:
            emit_p2_l0ffn(nc, es, A, x1, x2, tag="p2", sems=sems, sems_es=ses)
        with ExitStack() as es:
            emit_p3_l1attn(nc, es, A, x2, x3, tag="p3", sems=sems, sems_es=ses)
        with ExitStack() as es:
            emit_p4_l1ffn(nc, es, A, x3, x4, dd, tag="p4", sems=sems, sems_es=ses)
    return nc, names


_BUILD_CACHE = {}


def _get_phase_programs():
    if "phases" not in _BUILD_CACHE:
        _BUILD_CACHE["phases"] = [
            (build_p1_standalone(), P1_INPUTS),
            (build_p2_standalone(), P2_INPUTS),
            (build_p3_standalone(), P3_INPUTS),
            (build_p4_standalone(), P4_INPUTS),
        ]
    return _BUILD_CACHE["phases"]


def kernel(**inputs):
    A = _prep_shared(inputs)
    if "fused" not in _BUILD_CACHE:
        _BUILD_CACHE["fused"] = build_fused()
    nc, names = _BUILD_CACHE["fused"]
    in_maps = []
    for i in range(NCORES):
        m = {k: A[k] for k in names}
        m["xin"] = np.ascontiguousarray(inputs["L0_attn_X"][i])
        in_maps.append(m)
    res = run_bass_kernel_spmd(nc, in_maps, core_ids=list(range(NCORES)))
    return np.stack([res.results[i]["xout"] for i in range(NCORES)], axis=0)


# version 04
# speedup vs baseline: 1.0928x; 1.0928x over previous
import numpy as np
import ml_dtypes
from contextlib import ExitStack
import concourse.bass as bass
import concourse.mybir as mybir
from concourse.bass_utils import run_bass_kernel_spmd
from concourse.alu_op_type import AluOpType as OP

F32 = mybir.dt.float32
BF16 = mybir.dt.bfloat16
I32 = mybir.dt.int32
AF = mybir.ActivationFunctionType
AX = mybir.AxisListType.X
BFNP = ml_dtypes.bfloat16

ENGINES = ("sync", "tensor", "vector", "scalar", "gpsimd")
NCORES = 8
T = 2048
D = 2048
NT = T // 128
NC = D // 128
HFF = 5632
NHC = HFF // 128
INVD = 1.0 / D
EPS = 1e-5
ASCALE = 0.08838834764831845
NEG = -1.0e30


class Sem:
    __slots__ = ("h", "n")

    def __init__(self, h):
        self.h = h
        self.n = 0


class Bld:

    def __init__(self, nc, es, tag, sems=None, sems_es=None):
        self.nc, self.es, self.tag = nc, es, tag
        self.q = {e: [] for e in ENGINES}
        self._sems = sems if sems is not None else {}
        self._sems_es = sems_es
        self._uid = 0

    def fresh(self):
        self._uid += 1
        return self._uid

    def sem(self, name):
        if name not in self._sems:
            es = self._sems_es or self.es
            self._sems[name] = Sem(es.enter_context(self.nc.semaphore(f"sem_{name}_{self.tag}")))
        return self._sems[name]

    def sb(self, name, shape, dt, es=None):
        return (es or self.es).enter_context(self.nc.sbuf_tensor(f"{self.tag}_{name}", shape, dt))

    def ps(self, name, shape, dt, es=None):
        return (es or self.es).enter_context(self.nc.psum_tensor(f"{self.tag}_{name}", shape, dt))

    def op(self, eng, fn, sem=None, k=1):
        if sem is None:
            self.q[eng].append(fn)
            return None
        sem.n += k
        v = sem.n
        self.q[eng].append(lambda e, fn=fn, s=sem.h, k=k: fn(e).then_inc(s, k))
        return v

    def dma(self, eng, dst, src, sem):
        return self.op(eng, lambda e, d=dst, s=src: e.dma_start(d, s), sem, 16)

    def wait(self, eng, sem, v):
        if v:
            self.q[eng].append(lambda e, s=sem.h, v=v: e.wait_ge(s, v))

    def drain(self, eng):
        self.q[eng].append(lambda e: e.drain())

    def flush(self):
        q = self.q
        self.q = {e: [] for e in ENGINES}
        with self.nc.Block() as blk:
            for name in ENGINES:
                ops = q[name]
                if ops:
                    getattr(blk, name)(lambda eng, ops=ops: [f(eng) for f in ops])


def emit_consts(b, A):
    c = {}
    c["idb"] = b.sb("idb", [128, 128], BF16)
    c["idf"] = b.sb("idf", [128, 128], F32)
    c["ones1"] = b.sb("ones1", [1, 128], F32)
    c["epsc"] = b.sb("epsc", [128, 1], F32)
    c["onescol"] = b.sb("onescol", [128, 1], BF16)
    sDC = b.sem("dc")
    b.dma("sync", c["idb"][:], A["ident_bf"], sDC)
    b.dma("sync", c["idf"][:], A["ident_f32"], sDC)
    b.op("vector", lambda e: e.memset(c["ones1"][:], 1.0))
    b.op("vector", lambda e: e.memset(c["epsc"][:], EPS))
    b.op("vector", lambda e: e.memset(c["onescol"][:], 1.0))
    b.wait("sync", sDC, sDC.n)
    return c


def emit_bcast_row(b, c, row_sb, dst_sb, pa, width=D):
    sT, sV = b.sem("t"), b.sem("v")
    for ob in range(width // 512):
        vprev = sV.n
        b.wait("tensor", sV, vprev)
        b.op("tensor", lambda e, ob=ob: e.matmul(
            pa[:], c["ones1"][:], row_sb[:, ob * 512:(ob + 1) * 512], start=True, stop=True), sT)
        b.wait("vector", sT, sT.n)
        b.op("vector", lambda e, ob=ob: e.tensor_copy(dst_sb[:, ob * 512:(ob + 1) * 512], pa[:]), sV)


def emit_ln_stats(b, c, x_ap, bn6, mv, sc, nb, double_ln):
    sV, sS = b.sem("v"), b.sem("s")
    for i in range(4):
        b.op("vector", lambda e, i=i: e.bn_stats(bn6[:, i * 6:(i + 1) * 6], x_ap[:, i * 512:(i + 1) * 512]))
    b.drain("vector")
    vs = b.op("vector", lambda e: e.bn_aggr(mv[:], bn6[:]), sV)
    b.wait("scalar", sV, vs)
    ss = b.op("scalar", lambda e: e.activation(sc[:, 2:3], mv[:, 1:2], AF.Sqrt, bias=c["epsc"][:], scale=1.0), sS)
    b.wait("vector", sS, ss)
    b.op("vector", lambda e: e.reciprocal(sc[:, 3:4], sc[:, 2:3]))
    b.drain("vector")
    if double_ln:
        b.op("vector", lambda e: e.tensor_scalar(sc[:, 4:5], sc[:, 3:4], sc[:, 3:4], None, OP.mult))
        b.drain("vector")
        vv = b.op("vector", lambda e: e.tensor_scalar(sc[:, 5:6], mv[:, 1:2], sc[:, 4:5], None, OP.mult), sV)
        b.wait("scalar", sV, vv)
        ss = b.op("scalar", lambda e: e.activation(sc[:, 6:7], sc[:, 5:6], AF.Sqrt, bias=c["epsc"][:], scale=1.0), sS)
        b.wait("vector", sS, ss)
        b.op("vector", lambda e: e.reciprocal(sc[:, 7:8], sc[:, 6:7]))
        b.drain("vector")
        b.op("vector", lambda e: e.tensor_scalar(sc[:, 0:1], sc[:, 3:4], sc[:, 7:8], None, OP.mult))
    else:
        b.op("vector", lambda e: e.tensor_copy(sc[:, 0:1], sc[:, 3:4]))
    b.drain("vector")
    vv = b.op("vector", lambda e: e.scalar_tensor_tensor(nb[:], mv[:, 0:1], -1.0, sc[:, 0:1], OP.mult, OP.mult), sV)
    return vv


def emit_p2_l0ffn(nc, es, A, xin, xout, tag="p2", dbg=None, ntiles=NT, sems=None, sems_es=None):
    b = Bld(nc, es, tag, sems=sems, sems_es=sems_es)
    sV, sS, sT, sG = b.sem("v"), b.sem("s"), b.sem("t"), b.sem("g")
    sDC, sDW, sDX, sDO = b.sem("dc"), b.sem("dw"), b.sem("dx"), b.sem("do")

    c = emit_consts(b, A)
    gam = b.sb("gam", [128, 16], F32)
    bet = b.sb("bet", [128, 16], F32)
    brow = b.sb("brow", [1, D], F32)
    bias_b = b.sb("bias_b", [128, D], F32)
    wq = b.sb("wq", [128, NC * D], BF16)
    pa = b.ps("pa", [128, 512], F32)
    pb = b.ps("pb", [128, 512], F32)
    pta = b.ps("pta", [128, 512], BF16)
    ptb = b.ps("ptb", [128, 512], BF16)

    b.dma("sync", gam[:], A["l0f_gam"], sDC)
    b.dma("sync", bet[:], A["l0f_bet"], sDC)
    b.dma("sync", brow[:], A["l0f_bias"], sDC)
    b.wait("sync", sDC, sDC.n)
    b.flush()

    emit_bcast_row(b, c, brow, bias_b, pa)
    b.flush()

    with ExitStack() as wes:
        stg = [b.sb(f"stg{i}", [128, D], F32, wes) for i in range(2)]
        acc = b.sb("acc", [128, 16], F32, wes)
        dump = b.sb("dump", [128, D], F32, wes)
        small = b.sb("small", [1, 4], F32, wes)
        rinv = b.sb("rinv", [128, 1], F32, wes)
        tmp_col = b.sb("tmpcol", [128, 1], F32, wes)
        t1 = b.sb("t1", [128, D], BF16, wes)
        t2 = b.sb("t2", [128, D], BF16, wes)
        for cc in range(NC):
            if cc >= 2:
                b.wait("sync", sS, sS.n)
            dv = b.dma("sync", stg[cc % 2][:], A["l0f_w"][cc * 128:(cc + 1) * 128, :], sDW)
            b.wait("scalar", sDW, dv)
            b.op("scalar", lambda e, cc=cc: e.activation(
                dump[:], stg[cc % 2][:], AF.Abs, accum_out=acc[:, cc:cc + 1]), sS)
        b.flush()
        b.op("vector", lambda e: e.reduce_sum(tmp_col[:], acc[:], axis=AX))
        b.drain("vector")
        vv = b.op("vector", lambda e: e.engine_nop(), sV)
        b.wait("tensor", sV, vv)
        tv = b.op("tensor", lambda e: e.transpose(pa[0:1, 0:128], tmp_col[:], c["idf"][:]), sT)
        b.wait("vector", sT, tv)
        b.op("vector", lambda e: e.reduce_sum(small[:, 0:1], pa[0:1, 0:128], axis=AX))
        b.drain("vector")
        b.op("vector", lambda e: e.tensor_scalar(small[:, 1:2], small[:, 0:1], 1.0 / (D * D), EPS, OP.mult, OP.add))
        b.drain("vector")
        vv = b.op("vector", lambda e: e.reciprocal(small[:, 2:3], small[:, 1:2]), sV)
        b.wait("tensor", sV, vv)
        tv = b.op("tensor", lambda e: e.matmul(pb[:, 0:1], c["ones1"][:], small[:, 2:3], start=True, stop=True), sT)
        b.wait("vector", sT, tv)
        b.op("vector", lambda e: e.tensor_copy(rinv[:], pb[:, 0:1]))
        b.drain("vector")
        b.flush()
        for cc in range(NC):
            if cc >= 2:
                b.wait("sync", sV, sV.n)
            dv = b.dma("sync", stg[cc % 2][:], A["l0f_w"][cc * 128:(cc + 1) * 128, :], sDW)
            b.wait("vector", sDW, dv)
            b.op("vector", lambda e, cc=cc: e.tensor_scalar(
                t1[:], stg[cc % 2][:], rinv[:], 0.5, OP.mult, OP.is_gt))
            b.drain("vector")
            b.op("vector", lambda e, cc=cc: e.tensor_scalar(
                t2[:], stg[cc % 2][:], rinv[:], -0.5, OP.mult, OP.is_lt))
            b.drain("vector")
            b.op("vector", lambda e, cc=cc: e.tensor_tensor(
                wq[:, cc * D:(cc + 1) * D], t1[:], t2[:], OP.subtract), sV)
        b.flush()
        if dbg is not None:
            b.dma("scalar", dbg["rinv"], rinv[:], sDO)
            b.dma("scalar", dbg["wq0"], wq[:, 0:D], sDO)
            b.wait("scalar", sDO, sDO.n)
            b.flush()

    xt = [b.sb(f"xt{i}", [128, D], F32) for i in range(2)]
    ub = [b.sb(f"u{i}", [128, D], BF16) for i in range(2)]
    xnt = [b.sb(f"xnt{i}", [128, D], BF16) for i in range(2)]
    xb = [b.sb(f"xb{i}", [128, D], F32) for i in range(2)]
    x2t = [b.sb(f"x2t{i}", [128, D], F32) for i in range(2)]
    bn6 = b.sb("bn6", [128, 24], F32)
    mv = b.sb("mv", [128, 2], F32)
    scc = b.sb("scc", [128, 8], F32)
    nb = b.sb("nb", [128, 1], F32)
    xt_free = [(0, 0, 0), (0, 0, 0)]
    u_free = [0, 0]
    xnt_free = [0, 0]
    x2_free = [0, 0]
    xb_free = [0, 0]
    pmm = [pa, pb]
    pmm_free = [0, 0]
    ptr = [pta, ptb]
    ptr_free = [0, 0]
    last_uv = 0

    for tt in range(ntiles):
        ib = tt % 2
        b.wait("sync", sG, xt_free[ib][0])
        b.wait("sync", sV, xt_free[ib][1])
        b.wait("sync", sS, xt_free[ib][2])
        dv = b.dma("sync", xt[ib][:], xin[tt * 128:(tt + 1) * 128, :], sDX)
        b.wait("vector", sDX, dv)
        b.wait("vector", sS, last_uv)
        coeff_v = emit_ln_stats(b, c, xt[ib], bn6, mv, scc, nb, double_ln=True)
        b.wait("scalar", sDX, dv)
        b.wait("scalar", sV, coeff_v)
        b.wait("scalar", sT, u_free[ib])
        uv = b.op("scalar", lambda e, ib=ib: e.activation(
            ub[ib][:], xt[ib][:], AF.Identity, bias=nb[:], scale=scc[:, 0:1]), sS)
        last_uv = uv
        b.wait("gpsimd", sDX, dv)
        b.wait("gpsimd", sV, xb_free[ib])
        gv = b.op("gpsimd", lambda e, ib=ib: e.tensor_tensor(
            xb[ib][:], xt[ib][:], bias_b[:], OP.add), sG)
        xt_free[ib] = (gv, coeff_v, uv)
        b.wait("tensor", sS, uv)
        b.wait("scalar", sT, xnt_free[ib])
        for grp in range(4):
            pg = grp % 2
            b.wait("tensor", sS, ptr_free[pg])
            for j in range(4):
                cc = grp * 4 + j
                tv = b.op("tensor", lambda e, cc=cc, pg=pg, j=j, ib=ib: e.transpose(
                    ptr[pg][:, j * 128:(j + 1) * 128], ub[ib][:, cc * 128:(cc + 1) * 128], c["idb"][:]),
                    sT if j == 3 else None)
            if grp == 3:
                u_free[ib] = sT.n
            b.wait("scalar", sT, sT.n)
            for j in range(4):
                cc = grp * 4 + j
                ev = b.op("scalar", lambda e, cc=cc, pg=pg, j=j, ib=ib: e.activation(
                    xnt[ib][:, cc * 128:(cc + 1) * 128], ptr[pg][:, j * 128:(j + 1) * 128],
                    AF.Identity, bias=bet[:, cc:cc + 1], scale=gam[:, cc:cc + 1]), sS)
            ptr_free[pg] = sS.n
        xnt_ready = sS.n
        b.wait("tensor", sS, xnt_ready)
        for ob in range(4):
            pg = ob % 2
            b.wait("tensor", sV, pmm_free[pg])
            for cc in range(NC):
                mmv = b.op("tensor", lambda e, cc=cc, ob=ob, pg=pg, ib=ib: e.matmul(
                    pmm[pg][:], xnt[ib][:, cc * 128:(cc + 1) * 128],
                    wq[:, cc * D + ob * 512: cc * D + (ob + 1) * 512],
                    start=(cc == 0), stop=(cc == NC - 1)), sT if cc == NC - 1 else None)
            b.wait("vector", sT, mmv)
            b.wait("vector", sG, xt_free[ib][0])
            b.wait("vector", sDO, x2_free[ib])
            pmm_free[pg] = b.op("vector", lambda e, ob=ob, pg=pg, ib=ib: e.tensor_tensor(
                x2t[ib][:, ob * 512:(ob + 1) * 512], pmm[pg][:], xb[ib][:, ob * 512:(ob + 1) * 512], OP.add), sV)
        xb_free[ib] = sV.n
        xnt_free[ib] = sT.n
        b.wait("scalar", sV, sV.n)
        x2_free[ib] = b.dma("scalar", xout[tt * 128:(tt + 1) * 128, :], x2t[ib][:], sDO)
        if dbg is not None and tt == 0:
            b.dma("scalar", dbg["u0"], ub[0][:], sDO)
            b.dma("scalar", dbg["xnt0"], xnt[0][:], sDO)
            b.dma("scalar", dbg["biasb"], bias_b[:], sDO)
            b.dma("scalar", dbg["coef"], scc[:], sDO)
            b.dma("scalar", dbg["nb"], nb[:], sDO)
    b.wait("scalar", sDO, sDO.n)
    b.flush()


def emit_ln_phase(b, c, xin_ap, col_of, xnt_sb, gamcol, betcol, double_ln, tiles, les, ptpair):
    sV, sS, sT, sDX = b.sem("v"), b.sem("s"), b.sem("t"), b.sem("dx")
    u_ = b.fresh()
    xt = [b.sb(f"lnxt{i}_{u_}", [128, D], F32, les) for i in range(2)]
    ub = [b.sb(f"lnu{i}_{u_}", [128, D], BF16, les) for i in range(2)]
    bn6 = b.sb(f"lnbn6_{u_}", [128, 24], F32, les)
    mv = b.sb(f"lnmv_{u_}", [128, 2], F32, les)
    scc = b.sb(f"lnscc_{u_}", [128, 8], F32, les)
    nb = b.sb(f"lnnb_{u_}", [128, 1], F32, les)
    ptr = ptpair
    ptr_free = [0, 0]
    xt_free = [(0, 0), (0, 0)]
    u_free = [0, 0]
    last_uv = 0
    for ti, tt in enumerate(tiles):
        ib = ti % 2
        b.wait("sync", sV, xt_free[ib][0])
        b.wait("sync", sS, xt_free[ib][1])
        dv = b.dma("sync", xt[ib][:], xin_ap[tt * 128:(tt + 1) * 128, :], sDX)
        b.wait("vector", sDX, dv)
        b.wait("vector", sS, last_uv)
        coeff_v = emit_ln_stats(b, c, xt[ib], bn6, mv, scc, nb, double_ln=double_ln)
        b.wait("scalar", sDX, dv)
        b.wait("scalar", sV, coeff_v)
        b.wait("scalar", sT, u_free[ib])
        uv = b.op("scalar", lambda e, ib=ib: e.activation(
            ub[ib][:], xt[ib][:], AF.Identity, bias=nb[:], scale=scc[:, 0:1]), sS)
        last_uv = uv
        xt_free[ib] = (coeff_v, uv)
        b.wait("tensor", sS, uv)
        for grp in range(4):
            pg = grp % 2
            b.wait("tensor", sS, ptr_free[pg])
            for j in range(4):
                cc = grp * 4 + j
                b.op("tensor", lambda e, cc=cc, pg=pg, j=j, ib=ib: e.transpose(
                    ptr[pg][:, j * 128:(j + 1) * 128], ub[ib][:, cc * 128:(cc + 1) * 128], c["idb"][:]),
                    sT if j == 3 else None)
            if grp == 3:
                u_free[ib] = sT.n
            b.wait("scalar", sT, sT.n)
            for j in range(4):
                cc = grp * 4 + j
                if gamcol is not None:
                    b.op("scalar", lambda e, cc=cc, pg=pg, j=j, tt=tt: e.activation(
                        xnt_sb[:, col_of(cc, tt):col_of(cc, tt) + 128], ptr[pg][:, j * 128:(j + 1) * 128],
                        AF.Identity, bias=betcol(cc), scale=gamcol(cc)), sS)
                else:
                    b.op("scalar", lambda e, cc=cc, pg=pg, j=j, tt=tt: e.activation(
                        xnt_sb[:, col_of(cc, tt):col_of(cc, tt) + 128], ptr[pg][:, j * 128:(j + 1) * 128],
                        AF.Copy), sS)
            ptr_free[pg] = sS.n
    return sS.n


def emit_head_bound(b, c, st, qh, kh, nmx, mcol, bufs, ready):
    sV, sS, sT = b.sem("v"), b.sem("s"), b.sem("t")
    pbc = bufs["pbc"]
    sq = bufs["prow"][3]
    for i, src in ((0, qh), (1, kh)):
        b.wait("scalar", sS, ready if i == 0 else 0)
        b.wait("scalar", sT, st["nmm"])
        b.wait("scalar", sT, st["prow_free"][3])
        sqv = b.op("scalar", lambda e, src=src: e.activation(sq[:], src[:], AF.Square), sS)
        for tb in range(4):
            b.wait("tensor", sS, sqv)
            b.wait("tensor", sV, st["invlb_ev"])
            b.wait("tensor", sV, st["nred"])
            st["nmm"] = b.op("tensor", lambda e, tb=tb: e.matmul(
                pbc[0:1, 0:512], c["onescol"][:], sq[:, tb * 512:(tb + 1) * 512],
                start=True, stop=True), sT)
            b.wait("vector", sT, st["nmm"])
            st["nred"] = b.op("vector", lambda e, i=i, tb=tb: e.reduce_max(
                nmx[:, i * 4 + tb:i * 4 + tb + 1], pbc[0:1, 0:512], axis=AX), sV)
    b.wait("vector", sV, st["nred"])
    b.op("vector", lambda e: e.reduce_max(nmx[:, 8:9], nmx[:, 0:4], axis=AX))
    b.drain("vector")
    b.op("vector", lambda e: e.reduce_max(nmx[:, 9:10], nmx[:, 4:8], axis=AX))
    b.drain("vector")
    vv = b.op("vector", lambda e: e.tensor_tensor(nmx[:, 10:11], nmx[:, 8:9], nmx[:, 9:10], OP.mult), sV)
    b.wait("scalar", sV, vv)
    sv = b.op("scalar", lambda e: e.activation(nmx[:, 11:12], nmx[:, 10:11], AF.Sqrt), sS)
    b.wait("vector", sS, sv)
    vv = b.op("vector", lambda e: e.tensor_scalar(nmx[:, 12:13], nmx[:, 11:12], -ASCALE, None, OP.mult), sV)
    b.wait("tensor", sV, vv)
    tv = b.op("tensor", lambda e: e.matmul(pbc[:, 0:1], c["ones1"][:], nmx[:, 12:13],
                                           start=True, stop=True), sT)
    b.wait("vector", sT, tv)
    mv = b.op("vector", lambda e: e.tensor_copy(mcol[:], pbc[:, 0:1]), sV)
    st["invlb_ev"] = max(st["invlb_ev"], mv)
    return mv


def emit_attn_head(b, c, st, qh, kh, vtile, avt_slice, bufs, ready, mcol, mrdy):
    sV, sS, sT = b.sem("v"), b.sem("s"), b.sem("t")
    ps, pta, ptb, pbc, pav = bufs["ps"], bufs["pta"], bufs["ptb"], bufs["pbc"], bufs["pav"]
    prow, ls, lc = bufs["prow"], bufs["ls"], bufs["lc"]
    lrow, invl, invlb, msk, PT = bufs["lrow"], bufs["invl"], bufs["invlb"], bufs["msk"], bufs["PT"]
    b.wait("tensor", sS, ready)
    first_exp = True
    for qb in range(4):
        nkb = qb + 1
        for t4 in range(4):
            tau = qb * 4 + t4
            bank = (lambda kb, t4=t4, nkb=nkb: (t4 * nkb + kb) % 4 if nkb in (1, 2) else kb)
            for kb in range(nkb):
                bk = bank(kb)
                b.wait("tensor", sS, st["psfree"][bk])
                if bk == 0:
                    b.wait("tensor", sV, st["lrow_ev"])
                st["mm"][bk] = b.op("tensor", lambda e, bk=bk, kb=kb, tau=tau: e.matmul(
                    ps[bk][:], qh[:, tau * 128:(tau + 1) * 128], kh[:, kb * 512:(kb + 1) * 512],
                    start=True, stop=True), sT)
            bd = bank(qb)
            b.wait("vector", sT, st["mm"][bd])
            mv_ = b.op("vector", lambda e, bd=bd, t4=t4: e.tensor_tensor(
                ps[bd][:], ps[bd][:], msk[:, (3 - t4) * 128:(3 - t4) * 128 + 512], OP.add), sV)
            if first_exp:
                b.wait("scalar", sV, mrdy)
                first_exp = False
            b.wait("scalar", sT, st["prow_free"][t4])
            for kb in range(nkb):
                bk = bank(kb)
                if kb == qb:
                    b.wait("scalar", sV, mv_)
                else:
                    b.wait("scalar", sT, st["mm"][bk])
                st["psfree"][bk] = b.op("scalar", lambda e, bk=bk, kb=kb, t4=t4: e.activation(
                    prow[t4][:, kb * 512:(kb + 1) * 512], ps[bk][:], AF.Exp,
                    bias=mcol[:], scale=ASCALE, accum_out=ls[:, t4 * 4 + kb:t4 * 4 + kb + 1]), sS)
            b.wait("vector", sS, st["psfree"][bank(nkb - 1)])
            lcv = b.op("vector", lambda e, t4=t4, nkb=nkb: e.reduce_sum(
                lc[:, t4:t4 + 1], ls[:, t4 * 4:t4 * 4 + nkb], axis=AX), sV if t4 == 3 else None)
            b.drain("vector")
        b.wait("tensor", sV, lcv)
        b.wait("tensor", sS, st["psfree"][0])
        for t4 in range(4):
            b.op("tensor", lambda e, t4=t4: e.transpose(
                ps[0][0:1, t4 * 128:(t4 + 1) * 128], lc[:, t4:t4 + 1], c["idf"][:]), sT if t4 == 3 else None)
        ltr = sT.n
        b.wait("vector", sT, ltr)
        b.op("vector", lambda e: e.tensor_copy(lrow[:], ps[0][0:1, 0:512]))
        b.drain("vector")
        st["lrow_ev"] = b.op("vector", lambda e: e.reciprocal(invl[:], lrow[:]), sV)
        b.wait("tensor", sV, st["lrow_ev"])
        b.wait("tensor", sV, st["invlb_ev"])
        bcv = b.op("tensor", lambda e: e.matmul(pbc[:], c["ones1"][:], invl[:], start=True, stop=True), sT)
        b.wait("vector", sT, bcv)
        st["invlb_ev"] = b.op("vector", lambda e: e.tensor_copy(invlb[:], pbc[:]), sV)
        nk = 4 * nkb
        ptp = [pta, ptb]
        for kk in range(nk):
            pg = kk % 2
            b.wait("tensor", sS, st["ptevac"][pg])
            for t4 in range(4):
                b.op("tensor", lambda e, kk=kk, t4=t4, pg=pg: e.transpose(
                    ptp[pg][:, t4 * 128:(t4 + 1) * 128], prow[t4][:, kk * 128:(kk + 1) * 128], c["idb"][:]),
                    sT if t4 == 3 else None)
            trv = sT.n
            if kk >= nk - 2:
                for t4 in range(4):
                    st["prow_free"][t4] = max(st["prow_free"][t4], trv)
            b.wait("scalar", sT, trv)
            b.wait("scalar", sT, st["PTmm"][pg])
            st["ptevac"][pg] = b.op("scalar", lambda e, pg=pg, kk=kk: e.activation(
                PT[pg][:], ptp[pg][:], AF.Copy), sS)
            b.wait("tensor", sS, st["ptevac"][pg])
            if kk == 0:
                b.wait("tensor", sV, st["pav_ev"])
            st["PTmm"][pg] = b.op("tensor", lambda e, kk=kk, pg=pg, nk=nk: e.matmul(
                pav[:], vtile(kk), PT[pg][:], start=(kk == 0), stop=(kk == nk - 1)), sT)
        b.wait("vector", sT, st["PTmm"][(nk - 1) % 2])
        st["pav_ev"] = b.op("vector", lambda e, qb=qb: e.tensor_tensor(
            avt_slice(qb), pav[:], invlb[:], OP.mult), sV)


def emit_p1_l0attn(nc, es, A, xin, xout, tag="p1", ntiles=NT, sems=None, sems_es=None):
    b = Bld(nc, es, tag, sems=sems, sems_es=sems_es)
    sV, sS, sT, sG = b.sem("v"), b.sem("s"), b.sem("t"), b.sem("g")
    sDC, sDX, sDO = b.sem("dc"), b.sem("dx"), b.sem("do")
    sWA = b.sem("wa")

    c = emit_consts(b, A)
    lng = b.sb("lng", [128, 16], F32)
    lnb = b.sb("lnb", [128, 16], F32)
    bqc = b.sb("bqc", [128, 16], F32)
    bkc = b.sb("bkc", [128, 16], F32)
    bvb = b.sb("bvb", [128, 16], BF16)
    msk = b.sb("msk", [128, 896], F32)
    for dst, src_ in [(lng, A["l0_lng"]), (lnb, A["l0_lnb"]), (bqc, A["l0_bq"]),
                      (bkc, A["l0_bk"]), (msk, A["masks"])]:
        b.dma("sync", dst[:], src_, sDC)
    b.dma("gpsimd", bvb[:], A["l0_bv"], sDC)
    b.wait("sync", sDC, sDC.n)
    b.flush()

    ps = [b.ps(f"ps{i}", [128, 512], F32) for i in range(4)]
    pta = b.ps("pta", [128, 512], BF16)
    ptb = b.ps("ptb", [128, 512], BF16)
    pbc = b.ps("pbc", [128, 512], F32)
    pav = b.ps("pav", [128, 512], F32)

    AVT = b.sb("avt", [128, NC * T], BF16)

    st = dict(psfree=[0, 0, 0, 0], mm=[0, 0, 0, 0], prow_free=[0, 0, 0, 0],
              ptevac=[0, 0], PTmm=[0, 0], pav_ev=0, lrow_ev=0, invlb_ev=0,
              nmm=0, nred=0)

    with ExitStack() as xes:
        xnt = b.sb("xnt", [128, NC * T], BF16, xes)
        with ExitStack() as les:
            xnt_done = emit_ln_phase(
                b, c, xin, lambda cc, tt: cc * T + tt * 128, xnt,
                lambda cc: lng[:, cc:cc + 1], lambda cc: lnb[:, cc:cc + 1],
                True, range(ntiles), les, (pta, ptb))
            b.flush()

        with ExitStack() as aes:
            prow = [b.sb(f"prow{i}", [128, T], BF16, aes) for i in range(4)]
            ls = b.sb("ls", [128, 16], F32, aes)
            lc = b.sb("lc", [128, 4], F32, aes)
            nmx = b.sb("nmx", [1, 16], F32, aes)
            mcol = b.sb("mcol", [128, 1], F32, aes)
            lrowb = b.sb("lrowb", [1, 1024], F32, aes)
            lrow, invl = lrowb[:, 0:512], lrowb[:, 512:1024]
            invlb = b.sb("invlb", [128, 512], F32, aes)
            PT = [b.sb(f"PT{i}", [128, 512], BF16, aes) for i in range(2)]
            V_sb = b.sb("vsb", [128, 16 * 512], BF16, aes)
            qh = b.sb("qh", [128, T], BF16, aes)
            kh = b.sb("kh", [128, T], BF16, aes)
            wq_b = b.sb("wqb", [128, D], BF16, aes)
            wk_b = b.sb("wkb", [128, D], BF16, aes)
            bufs = dict(ps=ps, pta=pta, ptb=ptb, pbc=pbc, pav=pav, prow=prow, ls=ls,
                        lc=lc, lrow=lrow, invl=invl, invlb=invlb, msk=msk, PT=PT)

            qk_done = 0
            for g in range(4):
                with ExitStack() as ges:
                    wvg = b.sb(f"wvg{g}", [128, 8192], BF16, ges)
                    dvv = b.dma("gpsimd", wvg[:], A["l0_wv"][g * 128:(g + 1) * 128, :], sWA)
                    b.wait("tensor", sWA, dvv)
                    b.wait("tensor", sS, xnt_done)
                    b.wait("tensor", sV, st["lrow_ev"])
                    vev = [st["psfree"][0], st["psfree"][1]]
                    for tt in range(16):
                        pg = tt % 2
                        b.wait("tensor", sS, vev[pg])
                        mmv = None
                        for cc in range(NC):
                            mmv = b.op("tensor", lambda e, cc=cc, tt=tt, pg=pg, wvg=wvg: e.matmul(
                                ps[pg][:], xnt[:, cc * T + tt * 128: cc * T + (tt + 1) * 128],
                                wvg[:, cc * 512:(cc + 1) * 512],
                                start=(cc == 0), stop=(cc == NC - 1)), sT if cc == NC - 1 else None)
                        b.wait("scalar", sT, mmv)
                        vev[pg] = b.op("scalar", lambda e, tt=tt, pg=pg: e.activation(
                            V_sb[:, tt * 512:(tt + 1) * 512], ps[pg][:], AF.Copy), sS)
                    st["psfree"][0], st["psfree"][1] = vev
                    vprod_ev = sS.n
                    for hh in range(4):
                        h = g * 4 + hh
                        b.wait("gpsimd", sT, qk_done)
                        dq = b.dma("gpsimd", wq_b[:], A["l0_wq"][h * 128:(h + 1) * 128, :], sWA)
                        dk = b.dma("gpsimd", wk_b[:], A["l0_wk"][h * 128:(h + 1) * 128, :], sWA)
                        b.wait("tensor", sWA, dq)
                        kv = None
                        for (which, wbuf, dst, bias) in (("q", wq_b, qh, bqc), ("k", wk_b, kh, bkc)):
                            if which == "k":
                                b.wait("tensor", sWA, dk)
                            for tb in range(4):
                                pg = tb % 2
                                b.wait("tensor", sS, st["psfree"][pg])
                                mmv = None
                                for cc in range(NC):
                                    mmv = b.op("tensor", lambda e, cc=cc, tb=tb, pg=pg, wbuf=wbuf: e.matmul(
                                        ps[pg][:], wbuf[:, cc * 128:(cc + 1) * 128],
                                        xnt[:, cc * T + tb * 512: cc * T + tb * 512 + 512],
                                        start=(cc == 0), stop=(cc == NC - 1)), sT if cc == NC - 1 else None)
                                b.wait("scalar", sT, mmv)
                                st["psfree"][pg] = b.op("scalar", lambda e, tb=tb, pg=pg, dst=dst, bias=bias, h=h:
                                    e.activation(dst[:, tb * 512:(tb + 1) * 512], ps[pg][:], AF.Identity,
                                                 bias=bias[:, h:h + 1], scale=1.0), sS)
                            if which == "k":
                                kv = sS.n
                        qk_done = sT.n
                        mrdy = emit_head_bound(b, c, st, qh, kh, nmx, mcol, bufs, max(kv, vprod_ev))
                        emit_attn_head(
                            b, c, st, qh, kh,
                            lambda kk, hh=hh, V_sb=V_sb: V_sb[:, kk * 512 + hh * 128: kk * 512 + hh * 128 + 128],
                            lambda qb, h=h: AVT[:, h * T + qb * 512: h * T + (qb + 1) * 512],
                            bufs, max(kv, vprod_ev), mcol, mrdy)
                    b.flush()

    with ExitStack() as oes:
        wo = b.sb("wo", [128, NC * D], BF16, oes)
        borow = b.sb("borow", [1, D], F32, oes)
        boeff = b.sb("boeff", [1, D], F32, oes)
        bob = b.sb("bob", [128, D], F32, oes)
        xt = [b.sb(f"oxt{i}", [128, D], F32, oes) for i in range(2)]
        xbb = [b.sb(f"oxb{i}", [128, D], F32, oes) for i in range(2)]
        x1t = [b.sb(f"ox1{i}", [128, D], F32, oes) for i in range(2)]
        b.dma("sync", borow[:], A["l0_bo"], sDC)
        b.wait("sync", sDC, sDC.n)
        for cc in range(NC):
            b.dma("gpsimd", wo[:, cc * D:(cc + 1) * D], A["l0_wo"][cc * 128:(cc + 1) * 128, :], sWA)
        b.wait("tensor", sWA, sWA.n)
        b.flush()
        for ob in range(4):
            mmv = None
            for cc in range(NC):
                mmv = b.op("tensor", lambda e, cc=cc, ob=ob: e.matmul(
                    pbc[0:1, 0:512], bvb[:, cc:cc + 1], wo[:, cc * D + ob * 512: cc * D + (ob + 1) * 512],
                    start=(cc == 0), stop=(cc == NC - 1)), sT if cc == NC - 1 else None)
            b.wait("vector", sT, mmv)
            ev = b.op("vector", lambda e, ob=ob: e.tensor_tensor(
                boeff[:, ob * 512:(ob + 1) * 512], pbc[0:1, 0:512], borow[:, ob * 512:(ob + 1) * 512],
                OP.add), sV)
            b.wait("tensor", sV, ev)
        b.flush()
        emit_bcast_row(b, c, boeff, bob, pbc)
        b.flush()
        xt_free = [0, 0]
        x1_free = [0, 0]
        xb_free = [0, 0]
        pmm_free = [0, 0]
        for tt in range(ntiles):
            ib = tt % 2
            b.wait("sync", sG, xt_free[ib])
            dv = b.dma("sync", xt[ib][:], xin[tt * 128:(tt + 1) * 128, :], sDX)
            b.wait("gpsimd", sDX, dv)
            b.wait("gpsimd", sV, xb_free[ib])
            gv = b.op("gpsimd", lambda e, ib=ib: e.tensor_tensor(
                xbb[ib][:], xt[ib][:], bob[:], OP.add), sG)
            xt_free[ib] = gv
            for ob in range(4):
                pg = ob % 2
                b.wait("tensor", sV, pmm_free[pg])
                mmv = None
                for cc in range(NC):
                    mmv = b.op("tensor", lambda e, cc=cc, tt=tt, ob=ob, pg=pg: e.matmul(
                        ps[pg][:], AVT[:, cc * T + tt * 128: cc * T + (tt + 1) * 128],
                        wo[:, cc * D + ob * 512: cc * D + (ob + 1) * 512],
                        start=(cc == 0), stop=(cc == NC - 1)), sT if cc == NC - 1 else None)
                b.wait("vector", sT, mmv)
                b.wait("vector", sG, gv)
                b.wait("vector", sDO, x1_free[ib])
                pmm_free[pg] = b.op("vector", lambda e, ob=ob, pg=pg, ib=ib: e.tensor_tensor(
                    x1t[ib][:, ob * 512:(ob + 1) * 512], ps[pg][:], xbb[ib][:, ob * 512:(ob + 1) * 512],
                    OP.add), sV)
            xb_free[ib] = sV.n
            b.wait("scalar", sV, sV.n)
            x1_free[ib] = b.dma("scalar", xout[tt * 128:(tt + 1) * 128, :], x1t[ib][:], sDO)
        b.wait("scalar", sDO, sDO.n)
        b.flush()


def emit_p3_l1attn(nc, es, A, xin, xout, tag="p3", ntiles=NT, sems=None, sems_es=None):
    b = Bld(nc, es, tag, sems=sems, sems_es=sems_es)
    sV, sS, sT, sG = b.sem("v"), b.sem("s"), b.sem("t"), b.sem("g")
    sDC, sDX, sDO = b.sem("dc"), b.sem("dx"), b.sem("do")
    sWA = b.sem("wa")

    c = emit_consts(b, A)
    msk = b.sb("msk", [128, 896], F32)
    b.dma("sync", msk[:], A["masks"], sDC)
    b.wait("sync", sDC, sDC.n)
    b.flush()

    ps = [b.ps(f"ps{i}", [128, 512], F32) for i in range(4)]
    pta = b.ps("pta", [128, 512], BF16)
    ptb = b.ps("ptb", [128, 512], BF16)
    pbc = b.ps("pbc", [128, 512], F32)
    pav = b.ps("pav", [128, 512], F32)

    AVT = b.sb("avt", [128, NC * T], BF16)

    st = dict(psfree=[0, 0, 0, 0], mm=[0, 0, 0, 0], prow_free=[0, 0, 0, 0],
              ptevac=[0, 0], PTmm=[0, 0], pav_ev=0, lrow_ev=0, invlb_ev=0,
              nmm=0, nred=0)

    with ExitStack() as aes:
        prow = [b.sb(f"prow{i}", [128, T], BF16, aes) for i in range(4)]
        ls = b.sb("ls", [128, 16], F32, aes)
        lc = b.sb("lc", [128, 4], F32, aes)
        nmx = b.sb("nmx", [1, 16], F32, aes)
        mcol = b.sb("mcol", [128, 1], F32, aes)
        lrowb = b.sb("lrowb", [1, 1024], F32, aes)
        lrow, invl = lrowb[:, 0:512], lrowb[:, 512:1024]
        invlb = b.sb("invlb", [128, 512], F32, aes)
        PT = [b.sb(f"PT{i}", [128, 512], BF16, aes) for i in range(2)]
        bufs = dict(ps=ps, pta=pta, ptb=ptb, pbc=pbc, pav=pav, prow=prow, ls=ls,
                    lc=lc, lrow=lrow, invl=invl, invlb=invlb, msk=msk, PT=PT)
        with ExitStack() as xes:
            xnt = b.sb("xnt", [128, NC * T], BF16, xes)
            with ExitStack() as les:
                xnt_done = emit_ln_phase(
                    b, c, xin, lambda cc, tt: cc * T + tt * 128, xnt,
                    None, None, False, range(ntiles), les, (pta, ptb))
                b.flush()

            wdone = 0
            for g in range(4):
                with ExitStack() as ges:
                    Qg = b.sb(f"qg{g}", [128, 4 * T], BF16, ges)
                    Kg = b.sb(f"kg{g}", [128, T], BF16, ges)
                    Vt = b.sb(f"vt{g}", [128, T], BF16, ges)
                    Vg = b.sb(f"vg{g}", [128, T], BF16, ges)
                    wb = [b.sb(f"wb{g}{i}", [128, D], BF16, ges) for i in range(2)]
                    vtv = 0
                    for r in range(6):
                        pc = 6 * g + r
                        ib = r % 2
                        b.wait("gpsimd", sT, wdone if r < 2 else wdone)
                        dq = b.dma("gpsimd", wb[ib][:], A["l1_wqkv"][pc * 128:(pc + 1) * 128, :], sWA)
                        b.wait("tensor", sWA, dq)
                        dst = (Qg[:, r * T:(r + 1) * T] if r < 4 else (Kg[:] if r == 4 else Vt[:]))
                        for tb in range(4):
                            pg = tb % 2
                            b.wait("tensor", sS, st["psfree"][pg])
                            b.wait("tensor", sV, st["lrow_ev"])
                            mmv = None
                            for cc in range(NC):
                                mmv = b.op("tensor", lambda e, cc=cc, tb=tb, pg=pg, ib=ib: e.matmul(
                                    ps[pg][:], wb[ib][:, cc * 128:(cc + 1) * 128],
                                    xnt[:, cc * T + tb * 512: cc * T + tb * 512 + 512],
                                    start=(cc == 0), stop=(cc == NC - 1)), sT if cc == NC - 1 else None)
                            wdone = mmv
                            b.wait("scalar", sT, mmv)
                            st["psfree"][pg] = b.op("scalar", lambda e, tb=tb, pg=pg, dst=dst: e.activation(
                                dst[:, tb * 512:(tb + 1) * 512], ps[pg][:], AF.Copy), sS)
                        if r == 5:
                            vtv = sS.n
                    b.wait("tensor", sS, vtv)
                    ptp = [pta, ptb]
                    for g4 in range(4):
                        pg = g4 % 2
                        b.wait("tensor", sS, st["ptevac"][pg])
                        for j in range(4):
                            kk = g4 * 4 + j
                            b.op("tensor", lambda e, kk=kk, j=j, pg=pg: e.transpose(
                                ptp[pg][:, j * 128:(j + 1) * 128], Vt[:, kk * 128:(kk + 1) * 128],
                                c["idb"][:]), sT if j == 3 else None)
                        b.wait("scalar", sT, sT.n)
                        st["ptevac"][pg] = b.op("scalar", lambda e, g4=g4, pg=pg: e.activation(
                            Vg[:, g4 * 512:(g4 + 1) * 512], ptp[pg][:], AF.Copy), sS)
                    vgv = sS.n
                    for j in range(4):
                        h = g * 4 + j
                        mrdy = emit_head_bound(b, c, st, Qg[:, j * T:(j + 1) * T], Kg,
                                               nmx, mcol, bufs, vgv)
                        emit_attn_head(
                            b, c, st, Qg[:, j * T:(j + 1) * T], Kg,
                            lambda kk, Vg=Vg: Vg[:, kk * 128:(kk + 1) * 128],
                            lambda qb, h=h: AVT[:, h * T + qb * 512: h * T + (qb + 1) * 512],
                            bufs, vgv, mcol, mrdy)
                    b.flush()

    with ExitStack() as oes:
        wo = b.sb("wo", [128, NC * D], BF16, oes)
        xt = [b.sb(f"oxt{i}", [128, D], F32, oes) for i in range(2)]
        x1t = [b.sb(f"ox1{i}", [128, D], F32, oes) for i in range(2)]
        for cc in range(NC):
            b.dma("gpsimd", wo[:, cc * D:(cc + 1) * D], A["l1_wproj"][cc * 128:(cc + 1) * 128, :], sWA)
        b.wait("tensor", sWA, sWA.n)
        b.flush()
        xt_free = [0, 0]
        x1_free = [0, 0]
        pmm_free = [0, 0]
        for tt in range(ntiles):
            ib = tt % 2
            b.wait("sync", sV, xt_free[ib])
            dv = b.dma("sync", xt[ib][:], xin[tt * 128:(tt + 1) * 128, :], sDX)
            for ob in range(4):
                pg = ob % 2
                b.wait("tensor", sV, pmm_free[pg])
                mmv = None
                for cc in range(NC):
                    mmv = b.op("tensor", lambda e, cc=cc, tt=tt, ob=ob, pg=pg: e.matmul(
                        ps[pg][:], AVT[:, cc * T + tt * 128: cc * T + (tt + 1) * 128],
                        wo[:, cc * D + ob * 512: cc * D + (ob + 1) * 512],
                        start=(cc == 0), stop=(cc == NC - 1)), sT if cc == NC - 1 else None)
                b.wait("vector", sT, mmv)
                b.wait("vector", sDX, dv)
                b.wait("vector", sDO, x1_free[ib])
                pmm_free[pg] = b.op("vector", lambda e, ob=ob, pg=pg, ib=ib: e.tensor_tensor(
                    x1t[ib][:, ob * 512:(ob + 1) * 512], ps[pg][:], xt[ib][:, ob * 512:(ob + 1) * 512],
                    OP.add), sV)
            xt_free[ib] = sV.n
            b.wait("scalar", sV, sV.n)
            x1_free[ib] = b.dma("scalar", xout[tt * 128:(tt + 1) * 128, :], x1t[ib][:], sDO)
        b.wait("scalar", sDO, sDO.n)
        b.flush()


def emit_p4_l1ffn(nc, es, A, xin, xout, ddram, tag="p4", sems=None, sems_es=None):
    b = Bld(nc, es, tag, sems=sems, sems_es=sems_es)
    sV, sS, sT, sG = b.sem("v"), b.sem("s"), b.sem("t"), b.sem("g")
    sDC, sDX, sDO, sWA, sWB = b.sem("dc"), b.sem("dx"), b.sem("do"), b.sem("wa"), b.sem("wb")
    TH = 1024

    c = emit_consts(b, A)
    sgc = b.sb("sgc", [128, 44], F32)
    suc = b.sb("suc", [128, 44], F32)
    sdrow = b.sb("sdrow", [1, D], F32)
    sdb = b.sb("sdb", [128, D], F32)
    onesc = b.sb("onesc", [128, 1], BF16)
    b.dma("sync", sgc[:], A["l1_sg"], sDC)
    b.dma("sync", suc[:], A["l1_su"], sDC)
    b.dma("sync", sdrow[:], A["l1_sd"], sDC)
    b.op("vector", lambda e: e.memset(onesc[:], 1.0))
    b.wait("sync", sDC, sDC.n)
    b.flush()

    pta = b.ps("pta", [128, 512], BF16)
    ptb = b.ps("ptb", [128, 512], BF16)
    ps = [b.ps(f"ps{i}", [128, 512], F32) for i in range(4)]
    pgu = [b.ps(f"pgu{i}", [1, 512], F32) for i in range(2)]

    emit_bcast_row(b, c, sdrow, sdb, ps[0])
    b.flush()

    for half in range(2):
        off = half * 8
        with ExitStack() as hes:
            gu = b.sb(f"gu{half}", [128, 44 * TH], BF16, hes)
            gurow = b.sb(f"gurow{half}", [1, TH], F32, hes)
            m128g = b.sb(f"m128g{half}", [128, 8], F32, hes)
            with ExitStack() as xes:
                xnt = b.sb(f"xnt{half}", [128, NC * TH], BF16, xes)
                with ExitStack() as les:
                    emit_ln_phase(
                        b, c, xin, lambda cc, tt: cc * TH + (tt - off) * 128, xnt,
                        None, None, False, range(off, off + 8), les, (pta, ptb))
                    b.flush()
                with ExitStack() as ges:
                    wstg = [b.sb(f"wsg{half}{i}", [128, D], BF16, ges) for i in range(2)]
                    wstu = [b.sb(f"wsu{half}{i}", [128, D], BF16, ges) for i in range(2)]
                    wgb = [b.sb(f"wgb{half}{i}", [128, D], BF16, ges) for i in range(2)]
                    wub = [b.sb(f"wub{half}{i}", [128, D], BF16, ges) for i in range(2)]
                    grow = b.sb(f"grow{half}", [128, TH], BF16, ges)
                    urow = b.sb(f"urow{half}", [128, TH], BF16, ges)
                    stg_free = [(0, 0), (0, 0)]
                    wgb_free = [0, 0]
                    ev_free = [0, 0, 0, 0]
                    psfree = [0, 0, 0, 0]
                    growv = [0, 0]
                    for hc in range(NHC):
                        ib = hc % 2
                        b.wait("gpsimd", sV, stg_free[ib][0])
                        dg = b.dma("gpsimd", wstg[ib][:], A["l1_wg"][hc * 128:(hc + 1) * 128, :], sWA)
                        du = b.dma("gpsimd", wstu[ib][:], A["l1_wu"][hc * 128:(hc + 1) * 128, :], sWA)
                        b.wait("vector", sWA, dg)
                        b.wait("vector", sT, wgb_free[ib])
                        b.op("vector", lambda e, ib=ib: e.tensor_scalar(
                            wgb[ib][:], wstg[ib][:], 128.0, None, OP.subtract))
                        b.drain("vector")
                        b.wait("vector", sWA, du)
                        uq = b.op("vector", lambda e, ib=ib: e.tensor_scalar(
                            wub[ib][:], wstu[ib][:], 128.0, None, OP.subtract), sV)
                        stg_free[ib] = (uq, uq)
                        b.wait("tensor", sV, uq)
                        for which, base, wsrc, scol, func in (
                                ("g", 0, wgb[ib], sgc, AF.Silu), ("u", 2, wub[ib], suc, AF.Identity)):
                            for tb in range(2):
                                pg = base + tb
                                b.wait("tensor", sS, psfree[pg])
                                mmv = None
                                for cc in range(NC):
                                    mmv = b.op("tensor", lambda e, cc=cc, tb=tb, pg=pg, wsrc=wsrc: e.matmul(
                                        ps[pg][:], wsrc[:, cc * 128:(cc + 1) * 128],
                                        xnt[:, cc * TH + tb * 512: cc * TH + tb * 512 + 512],
                                        start=(cc == 0), stop=(cc == NC - 1)), sT if cc == NC - 1 else None)
                                b.wait("scalar", sT, mmv)
                                dst = grow if which == "g" else urow
                                b.wait("scalar", sV, growv[tb])
                                psfree[pg] = b.op("scalar", lambda e, tb=tb, pg=pg, dst=dst, scol=scol, hc=hc, func=func:
                                    e.activation(dst[:, tb * 512:(tb + 1) * 512], ps[pg][:], func,
                                                 bias=0.0, scale=scol[:, hc:hc + 1]), sS)
                        wgb_free[ib] = sT.n
                        for tb in range(2):
                            b.wait("vector", sS, psfree[2 + tb])
                            growv[tb] = b.op("vector", lambda e, hc=hc, tb=tb: e.tensor_tensor(
                                gu[:, hc * TH + tb * 512: hc * TH + tb * 512 + 512],
                                grow[:, tb * 512:(tb + 1) * 512], urow[:, tb * 512:(tb + 1) * 512],
                                OP.mult), sV)
                            b.wait("tensor", sV, growv[tb])
                            b.op("tensor", lambda e, hc=hc, tb=tb: e.matmul(
                                pgu[tb][:], onesc[:], gu[:, hc * TH + tb * 512: hc * TH + tb * 512 + 512],
                                start=(hc == 0), stop=(hc == NHC - 1)), sT if hc == NHC - 1 else None)
                    gusum_mm = sT.n
                    b.flush()
            b.wait("vector", sT, gusum_mm)
            for tb in range(2):
                b.op("vector", lambda e, tb=tb: e.tensor_copy(gurow[:, tb * 512:(tb + 1) * 512], pgu[tb][:]))
            b.drain("vector")
            grv = b.op("vector", lambda e: e.engine_nop(), sV)
            b.wait("tensor", sV, grv)
            for j in range(8):
                b.op("tensor", lambda e, j=j: e.matmul(
                    ps[0][:, j:j + 1], gurow[:, j * 128:(j + 1) * 128], c["ones1"][0:1, 0:1],
                    start=True, stop=True), sT if j == 7 else None)
            gtv = sT.n
            b.wait("vector", sT, gtv)
            b.op("vector", lambda e: e.tensor_scalar(m128g[:], ps[0][:, 0:8], -128.0, None, OP.mult))
            b.drain("vector")
            m128v = b.op("vector", lambda e: e.engine_nop(), sV)
            b.flush()

            with ExitStack() as des:
                wdb = [b.sb(f"wdb{half}{i}", [128, 44 * 512], BF16, des) for i in range(2)]
                tmp = [b.sb(f"dtmp{half}{i}", [128, 512], F32, des) for i in range(2)]
                wd_used = [0, 0]
                tmp_free = [0, 0]
                dmad = []
                for db in range(4):
                    ibw = db % 2
                    b.wait("gpsimd", sT, wd_used[ibw])
                    dws = []
                    for q in range(4):
                        lo = q * 11
                        dws.append(b.dma(
                            "gpsimd",
                            wdb[ibw][:].rearrange("p (hc j) -> p hc j", j=512)[:, lo:lo + 11, :],
                            A["l1_wd"][db * 5632 + lo * 128: db * 5632 + (lo + 11) * 128, :]
                            .rearrange("(hc p) j -> p hc j", p=128),
                            sWB))
                    for tt in range(off, off + 8):
                        ti = tt - off
                        pg = 2 + ti % 2
                        b.wait("tensor", sWB, dws[-1])
                        if len(dmad) >= 2:
                            b.wait("tensor", sV, dmad[-2][1])
                        b.wait("tensor", sV, grv)
                        mmv = None
                        for hc in range(NHC):
                            mmv = b.op("tensor", lambda e, hc=hc, ti=ti, pg=pg, ibw=ibw: e.matmul(
                                ps[pg][:], gu[:, hc * TH + ti * 128: hc * TH + (ti + 1) * 128],
                                wdb[ibw][:, hc * 512:(hc + 1) * 512],
                                start=(hc == 0), stop=(hc == NHC - 1)), sT if hc == NHC - 1 else None)
                        wd_used[ibw] = mmv
                        ibt = ti % 2
                        b.wait("vector", sT, mmv)
                        b.wait("vector", sV, m128v)
                        b.wait("vector", sDO, tmp_free[ibt])
                        vv = b.op("vector", lambda e, ti=ti, pg=pg, ibt=ibt, db=db: e.scalar_tensor_tensor(
                            tmp[ibt][:], ps[pg][:], m128g[:, ti:ti + 1], sdb[:, db * 512:(db + 1) * 512],
                            OP.add, OP.mult), sV)
                        dmad.append((None, vv))
                        b.wait("scalar", sV, vv)
                        tmp_free[ibt] = b.dma(
                            "scalar", ddram[tt * 128:(tt + 1) * 128, db * 512:(db + 1) * 512],
                            tmp[ibt][:], sDO)
                b.flush()

        with ExitStack() as fes:
            dxt = [b.sb(f"fx{half}{i}", [128, D], F32, fes) for i in range(2)]
            ddt = [b.sb(f"fd{half}{i}", [128, D], F32, fes) for i in range(2)]
            x4t = [b.sb(f"fo{half}{i}", [128, D], F32, fes) for i in range(2)]
            bfree = [0, 0]
            ofree = [0, 0]
            b.wait("sync", sDO, sDO.n)
            for tt in range(off, off + 8):
                ib = tt % 2
                b.wait("sync", sV, bfree[ib])
                d1 = b.dma("sync", dxt[ib][:], xin[tt * 128:(tt + 1) * 128, :], sDX)
                d2 = b.dma("sync", ddt[ib][:], ddram[tt * 128:(tt + 1) * 128, :], sDX)
                b.wait("vector", sDX, d2)
                b.wait("vector", sDO, ofree[ib])
                bfree[ib] = b.op("vector", lambda e, ib=ib: e.tensor_tensor(
                    x4t[ib][:], dxt[ib][:], ddt[ib][:], OP.add), sV)
                b.wait("scalar", sV, bfree[ib])
                ofree[ib] = b.dma("scalar", xout[tt * 128:(tt + 1) * 128, :], x4t[ib][:], sDO)
            b.wait("scalar", sDO, sDO.n)
            b.flush()


P4_INPUTS = ["ident_bf", "ident_f32", "l1_wg", "l1_wu", "l1_wd", "l1_sg", "l1_su", "l1_sd"]


def build_p4_standalone():
    nc = bass.Bass("TRN2", target_bir_lowering=False)
    A = declare_shared(nc, P4_INPUTS)
    xin = nc.dram_tensor("xin", [T, D], F32, kind="ExternalInput").ap()
    xout = nc.dram_tensor("xout", [T, D], F32, kind="ExternalOutput").ap()
    ddram = nc.dram_tensor("ddram", [T, D], F32, kind="Internal").ap()
    with ExitStack() as es:
        emit_p4_l1ffn(nc, es, A, xin, xout, ddram)
    return nc


P3_INPUTS = ["ident_bf", "ident_f32", "masks", "l1_wqkv", "l1_wproj"]


def build_p3_standalone(ntiles=NT):
    nc = bass.Bass("TRN2", target_bir_lowering=False)
    A = declare_shared(nc, P3_INPUTS)
    xin = nc.dram_tensor("xin", [T, D], F32, kind="ExternalInput").ap()
    xout = nc.dram_tensor("xout", [T, D], F32, kind="ExternalOutput").ap()
    with ExitStack() as es:
        emit_p3_l1attn(nc, es, A, xin, xout, ntiles=ntiles)
    return nc


P1_INPUTS = ["ident_bf", "ident_f32", "masks", "l0_wq", "l0_wk", "l0_wv", "l0_wo",
             "l0_lng", "l0_lnb", "l0_bq", "l0_bk", "l0_bv", "l0_bo"]


def build_p1_standalone(ntiles=NT):
    nc = bass.Bass("TRN2", target_bir_lowering=False)
    A = declare_shared(nc, P1_INPUTS)
    xin = nc.dram_tensor("xin", [T, D], F32, kind="ExternalInput").ap()
    xout = nc.dram_tensor("xout", [T, D], F32, kind="ExternalOutput").ap()
    with ExitStack() as es:
        emit_p1_l0attn(nc, es, A, xin, xout, ntiles=ntiles)
    return nc



def _prep_shared(inp):
    A = {}
    A["ident_bf"] = np.eye(128, dtype=BFNP)
    A["ident_f32"] = np.eye(128, dtype=np.float32)
    p = np.arange(128)[:, None]
    mcol = np.arange(896)[None, :]
    A["masks"] = np.where(mcol <= p + 384, 0.0, NEG).astype(np.float32)

    def colchunks(v):
        return np.ascontiguousarray(v.reshape(-1, 128).T)

    A["l0_wq"] = np.ascontiguousarray(
        inp["L0_attn_Wq"].reshape(16, 128, 16, 128).transpose(2, 1, 0, 3).reshape(2048, 2048))
    A["l0_wk"] = np.ascontiguousarray(
        inp["L0_attn_Wk"].reshape(16, 128, 16, 128).transpose(2, 1, 0, 3).reshape(2048, 2048))
    A["l0_wv"] = np.ascontiguousarray(
        inp["L0_attn_Wv"].reshape(16, 128, 4, 512).transpose(2, 1, 0, 3).reshape(512, 8192))
    A["l0_wo"] = np.ascontiguousarray(inp["L0_attn_Wo"])
    A["l0_lng"] = colchunks(inp["L0_attn_ln_g"])
    A["l0_lnb"] = colchunks(inp["L0_attn_ln_b"])
    A["l0_bq"] = colchunks(inp["L0_attn_bq"])
    A["l0_bk"] = colchunks(inp["L0_attn_bk"])
    A["l0_bv"] = colchunks(inp["L0_attn_bv"])
    A["l0_bo"] = np.ascontiguousarray(inp["L0_attn_bo"].reshape(1, 2048))
    A["l0f_w"] = np.ascontiguousarray(inp["L0_ffn_weight"].T)
    A["l0f_gam"] = colchunks(inp["L0_ffn_ln_gamma"])
    A["l0f_bet"] = colchunks(inp["L0_ffn_ln_beta"])
    A["l0f_bias"] = np.ascontiguousarray(inp["L0_ffn_bias"].reshape(1, 2048))
    A["l1_wqkv"] = np.ascontiguousarray(
        inp["L1_attn_weight_qkv"].T.reshape(16, 128, 24, 128).transpose(2, 1, 0, 3).reshape(3072, 2048))
    A["l1_wproj"] = np.ascontiguousarray(inp["L1_attn_weight_proj"].T)
    A["l1_wg"] = np.ascontiguousarray(
        inp["L1_ffn_wq_gate"].T.reshape(16, 128, 44, 128).transpose(2, 1, 0, 3).reshape(5632, 2048))
    A["l1_wu"] = np.ascontiguousarray(
        inp["L1_ffn_wq_up"].T.reshape(16, 128, 44, 128).transpose(2, 1, 0, 3).reshape(5632, 2048))
    A["l1_wd"] = np.ascontiguousarray(
        inp["L1_ffn_wq_down"].T.reshape(44, 128, 4, 512).transpose(2, 0, 1, 3).reshape(22528, 512))
    A["l1_sg"] = colchunks(inp["L1_ffn_s_gate"])
    A["l1_su"] = colchunks(inp["L1_ffn_s_up"])
    A["l1_sd"] = np.ascontiguousarray(inp["L1_ffn_s_down"].reshape(1, 2048))
    return A


SHARED_SPECS = {
    "ident_bf": ([128, 128], BF16), "ident_f32": ([128, 128], F32),
    "masks": ([128, 896], F32),
    "l0_wq": ([2048, 2048], F32), "l0_wk": ([2048, 2048], F32),
    "l0_wv": ([512, 8192], F32), "l0_wo": ([2048, 2048], F32),
    "l0_lng": ([128, 16], F32), "l0_lnb": ([128, 16], F32),
    "l0_bq": ([128, 16], F32), "l0_bk": ([128, 16], F32), "l0_bv": ([128, 16], F32),
    "l0_bo": ([1, 2048], F32),
    "l0f_w": ([2048, 2048], F32), "l0f_gam": ([128, 16], F32),
    "l0f_bet": ([128, 16], F32), "l0f_bias": ([1, 2048], F32),
    "l1_wqkv": ([3072, 2048], F32), "l1_wproj": ([2048, 2048], F32),
    "l1_wg": ([5632, 2048], I32), "l1_wu": ([5632, 2048], I32), "l1_wd": ([22528, 512], I32),
    "l1_sg": ([128, 44], F32), "l1_su": ([128, 44], F32), "l1_sd": ([1, 2048], F32),
}


def declare_shared(nc, names):
    aps = {}
    for n in names:
        shape, dt = SHARED_SPECS[n]
        aps[n] = nc.dram_tensor(n, shape, dt, kind="ExternalInput").ap()
    return aps


P2_INPUTS = ["ident_bf", "ident_f32", "l0f_w", "l0f_gam", "l0f_bet", "l0f_bias"]


def build_p2_standalone(debug=False, ntiles=NT):
    nc = bass.Bass("TRN2", target_bir_lowering=False)
    A = declare_shared(nc, P2_INPUTS)
    xin = nc.dram_tensor("xin", [T, D], F32, kind="ExternalInput").ap()
    xout = nc.dram_tensor("xout", [T, D], F32, kind="ExternalOutput").ap()
    dbg = None
    if debug:
        dbg = {
            "rinv": nc.dram_tensor("d_rinv", [128, 1], F32, kind="ExternalOutput").ap(),
            "wq0": nc.dram_tensor("d_wq0", [128, D], BF16, kind="ExternalOutput").ap(),
            "u0": nc.dram_tensor("d_u0", [128, D], BF16, kind="ExternalOutput").ap(),
            "xnt0": nc.dram_tensor("d_xnt0", [128, D], BF16, kind="ExternalOutput").ap(),
            "biasb": nc.dram_tensor("d_biasb", [128, D], F32, kind="ExternalOutput").ap(),
            "coef": nc.dram_tensor("d_coef", [128, 8], F32, kind="ExternalOutput").ap(),
            "nb": nc.dram_tensor("d_nb", [128, 1], F32, kind="ExternalOutput").ap(),
        }
    with ExitStack() as es:
        emit_p2_l0ffn(nc, es, A, xin, xout, dbg=dbg, ntiles=ntiles)
    return nc


def build_fused():
    nc = bass.Bass("TRN2", target_bir_lowering=False)
    names = sorted(set(P1_INPUTS) | set(P2_INPUTS) | set(P3_INPUTS) | set(P4_INPUTS))
    A = declare_shared(nc, names)
    x0 = nc.dram_tensor("xin", [T, D], F32, kind="ExternalInput").ap()
    x4 = nc.dram_tensor("xout", [T, D], F32, kind="ExternalOutput").ap()
    x1 = nc.dram_tensor("x1s", [T, D], F32, kind="Internal").ap()
    x2 = nc.dram_tensor("x2s", [T, D], F32, kind="Internal").ap()
    x3 = nc.dram_tensor("x3s", [T, D], F32, kind="Internal").ap()
    dd = nc.dram_tensor("ddram", [T, D], F32, kind="Internal").ap()
    sems = {}
    with ExitStack() as ses:
        with ExitStack() as es:
            emit_p1_l0attn(nc, es, A, x0, x1, tag="p1", sems=sems, sems_es=ses)
        with ExitStack() as es:
            emit_p2_l0ffn(nc, es, A, x1, x2, tag="p2", sems=sems, sems_es=ses)
        with ExitStack() as es:
            emit_p3_l1attn(nc, es, A, x2, x3, tag="p3", sems=sems, sems_es=ses)
        with ExitStack() as es:
            emit_p4_l1ffn(nc, es, A, x3, x4, dd, tag="p4", sems=sems, sems_es=ses)
    return nc, names


_BUILD_CACHE = {}


def _get_phase_programs():
    if "phases" not in _BUILD_CACHE:
        _BUILD_CACHE["phases"] = [
            (build_p1_standalone(), P1_INPUTS),
            (build_p2_standalone(), P2_INPUTS),
            (build_p3_standalone(), P3_INPUTS),
            (build_p4_standalone(), P4_INPUTS),
        ]
    return _BUILD_CACHE["phases"]


def kernel(**inputs):
    A = _prep_shared(inputs)
    if "fused" not in _BUILD_CACHE:
        _BUILD_CACHE["fused"] = build_fused()
    nc, names = _BUILD_CACHE["fused"]
    in_maps = []
    for i in range(NCORES):
        m = {k: A[k] for k in names}
        m["xin"] = np.ascontiguousarray(inputs["L0_attn_X"][i])
        in_maps.append(m)
    res = run_bass_kernel_spmd(nc, in_maps, core_ids=list(range(NCORES)))
    return np.stack([res.results[i]["xout"] for i in range(NCORES)], axis=0)


# version 05
# speedup vs baseline: 1.1850x; 1.0843x over previous
import numpy as np
import ml_dtypes
from contextlib import ExitStack
import concourse.bass as bass
import concourse.mybir as mybir
from concourse.bass_utils import run_bass_kernel_spmd
from concourse.alu_op_type import AluOpType as OP

F32 = mybir.dt.float32
BF16 = mybir.dt.bfloat16
I32 = mybir.dt.int32
AF = mybir.ActivationFunctionType
AX = mybir.AxisListType.X
BFNP = ml_dtypes.bfloat16

ENGINES = ("sync", "tensor", "vector", "scalar", "gpsimd")
NCORES = 8
T = 2048
D = 2048
NT = T // 128
NC = D // 128
HFF = 5632
NHC = HFF // 128
INVD = 1.0 / D
EPS = 1e-5
ASCALE = 0.08838834764831845
NEG = -1.0e30


class Sem:
    __slots__ = ("h", "n")

    def __init__(self, h):
        self.h = h
        self.n = 0


class Bld:

    def __init__(self, nc, es, tag, sems=None, sems_es=None):
        self.nc, self.es, self.tag = nc, es, tag
        self.q = {e: [] for e in ENGINES}
        self._sems = sems if sems is not None else {}
        self._sems_es = sems_es
        self._uid = 0

    def fresh(self):
        self._uid += 1
        return self._uid

    def sem(self, name):
        if name not in self._sems:
            es = self._sems_es or self.es
            self._sems[name] = Sem(es.enter_context(self.nc.semaphore(f"sem_{name}_{self.tag}")))
        return self._sems[name]

    def sb(self, name, shape, dt, es=None):
        return (es or self.es).enter_context(self.nc.sbuf_tensor(f"{self.tag}_{name}", shape, dt))

    def ps(self, name, shape, dt, es=None):
        return (es or self.es).enter_context(self.nc.psum_tensor(f"{self.tag}_{name}", shape, dt))

    def op(self, eng, fn, sem=None, k=1):
        if sem is None:
            self.q[eng].append(fn)
            return None
        sem.n += k
        v = sem.n
        self.q[eng].append(lambda e, fn=fn, s=sem.h, k=k: fn(e).then_inc(s, k))
        return v

    def dma(self, eng, dst, src, sem):
        return self.op(eng, lambda e, d=dst, s=src: e.dma_start(d, s), sem, 16)

    def wait(self, eng, sem, v):
        if v:
            self.q[eng].append(lambda e, s=sem.h, v=v: e.wait_ge(s, v))

    def drain(self, eng):
        self.q[eng].append(lambda e: e.drain())

    def flush(self):
        q = self.q
        self.q = {e: [] for e in ENGINES}
        with self.nc.Block() as blk:
            for name in ENGINES:
                ops = q[name]
                if ops:
                    getattr(blk, name)(lambda eng, ops=ops: [f(eng) for f in ops])


def emit_consts(b, A):
    c = {}
    c["idb"] = b.sb("idb", [128, 128], BF16)
    c["idf"] = b.sb("idf", [128, 128], F32)
    c["ones1"] = b.sb("ones1", [1, 128], F32)
    c["epsc"] = b.sb("epsc", [128, 1], F32)
    c["onescol"] = b.sb("onescol", [128, 1], BF16)
    sDC = b.sem("dc")
    b.dma("sync", c["idb"][:], A["ident_bf"], sDC)
    b.dma("sync", c["idf"][:], A["ident_f32"], sDC)
    b.op("vector", lambda e: e.memset(c["ones1"][:], 1.0))
    b.op("vector", lambda e: e.memset(c["epsc"][:], EPS))
    b.op("vector", lambda e: e.memset(c["onescol"][:], 1.0))
    b.wait("sync", sDC, sDC.n)
    return c


def emit_bcast_row(b, c, row_sb, dst_sb, pa, width=D):
    sT, sV = b.sem("t"), b.sem("v")
    for ob in range(width // 512):
        vprev = sV.n
        b.wait("tensor", sV, vprev)
        b.op("tensor", lambda e, ob=ob: e.matmul(
            pa[:], c["ones1"][:], row_sb[:, ob * 512:(ob + 1) * 512], start=True, stop=True), sT)
        b.wait("vector", sT, sT.n)
        b.op("vector", lambda e, ob=ob: e.tensor_copy(dst_sb[:, ob * 512:(ob + 1) * 512], pa[:]), sV)


def emit_ln_stats(b, c, x_ap, bn6, mv, sc, nb, double_ln):
    sV, sS = b.sem("v"), b.sem("s")
    for i in range(4):
        b.op("vector", lambda e, i=i: e.bn_stats(bn6[:, i * 6:(i + 1) * 6], x_ap[:, i * 512:(i + 1) * 512]))
    b.drain("vector")
    vs = b.op("vector", lambda e: e.bn_aggr(mv[:], bn6[:]), sV)
    b.wait("scalar", sV, vs)
    ss = b.op("scalar", lambda e: e.activation(sc[:, 2:3], mv[:, 1:2], AF.Sqrt, bias=c["epsc"][:], scale=1.0), sS)
    b.wait("vector", sS, ss)
    b.op("vector", lambda e: e.reciprocal(sc[:, 3:4], sc[:, 2:3]))
    b.drain("vector")
    if double_ln:
        b.op("vector", lambda e: e.tensor_scalar(sc[:, 4:5], sc[:, 3:4], sc[:, 3:4], None, OP.mult))
        b.drain("vector")
        vv = b.op("vector", lambda e: e.tensor_scalar(sc[:, 5:6], mv[:, 1:2], sc[:, 4:5], None, OP.mult), sV)
        b.wait("scalar", sV, vv)
        ss = b.op("scalar", lambda e: e.activation(sc[:, 6:7], sc[:, 5:6], AF.Sqrt, bias=c["epsc"][:], scale=1.0), sS)
        b.wait("vector", sS, ss)
        b.op("vector", lambda e: e.reciprocal(sc[:, 7:8], sc[:, 6:7]))
        b.drain("vector")
        b.op("vector", lambda e: e.tensor_scalar(sc[:, 0:1], sc[:, 3:4], sc[:, 7:8], None, OP.mult))
    else:
        b.op("vector", lambda e: e.tensor_copy(sc[:, 0:1], sc[:, 3:4]))
    b.drain("vector")
    vv = b.op("vector", lambda e: e.scalar_tensor_tensor(nb[:], mv[:, 0:1], -1.0, sc[:, 0:1], OP.mult, OP.mult), sV)
    return vv


def emit_p2_l0ffn(nc, es, A, xin, xout, tag="p2", dbg=None, ntiles=NT, sems=None, sems_es=None):
    b = Bld(nc, es, tag, sems=sems, sems_es=sems_es)
    sV, sS, sT, sG = b.sem("v"), b.sem("s"), b.sem("t"), b.sem("g")
    sDC, sDW, sDX, sDO = b.sem("dc"), b.sem("dw"), b.sem("dx"), b.sem("do")

    c = emit_consts(b, A)
    gam = b.sb("gam", [128, 16], F32)
    bet = b.sb("bet", [128, 16], F32)
    brow = b.sb("brow", [1, D], F32)
    bias_b = b.sb("bias_b", [128, D], F32)
    wq = b.sb("wq", [128, NC * D], BF16)
    pa = b.ps("pa", [128, 512], F32)
    pb = b.ps("pb", [128, 512], F32)
    pta = b.ps("pta", [128, 512], BF16)
    ptb = b.ps("ptb", [128, 512], BF16)

    b.dma("sync", gam[:], A["l0f_gam"], sDC)
    b.dma("sync", bet[:], A["l0f_bet"], sDC)
    b.dma("sync", brow[:], A["l0f_bias"], sDC)
    b.wait("sync", sDC, sDC.n)
    b.flush()

    emit_bcast_row(b, c, brow, bias_b, pa)
    b.flush()

    with ExitStack() as wes:
        stg = [b.sb(f"stg{i}", [128, D], F32, wes) for i in range(2)]
        acc = b.sb("acc", [128, 16], F32, wes)
        dump = b.sb("dump", [128, D], F32, wes)
        small = b.sb("small", [1, 4], F32, wes)
        rinv = b.sb("rinv", [128, 1], F32, wes)
        tmp_col = b.sb("tmpcol", [128, 1], F32, wes)
        t1 = b.sb("t1", [128, D], BF16, wes)
        t2 = b.sb("t2", [128, D], BF16, wes)
        for cc in range(NC):
            if cc >= 2:
                b.wait("sync", sS, sS.n)
            dv = b.dma("sync", stg[cc % 2][:], A["l0f_w"][cc * 128:(cc + 1) * 128, :], sDW)
            b.wait("scalar", sDW, dv)
            b.op("scalar", lambda e, cc=cc: e.activation(
                dump[:], stg[cc % 2][:], AF.Abs, accum_out=acc[:, cc:cc + 1]), sS)
        b.flush()
        b.op("vector", lambda e: e.reduce_sum(tmp_col[:], acc[:], axis=AX))
        b.drain("vector")
        vv = b.op("vector", lambda e: e.engine_nop(), sV)
        b.wait("tensor", sV, vv)
        tv = b.op("tensor", lambda e: e.transpose(pa[0:1, 0:128], tmp_col[:], c["idf"][:]), sT)
        b.wait("vector", sT, tv)
        b.op("vector", lambda e: e.reduce_sum(small[:, 0:1], pa[0:1, 0:128], axis=AX))
        b.drain("vector")
        b.op("vector", lambda e: e.tensor_scalar(small[:, 1:2], small[:, 0:1], 1.0 / (D * D), EPS, OP.mult, OP.add))
        b.drain("vector")
        vv = b.op("vector", lambda e: e.reciprocal(small[:, 2:3], small[:, 1:2]), sV)
        b.wait("tensor", sV, vv)
        tv = b.op("tensor", lambda e: e.matmul(pb[:, 0:1], c["ones1"][:], small[:, 2:3], start=True, stop=True), sT)
        b.wait("vector", sT, tv)
        b.op("vector", lambda e: e.tensor_copy(rinv[:], pb[:, 0:1]))
        b.drain("vector")
        b.flush()
        for cc in range(NC):
            if cc >= 2:
                b.wait("sync", sV, sV.n)
            dv = b.dma("sync", stg[cc % 2][:], A["l0f_w"][cc * 128:(cc + 1) * 128, :], sDW)
            b.wait("vector", sDW, dv)
            b.op("vector", lambda e, cc=cc: e.tensor_scalar(
                t1[:], stg[cc % 2][:], rinv[:], 0.5, OP.mult, OP.is_gt))
            b.drain("vector")
            b.op("vector", lambda e, cc=cc: e.tensor_scalar(
                t2[:], stg[cc % 2][:], rinv[:], -0.5, OP.mult, OP.is_lt))
            b.drain("vector")
            b.op("vector", lambda e, cc=cc: e.tensor_tensor(
                wq[:, cc * D:(cc + 1) * D], t1[:], t2[:], OP.subtract), sV)
        b.flush()
        if dbg is not None:
            b.dma("scalar", dbg["rinv"], rinv[:], sDO)
            b.dma("scalar", dbg["wq0"], wq[:, 0:D], sDO)
            b.wait("scalar", sDO, sDO.n)
            b.flush()

    xt = [b.sb(f"xt{i}", [128, D], F32) for i in range(2)]
    ub = [b.sb(f"u{i}", [128, D], BF16) for i in range(2)]
    xnt = [b.sb(f"xnt{i}", [128, D], BF16) for i in range(2)]
    xb = [b.sb(f"xb{i}", [128, D], F32) for i in range(2)]
    x2t = [b.sb(f"x2t{i}", [128, D], F32) for i in range(2)]
    bn6 = b.sb("bn6", [128, 24], F32)
    mv = b.sb("mv", [128, 2], F32)
    scc = b.sb("scc", [128, 8], F32)
    nb = b.sb("nb", [128, 1], F32)
    xt_free = [(0, 0, 0), (0, 0, 0)]
    u_free = [0, 0]
    xnt_free = [0, 0]
    x2_free = [0, 0]
    xb_free = [0, 0]
    pmm = [pa, pb]
    pmm_free = [0, 0]
    ptr = [pta, ptb]
    ptr_free = [0, 0]
    last_uv = 0

    for tt in range(ntiles):
        ib = tt % 2
        b.wait("sync", sG, xt_free[ib][0])
        b.wait("sync", sV, xt_free[ib][1])
        b.wait("sync", sS, xt_free[ib][2])
        dv = b.dma("sync", xt[ib][:], xin[tt * 128:(tt + 1) * 128, :], sDX)
        b.wait("vector", sDX, dv)
        b.wait("vector", sS, last_uv)
        coeff_v = emit_ln_stats(b, c, xt[ib], bn6, mv, scc, nb, double_ln=True)
        b.wait("scalar", sDX, dv)
        b.wait("scalar", sV, coeff_v)
        b.wait("scalar", sT, u_free[ib])
        uv = b.op("scalar", lambda e, ib=ib: e.activation(
            ub[ib][:], xt[ib][:], AF.Identity, bias=nb[:], scale=scc[:, 0:1]), sS)
        last_uv = uv
        b.wait("gpsimd", sDX, dv)
        b.wait("gpsimd", sV, xb_free[ib])
        gv = b.op("gpsimd", lambda e, ib=ib: e.tensor_tensor(
            xb[ib][:], xt[ib][:], bias_b[:], OP.add), sG)
        xt_free[ib] = (gv, coeff_v, uv)
        b.wait("tensor", sS, uv)
        b.wait("scalar", sT, xnt_free[ib])
        for grp in range(4):
            pg = grp % 2
            b.wait("tensor", sS, ptr_free[pg])
            for j in range(4):
                cc = grp * 4 + j
                tv = b.op("tensor", lambda e, cc=cc, pg=pg, j=j, ib=ib: e.transpose(
                    ptr[pg][:, j * 128:(j + 1) * 128], ub[ib][:, cc * 128:(cc + 1) * 128], c["idb"][:]),
                    sT if j == 3 else None)
            if grp == 3:
                u_free[ib] = sT.n
            b.wait("scalar", sT, sT.n)
            for j in range(4):
                cc = grp * 4 + j
                ev = b.op("scalar", lambda e, cc=cc, pg=pg, j=j, ib=ib: e.activation(
                    xnt[ib][:, cc * 128:(cc + 1) * 128], ptr[pg][:, j * 128:(j + 1) * 128],
                    AF.Identity, bias=bet[:, cc:cc + 1], scale=gam[:, cc:cc + 1]), sS)
            ptr_free[pg] = sS.n
        xnt_ready = sS.n
        b.wait("tensor", sS, xnt_ready)
        for ob in range(4):
            pg = ob % 2
            b.wait("tensor", sV, pmm_free[pg])
            for cc in range(NC):
                mmv = b.op("tensor", lambda e, cc=cc, ob=ob, pg=pg, ib=ib: e.matmul(
                    pmm[pg][:], xnt[ib][:, cc * 128:(cc + 1) * 128],
                    wq[:, cc * D + ob * 512: cc * D + (ob + 1) * 512],
                    start=(cc == 0), stop=(cc == NC - 1)), sT if cc == NC - 1 else None)
            b.wait("vector", sT, mmv)
            b.wait("vector", sG, xt_free[ib][0])
            b.wait("vector", sDO, x2_free[ib])
            pmm_free[pg] = b.op("vector", lambda e, ob=ob, pg=pg, ib=ib: e.tensor_tensor(
                x2t[ib][:, ob * 512:(ob + 1) * 512], pmm[pg][:], xb[ib][:, ob * 512:(ob + 1) * 512], OP.add), sV)
        xb_free[ib] = sV.n
        xnt_free[ib] = sT.n
        b.wait("scalar", sV, sV.n)
        x2_free[ib] = b.dma("scalar", xout[tt * 128:(tt + 1) * 128, :], x2t[ib][:], sDO)
        if dbg is not None and tt == 0:
            b.dma("scalar", dbg["u0"], ub[0][:], sDO)
            b.dma("scalar", dbg["xnt0"], xnt[0][:], sDO)
            b.dma("scalar", dbg["biasb"], bias_b[:], sDO)
            b.dma("scalar", dbg["coef"], scc[:], sDO)
            b.dma("scalar", dbg["nb"], nb[:], sDO)
    b.wait("scalar", sDO, sDO.n)
    b.flush()


def emit_ln_phase(b, c, xin_ap, col_of, xnt_sb, gamcol, betcol, double_ln, tiles, les, ptpair):
    sV, sS, sT, sDX = b.sem("v"), b.sem("s"), b.sem("t"), b.sem("dx")
    u_ = b.fresh()
    xt = [b.sb(f"lnxt{i}_{u_}", [128, D], F32, les) for i in range(2)]
    ub = [b.sb(f"lnu{i}_{u_}", [128, D], BF16, les) for i in range(2)]
    bn6 = b.sb(f"lnbn6_{u_}", [128, 24], F32, les)
    mv = b.sb(f"lnmv_{u_}", [128, 2], F32, les)
    scc = b.sb(f"lnscc_{u_}", [128, 8], F32, les)
    nb = b.sb(f"lnnb_{u_}", [128, 1], F32, les)
    ptr = ptpair
    ptr_free = [0, 0]
    xt_free = [(0, 0), (0, 0)]
    u_free = [0, 0]
    last_uv = 0
    for ti, tt in enumerate(tiles):
        ib = ti % 2
        b.wait("sync", sV, xt_free[ib][0])
        b.wait("sync", sS, xt_free[ib][1])
        dv = b.dma("sync", xt[ib][:], xin_ap[tt * 128:(tt + 1) * 128, :], sDX)
        b.wait("vector", sDX, dv)
        b.wait("vector", sS, last_uv)
        coeff_v = emit_ln_stats(b, c, xt[ib], bn6, mv, scc, nb, double_ln=double_ln)
        b.wait("scalar", sDX, dv)
        b.wait("scalar", sV, coeff_v)
        b.wait("scalar", sT, u_free[ib])
        uv = b.op("scalar", lambda e, ib=ib: e.activation(
            ub[ib][:], xt[ib][:], AF.Identity, bias=nb[:], scale=scc[:, 0:1]), sS)
        last_uv = uv
        xt_free[ib] = (coeff_v, uv)
        b.wait("tensor", sS, uv)
        for grp in range(4):
            pg = grp % 2
            b.wait("tensor", sS, ptr_free[pg])
            for j in range(4):
                cc = grp * 4 + j
                b.op("tensor", lambda e, cc=cc, pg=pg, j=j, ib=ib: e.transpose(
                    ptr[pg][:, j * 128:(j + 1) * 128], ub[ib][:, cc * 128:(cc + 1) * 128], c["idb"][:]),
                    sT if j == 3 else None)
            if grp == 3:
                u_free[ib] = sT.n
            b.wait("scalar", sT, sT.n)
            for j in range(4):
                cc = grp * 4 + j
                if gamcol is not None:
                    b.op("scalar", lambda e, cc=cc, pg=pg, j=j, tt=tt: e.activation(
                        xnt_sb[:, col_of(cc, tt):col_of(cc, tt) + 128], ptr[pg][:, j * 128:(j + 1) * 128],
                        AF.Identity, bias=betcol(cc), scale=gamcol(cc)), sS)
                else:
                    b.op("scalar", lambda e, cc=cc, pg=pg, j=j, tt=tt: e.activation(
                        xnt_sb[:, col_of(cc, tt):col_of(cc, tt) + 128], ptr[pg][:, j * 128:(j + 1) * 128],
                        AF.Copy), sS)
            ptr_free[pg] = sS.n
    return sS.n


def emit_head_bound(b, c, st, qh, kh, nmx, mcol, bufs, ready):
    sV, sS, sT = b.sem("v"), b.sem("s"), b.sem("t")
    pbc, pav = bufs["pbc"], bufs["pav"]
    sqa, sqb_ = bufs["prow"][2], bufs["prow"][3]
    b.wait("scalar", sS, ready)
    b.wait("scalar", sT, st["nmm"])
    b.wait("scalar", sT, st["prow_free"][2])
    b.wait("scalar", sT, st["prow_free"][3])
    sqv = b.op("scalar", lambda e: e.activation(sqa[:], qh[:], AF.Square), sS)
    sqv = b.op("scalar", lambda e: e.activation(sqb_[:], kh[:], AF.Square), sS)
    pp = [pbc, pav]
    srcs = [(sqa, 0), (sqa, 1), (sqa, 2), (sqa, 3), (sqb_, 0), (sqb_, 1), (sqb_, 2), (sqb_, 3)]
    b.wait("tensor", sS, sqv)
    b.wait("tensor", sV, st["pav_ev"])
    b.wait("tensor", sV, st["nred"])
    for i in range(0, 8, 2):
        for j in range(2):
            sq_, tb = srcs[i + j]
            st["nmm"] = b.op("tensor", lambda e, sq_=sq_, tb=tb, j=j: e.matmul(
                pp[j][0:1, 0:512], c["onescol"][:], sq_[:, tb * 512:(tb + 1) * 512],
                start=True, stop=True), sT)
        b.wait("vector", sT, st["nmm"])
        for j in range(2):
            st["nred"] = b.op("vector", lambda e, i=i, j=j: e.reduce_max(
                nmx[:, i + j:i + j + 1], pp[j][0:1, 0:512], axis=AX), sV)
        if i < 6:
            b.wait("tensor", sV, st["nred"])
    b.wait("vector", sV, st["nred"])
    b.op("vector", lambda e: e.reduce_max(nmx[:, 8:9], nmx[:, 0:4], axis=AX))
    b.drain("vector")
    b.op("vector", lambda e: e.reduce_max(nmx[:, 9:10], nmx[:, 4:8], axis=AX))
    b.drain("vector")
    vv = b.op("vector", lambda e: e.tensor_tensor(nmx[:, 10:11], nmx[:, 8:9], nmx[:, 9:10], OP.mult), sV)
    b.wait("scalar", sV, vv)
    sv = b.op("scalar", lambda e: e.activation(nmx[:, 11:12], nmx[:, 10:11], AF.Sqrt), sS)
    b.wait("vector", sS, sv)
    vv = b.op("vector", lambda e: e.tensor_scalar(nmx[:, 12:13], nmx[:, 11:12], -ASCALE, None, OP.mult), sV)
    b.wait("tensor", sV, vv)
    tv = b.op("tensor", lambda e: e.matmul(pbc[:, 0:1], c["ones1"][:], nmx[:, 12:13],
                                           start=True, stop=True), sT)
    b.wait("vector", sT, tv)
    mv = b.op("vector", lambda e: e.tensor_copy(mcol[:], pbc[:, 0:1]), sV)
    st["nred"] = mv
    return mv


def emit_attn_head(b, c, st, qh, kh, vtile, avt_slice, bufs, ready, mcol, mrdy):
    sV, sS, sT = b.sem("v"), b.sem("s"), b.sem("t")
    ps, pta, ptb, pav = bufs["ps"], bufs["pta"], bufs["ptb"], bufs["pav"]
    prow, ls, lc, msk, PT = bufs["prow"], bufs["ls"], bufs["lc"], bufs["msk"], bufs["PT"]
    b.wait("tensor", sS, ready)
    first_exp = True
    for qb in range(4):
        nkb = qb + 1
        gsz = max(1, 4 // nkb)
        for g0 in range(0, 4, gsz):
            t4s = list(range(g0, g0 + gsz))
            bank = lambda t4, kb, nkb=nkb, gsz=gsz: ((t4 % gsz) * nkb + kb) % 4
            mmv_of = {}
            for t4 in t4s:
                tau = qb * 4 + t4
                for kb in range(nkb):
                    bk = bank(t4, kb)
                    b.wait("tensor", sS, st["psfree"][bk])
                    v = b.op("tensor", lambda e, bk=bk, kb=kb, tau=tau: e.matmul(
                        ps[bk][:], qh[:, tau * 128:(tau + 1) * 128], kh[:, kb * 512:(kb + 1) * 512],
                        start=True, stop=True), sT)
                    mmv_of[(t4, kb)] = v
            mv_of = {}
            for t4 in t4s:
                bd = bank(t4, qb)
                b.wait("vector", sT, mmv_of[(t4, qb)])
                mv_of[t4] = b.op("vector", lambda e, bd=bd, t4=t4: e.tensor_tensor(
                    ps[bd][:], ps[bd][:], msk[:, (3 - t4) * 128:(3 - t4) * 128 + 512], OP.add), sV)
            if first_exp:
                b.wait("scalar", sV, mrdy)
                first_exp = False
            for t4 in t4s:
                b.wait("scalar", sT, st["prow_free"][t4])
                for kb in range(nkb):
                    bk = bank(t4, kb)
                    if kb == qb:
                        b.wait("scalar", sV, mv_of[t4])
                    else:
                        b.wait("scalar", sT, mmv_of[(t4, kb)])
                    st["psfree"][bk] = b.op("scalar", lambda e, bk=bk, kb=kb, t4=t4: e.activation(
                        prow[t4][:, kb * 512:(kb + 1) * 512], ps[bk][:], AF.Exp,
                        bias=mcol[:], scale=ASCALE, accum_out=ls[:, t4 * 4 + kb:t4 * 4 + kb + 1]), sS)
            for t4 in t4s:
                b.wait("vector", sS, st["psfree"][bank(t4, nkb - 1)])
                b.op("vector", lambda e, t4=t4, nkb=nkb: e.reduce_sum(
                    lc[:, t4:t4 + 1], ls[:, t4 * 4:t4 * 4 + nkb], axis=AX))
                b.drain("vector")
                b.op("vector", lambda e, t4=t4: e.reciprocal(lc[:, 4 + t4:5 + t4], lc[:, t4:t4 + 1]))
                b.drain("vector")
                st["pscaled"][t4] = b.op("vector", lambda e, t4=t4, nkb=nkb: e.tensor_scalar(
                    prow[t4][:, 0:nkb * 512], prow[t4][:, 0:nkb * 512], lc[:, 4 + t4:5 + t4],
                    None, OP.mult), sV)
                b.drain("vector")
        nk = 4 * nkb
        ptp = [pta, ptb]
        for kk in range(nk):
            pg = kk % 2
            b.wait("tensor", sS, st["ptevac"][pg])
            if kk == 0:
                for t4 in range(4):
                    b.wait("tensor", sV, st["pscaled"][t4])
            for t4 in range(4):
                b.op("tensor", lambda e, kk=kk, t4=t4, pg=pg: e.transpose(
                    ptp[pg][:, t4 * 128:(t4 + 1) * 128], prow[t4][:, kk * 128:(kk + 1) * 128], c["idb"][:]),
                    sT if t4 == 3 else None)
            trv = sT.n
            if kk == nk - 1:
                for t4 in range(4):
                    st["prow_free"][t4] = trv
            b.wait("scalar", sT, trv)
            b.wait("scalar", sT, st["PTmm"][pg])
            st["ptevac"][pg] = b.op("scalar", lambda e, pg=pg, kk=kk: e.activation(
                PT[pg][:], ptp[pg][:], AF.Copy), sS)
            b.wait("tensor", sS, st["ptevac"][pg])
            if kk == 0:
                b.wait("tensor", sV, st["pav_ev"])
            st["PTmm"][pg] = b.op("tensor", lambda e, kk=kk, pg=pg, nk=nk: e.matmul(
                pav[:], vtile(kk), PT[pg][:], start=(kk == 0), stop=(kk == nk - 1)), sT)
        b.wait("vector", sT, st["PTmm"][(nk - 1) % 2])
        st["pav_ev"] = b.op("vector", lambda e, qb=qb: e.tensor_copy(avt_slice(qb), pav[:]), sV)


def emit_p1_l0attn(nc, es, A, xin, xout, tag="p1", ntiles=NT, sems=None, sems_es=None):
    b = Bld(nc, es, tag, sems=sems, sems_es=sems_es)
    sV, sS, sT, sG = b.sem("v"), b.sem("s"), b.sem("t"), b.sem("g")
    sDC, sDX, sDO = b.sem("dc"), b.sem("dx"), b.sem("do")
    sWA = b.sem("wa")

    c = emit_consts(b, A)
    lng = b.sb("lng", [128, 16], F32)
    lnb = b.sb("lnb", [128, 16], F32)
    bqc = b.sb("bqc", [128, 16], F32)
    bkc = b.sb("bkc", [128, 16], F32)
    bvb = b.sb("bvb", [128, 16], BF16)
    msk = b.sb("msk", [128, 896], F32)
    for dst, src_ in [(lng, A["l0_lng"]), (lnb, A["l0_lnb"]), (bqc, A["l0_bq"]),
                      (bkc, A["l0_bk"]), (msk, A["masks"])]:
        b.dma("sync", dst[:], src_, sDC)
    b.dma("gpsimd", bvb[:], A["l0_bv"], sDC)
    b.wait("sync", sDC, sDC.n)
    b.flush()

    ps = [b.ps(f"ps{i}", [128, 512], F32) for i in range(4)]
    pta = b.ps("pta", [128, 512], BF16)
    ptb = b.ps("ptb", [128, 512], BF16)
    pbc = b.ps("pbc", [128, 512], F32)
    pav = b.ps("pav", [128, 512], F32)

    AVT = b.sb("avt", [128, NC * T], BF16)

    st = dict(psfree=[0, 0, 0, 0], mm=[0, 0, 0, 0], prow_free=[0, 0, 0, 0],
              ptevac=[0, 0], PTmm=[0, 0], pav_ev=0, pscaled=[0, 0, 0, 0],
              nmm=0, nred=0)

    with ExitStack() as xes:
        xnt = b.sb("xnt", [128, NC * T], BF16, xes)
        with ExitStack() as les:
            xnt_done = emit_ln_phase(
                b, c, xin, lambda cc, tt: cc * T + tt * 128, xnt,
                lambda cc: lng[:, cc:cc + 1], lambda cc: lnb[:, cc:cc + 1],
                True, range(ntiles), les, (pta, ptb))
            b.flush()

        with ExitStack() as aes:
            prow = [b.sb(f"prow{i}", [128, T], BF16, aes) for i in range(4)]
            ls = b.sb("ls", [128, 16], F32, aes)
            lc = b.sb("lc", [128, 8], F32, aes)
            nmx = b.sb("nmx", [1, 16], F32, aes)
            mcol = b.sb("mcol", [128, 1], F32, aes)
            PT = [b.sb(f"PT{i}", [128, 512], BF16, aes) for i in range(2)]
            V_sb = b.sb("vsb", [128, 16 * 512], BF16, aes)
            qh = b.sb("qh", [128, T], BF16, aes)
            kh = b.sb("kh", [128, T], BF16, aes)
            wq_b = b.sb("wqb", [128, D], BF16, aes)
            wk_b = b.sb("wkb", [128, D], BF16, aes)
            bufs = dict(ps=ps, pta=pta, ptb=ptb, pbc=pbc, pav=pav, prow=prow, ls=ls,
                        lc=lc, msk=msk, PT=PT)

            qk_done = 0
            for g in range(4):
                with ExitStack() as ges:
                    wvg = b.sb(f"wvg{g}", [128, 8192], BF16, ges)
                    dvv = b.dma("gpsimd", wvg[:], A["l0_wv"][g * 128:(g + 1) * 128, :], sWA)
                    b.wait("tensor", sWA, dvv)
                    b.wait("tensor", sS, xnt_done)
                    vev = [st["psfree"][0], st["psfree"][1]]
                    for tt in range(16):
                        pg = tt % 2
                        b.wait("tensor", sS, vev[pg])
                        mmv = None
                        for cc in range(NC):
                            mmv = b.op("tensor", lambda e, cc=cc, tt=tt, pg=pg, wvg=wvg: e.matmul(
                                ps[pg][:], xnt[:, cc * T + tt * 128: cc * T + (tt + 1) * 128],
                                wvg[:, cc * 512:(cc + 1) * 512],
                                start=(cc == 0), stop=(cc == NC - 1)), sT if cc == NC - 1 else None)
                        b.wait("scalar", sT, mmv)
                        vev[pg] = b.op("scalar", lambda e, tt=tt, pg=pg: e.activation(
                            V_sb[:, tt * 512:(tt + 1) * 512], ps[pg][:], AF.Copy), sS)
                    st["psfree"][0], st["psfree"][1] = vev
                    vprod_ev = sS.n
                    for hh in range(4):
                        h = g * 4 + hh
                        b.wait("gpsimd", sT, qk_done)
                        dq = b.dma("gpsimd", wq_b[:], A["l0_wq"][h * 128:(h + 1) * 128, :], sWA)
                        dk = b.dma("gpsimd", wk_b[:], A["l0_wk"][h * 128:(h + 1) * 128, :], sWA)
                        b.wait("tensor", sWA, dq)
                        kv = None
                        for (which, wbuf, dst, bias) in (("q", wq_b, qh, bqc), ("k", wk_b, kh, bkc)):
                            if which == "k":
                                b.wait("tensor", sWA, dk)
                            for tb in range(4):
                                pg = tb % 2
                                b.wait("tensor", sS, st["psfree"][pg])
                                mmv = None
                                for cc in range(NC):
                                    mmv = b.op("tensor", lambda e, cc=cc, tb=tb, pg=pg, wbuf=wbuf: e.matmul(
                                        ps[pg][:], wbuf[:, cc * 128:(cc + 1) * 128],
                                        xnt[:, cc * T + tb * 512: cc * T + tb * 512 + 512],
                                        start=(cc == 0), stop=(cc == NC - 1)), sT if cc == NC - 1 else None)
                                b.wait("scalar", sT, mmv)
                                st["psfree"][pg] = b.op("scalar", lambda e, tb=tb, pg=pg, dst=dst, bias=bias, h=h:
                                    e.activation(dst[:, tb * 512:(tb + 1) * 512], ps[pg][:], AF.Identity,
                                                 bias=bias[:, h:h + 1], scale=1.0), sS)
                            if which == "k":
                                kv = sS.n
                        qk_done = sT.n
                        mrdy = emit_head_bound(b, c, st, qh, kh, nmx, mcol, bufs, max(kv, vprod_ev))
                        emit_attn_head(
                            b, c, st, qh, kh,
                            lambda kk, hh=hh, V_sb=V_sb: V_sb[:, kk * 512 + hh * 128: kk * 512 + hh * 128 + 128],
                            lambda qb, h=h: AVT[:, h * T + qb * 512: h * T + (qb + 1) * 512],
                            bufs, max(kv, vprod_ev), mcol, mrdy)
                    b.flush()

    with ExitStack() as oes:
        wo = b.sb("wo", [128, NC * D], BF16, oes)
        borow = b.sb("borow", [1, D], F32, oes)
        boeff = b.sb("boeff", [1, D], F32, oes)
        bob = b.sb("bob", [128, D], F32, oes)
        xt = [b.sb(f"oxt{i}", [128, D], F32, oes) for i in range(2)]
        xbb = [b.sb(f"oxb{i}", [128, D], F32, oes) for i in range(2)]
        x1t = [b.sb(f"ox1{i}", [128, D], F32, oes) for i in range(2)]
        b.dma("sync", borow[:], A["l0_bo"], sDC)
        b.wait("sync", sDC, sDC.n)
        for cc in range(NC):
            b.dma("gpsimd", wo[:, cc * D:(cc + 1) * D], A["l0_wo"][cc * 128:(cc + 1) * 128, :], sWA)
        b.wait("tensor", sWA, sWA.n)
        b.flush()
        for ob in range(4):
            mmv = None
            for cc in range(NC):
                mmv = b.op("tensor", lambda e, cc=cc, ob=ob: e.matmul(
                    pbc[0:1, 0:512], bvb[:, cc:cc + 1], wo[:, cc * D + ob * 512: cc * D + (ob + 1) * 512],
                    start=(cc == 0), stop=(cc == NC - 1)), sT if cc == NC - 1 else None)
            b.wait("vector", sT, mmv)
            ev = b.op("vector", lambda e, ob=ob: e.tensor_tensor(
                boeff[:, ob * 512:(ob + 1) * 512], pbc[0:1, 0:512], borow[:, ob * 512:(ob + 1) * 512],
                OP.add), sV)
            b.wait("tensor", sV, ev)
        b.flush()
        emit_bcast_row(b, c, boeff, bob, pbc)
        b.flush()
        xt_free = [0, 0]
        x1_free = [0, 0]
        xb_free = [0, 0]
        pmm_free = [0, 0]
        for tt in range(ntiles):
            ib = tt % 2
            b.wait("sync", sG, xt_free[ib])
            dv = b.dma("sync", xt[ib][:], xin[tt * 128:(tt + 1) * 128, :], sDX)
            b.wait("gpsimd", sDX, dv)
            b.wait("gpsimd", sV, xb_free[ib])
            gv = b.op("gpsimd", lambda e, ib=ib: e.tensor_tensor(
                xbb[ib][:], xt[ib][:], bob[:], OP.add), sG)
            xt_free[ib] = gv
            for ob in range(4):
                pg = ob % 2
                b.wait("tensor", sV, pmm_free[pg])
                mmv = None
                for cc in range(NC):
                    mmv = b.op("tensor", lambda e, cc=cc, tt=tt, ob=ob, pg=pg: e.matmul(
                        ps[pg][:], AVT[:, cc * T + tt * 128: cc * T + (tt + 1) * 128],
                        wo[:, cc * D + ob * 512: cc * D + (ob + 1) * 512],
                        start=(cc == 0), stop=(cc == NC - 1)), sT if cc == NC - 1 else None)
                b.wait("vector", sT, mmv)
                b.wait("vector", sG, gv)
                b.wait("vector", sDO, x1_free[ib])
                pmm_free[pg] = b.op("vector", lambda e, ob=ob, pg=pg, ib=ib: e.tensor_tensor(
                    x1t[ib][:, ob * 512:(ob + 1) * 512], ps[pg][:], xbb[ib][:, ob * 512:(ob + 1) * 512],
                    OP.add), sV)
            xb_free[ib] = sV.n
            b.wait("scalar", sV, sV.n)
            x1_free[ib] = b.dma("scalar", xout[tt * 128:(tt + 1) * 128, :], x1t[ib][:], sDO)
        b.wait("scalar", sDO, sDO.n)
        b.flush()


def emit_p3_l1attn(nc, es, A, xin, xout, tag="p3", ntiles=NT, sems=None, sems_es=None):
    b = Bld(nc, es, tag, sems=sems, sems_es=sems_es)
    sV, sS, sT, sG = b.sem("v"), b.sem("s"), b.sem("t"), b.sem("g")
    sDC, sDX, sDO = b.sem("dc"), b.sem("dx"), b.sem("do")
    sWA = b.sem("wa")

    c = emit_consts(b, A)
    msk = b.sb("msk", [128, 896], F32)
    b.dma("sync", msk[:], A["masks"], sDC)
    b.wait("sync", sDC, sDC.n)
    b.flush()

    ps = [b.ps(f"ps{i}", [128, 512], F32) for i in range(4)]
    pta = b.ps("pta", [128, 512], BF16)
    ptb = b.ps("ptb", [128, 512], BF16)
    pbc = b.ps("pbc", [128, 512], F32)
    pav = b.ps("pav", [128, 512], F32)

    AVT = b.sb("avt", [128, NC * T], BF16)

    st = dict(psfree=[0, 0, 0, 0], mm=[0, 0, 0, 0], prow_free=[0, 0, 0, 0],
              ptevac=[0, 0], PTmm=[0, 0], pav_ev=0, pscaled=[0, 0, 0, 0],
              nmm=0, nred=0)

    with ExitStack() as aes:
        prow = [b.sb(f"prow{i}", [128, T], BF16, aes) for i in range(4)]
        ls = b.sb("ls", [128, 16], F32, aes)
        lc = b.sb("lc", [128, 8], F32, aes)
        nmx = b.sb("nmx", [1, 16], F32, aes)
        mcol = b.sb("mcol", [128, 1], F32, aes)
        PT = [b.sb(f"PT{i}", [128, 512], BF16, aes) for i in range(2)]
        bufs = dict(ps=ps, pta=pta, ptb=ptb, pbc=pbc, pav=pav, prow=prow, ls=ls,
                    lc=lc, msk=msk, PT=PT)
        with ExitStack() as xes:
            xnt = b.sb("xnt", [128, NC * T], BF16, xes)
            with ExitStack() as les:
                xnt_done = emit_ln_phase(
                    b, c, xin, lambda cc, tt: cc * T + tt * 128, xnt,
                    None, None, False, range(ntiles), les, (pta, ptb))
                b.flush()

            wdone = 0
            for g in range(4):
                with ExitStack() as ges:
                    Qg = b.sb(f"qg{g}", [128, 4 * T], BF16, ges)
                    Kg = b.sb(f"kg{g}", [128, T], BF16, ges)
                    Vt = b.sb(f"vt{g}", [128, T], BF16, ges)
                    Vg = b.sb(f"vg{g}", [128, T], BF16, ges)
                    wb = [b.sb(f"wb{g}{i}", [128, D], BF16, ges) for i in range(2)]
                    vtv = 0
                    for r in range(6):
                        pc = 6 * g + r
                        ib = r % 2
                        b.wait("gpsimd", sT, wdone if r < 2 else wdone)
                        dq = b.dma("gpsimd", wb[ib][:], A["l1_wqkv"][pc * 128:(pc + 1) * 128, :], sWA)
                        b.wait("tensor", sWA, dq)
                        dst = (Qg[:, r * T:(r + 1) * T] if r < 4 else (Kg[:] if r == 4 else Vt[:]))
                        for tb in range(4):
                            pg = tb % 2
                            b.wait("tensor", sS, st["psfree"][pg])
                            mmv = None
                            for cc in range(NC):
                                mmv = b.op("tensor", lambda e, cc=cc, tb=tb, pg=pg, ib=ib: e.matmul(
                                    ps[pg][:], wb[ib][:, cc * 128:(cc + 1) * 128],
                                    xnt[:, cc * T + tb * 512: cc * T + tb * 512 + 512],
                                    start=(cc == 0), stop=(cc == NC - 1)), sT if cc == NC - 1 else None)
                            wdone = mmv
                            b.wait("scalar", sT, mmv)
                            st["psfree"][pg] = b.op("scalar", lambda e, tb=tb, pg=pg, dst=dst: e.activation(
                                dst[:, tb * 512:(tb + 1) * 512], ps[pg][:], AF.Copy), sS)
                        if r == 5:
                            vtv = sS.n
                    b.wait("tensor", sS, vtv)
                    ptp = [pta, ptb]
                    for g4 in range(4):
                        pg = g4 % 2
                        b.wait("tensor", sS, st["ptevac"][pg])
                        for j in range(4):
                            kk = g4 * 4 + j
                            b.op("tensor", lambda e, kk=kk, j=j, pg=pg: e.transpose(
                                ptp[pg][:, j * 128:(j + 1) * 128], Vt[:, kk * 128:(kk + 1) * 128],
                                c["idb"][:]), sT if j == 3 else None)
                        b.wait("scalar", sT, sT.n)
                        st["ptevac"][pg] = b.op("scalar", lambda e, g4=g4, pg=pg: e.activation(
                            Vg[:, g4 * 512:(g4 + 1) * 512], ptp[pg][:], AF.Copy), sS)
                    vgv = sS.n
                    for j in range(4):
                        h = g * 4 + j
                        mrdy = emit_head_bound(b, c, st, Qg[:, j * T:(j + 1) * T], Kg,
                                               nmx, mcol, bufs, vgv)
                        emit_attn_head(
                            b, c, st, Qg[:, j * T:(j + 1) * T], Kg,
                            lambda kk, Vg=Vg: Vg[:, kk * 128:(kk + 1) * 128],
                            lambda qb, h=h: AVT[:, h * T + qb * 512: h * T + (qb + 1) * 512],
                            bufs, vgv, mcol, mrdy)
                    b.flush()

    with ExitStack() as oes:
        wo = b.sb("wo", [128, NC * D], BF16, oes)
        xt = [b.sb(f"oxt{i}", [128, D], F32, oes) for i in range(2)]
        x1t = [b.sb(f"ox1{i}", [128, D], F32, oes) for i in range(2)]
        for cc in range(NC):
            b.dma("gpsimd", wo[:, cc * D:(cc + 1) * D], A["l1_wproj"][cc * 128:(cc + 1) * 128, :], sWA)
        b.wait("tensor", sWA, sWA.n)
        b.flush()
        xt_free = [0, 0]
        x1_free = [0, 0]
        pmm_free = [0, 0]
        for tt in range(ntiles):
            ib = tt % 2
            b.wait("sync", sV, xt_free[ib])
            dv = b.dma("sync", xt[ib][:], xin[tt * 128:(tt + 1) * 128, :], sDX)
            for ob in range(4):
                pg = ob % 2
                b.wait("tensor", sV, pmm_free[pg])
                mmv = None
                for cc in range(NC):
                    mmv = b.op("tensor", lambda e, cc=cc, tt=tt, ob=ob, pg=pg: e.matmul(
                        ps[pg][:], AVT[:, cc * T + tt * 128: cc * T + (tt + 1) * 128],
                        wo[:, cc * D + ob * 512: cc * D + (ob + 1) * 512],
                        start=(cc == 0), stop=(cc == NC - 1)), sT if cc == NC - 1 else None)
                b.wait("vector", sT, mmv)
                b.wait("vector", sDX, dv)
                b.wait("vector", sDO, x1_free[ib])
                pmm_free[pg] = b.op("vector", lambda e, ob=ob, pg=pg, ib=ib: e.tensor_tensor(
                    x1t[ib][:, ob * 512:(ob + 1) * 512], ps[pg][:], xt[ib][:, ob * 512:(ob + 1) * 512],
                    OP.add), sV)
            xt_free[ib] = sV.n
            b.wait("scalar", sV, sV.n)
            x1_free[ib] = b.dma("scalar", xout[tt * 128:(tt + 1) * 128, :], x1t[ib][:], sDO)
        b.wait("scalar", sDO, sDO.n)
        b.flush()


def emit_p4_l1ffn(nc, es, A, xin, xout, ddram, tag="p4", sems=None, sems_es=None):
    b = Bld(nc, es, tag, sems=sems, sems_es=sems_es)
    sV, sS, sT, sG = b.sem("v"), b.sem("s"), b.sem("t"), b.sem("g")
    sDC, sDX, sDO, sWA, sWB = b.sem("dc"), b.sem("dx"), b.sem("do"), b.sem("wa"), b.sem("wb")
    TH = 1024

    c = emit_consts(b, A)
    sgc = b.sb("sgc", [128, 44], F32)
    suc = b.sb("suc", [128, 44], F32)
    sdrow = b.sb("sdrow", [1, D], F32)
    sdb = b.sb("sdb", [128, D], F32)
    onesc = b.sb("onesc", [128, 1], BF16)
    b.dma("sync", sgc[:], A["l1_sg"], sDC)
    b.dma("sync", suc[:], A["l1_su"], sDC)
    b.dma("sync", sdrow[:], A["l1_sd"], sDC)
    b.op("vector", lambda e: e.memset(onesc[:], 1.0))
    b.wait("sync", sDC, sDC.n)
    b.flush()

    pta = b.ps("pta", [128, 512], BF16)
    ptb = b.ps("ptb", [128, 512], BF16)
    ps = [b.ps(f"ps{i}", [128, 512], F32) for i in range(4)]
    pgu = [b.ps(f"pgu{i}", [1, 512], F32) for i in range(2)]

    emit_bcast_row(b, c, sdrow, sdb, ps[0])
    b.flush()

    for half in range(2):
        off = half * 8
        with ExitStack() as hes:
            gu = b.sb(f"gu{half}", [128, 44 * TH], BF16, hes)
            gurow = b.sb(f"gurow{half}", [1, TH], F32, hes)
            m128g = b.sb(f"m128g{half}", [128, 8], F32, hes)
            with ExitStack() as xes:
                xnt = b.sb(f"xnt{half}", [128, NC * TH], BF16, xes)
                with ExitStack() as les:
                    emit_ln_phase(
                        b, c, xin, lambda cc, tt: cc * TH + (tt - off) * 128, xnt,
                        None, None, False, range(off, off + 8), les, (pta, ptb))
                    b.flush()
                with ExitStack() as ges:
                    wstg = [b.sb(f"wsg{half}{i}", [128, D], BF16, ges) for i in range(2)]
                    wstu = [b.sb(f"wsu{half}{i}", [128, D], BF16, ges) for i in range(2)]
                    wgb = [b.sb(f"wgb{half}{i}", [128, D], BF16, ges) for i in range(2)]
                    wub = [b.sb(f"wub{half}{i}", [128, D], BF16, ges) for i in range(2)]
                    grow = b.sb(f"grow{half}", [128, TH], BF16, ges)
                    urow = b.sb(f"urow{half}", [128, TH], BF16, ges)
                    stg_free = [(0, 0), (0, 0)]
                    wgb_free = [0, 0]
                    ev_free = [0, 0, 0, 0]
                    psfree = [0, 0, 0, 0]
                    growv = [0, 0]
                    for hc in range(NHC):
                        ib = hc % 2
                        b.wait("gpsimd", sV, stg_free[ib][0])
                        dg = b.dma("gpsimd", wstg[ib][:], A["l1_wg"][hc * 128:(hc + 1) * 128, :], sWA)
                        du = b.dma("gpsimd", wstu[ib][:], A["l1_wu"][hc * 128:(hc + 1) * 128, :], sWA)
                        b.wait("vector", sWA, dg)
                        b.wait("vector", sT, wgb_free[ib])
                        b.op("vector", lambda e, ib=ib: e.tensor_scalar(
                            wgb[ib][:], wstg[ib][:], 128.0, None, OP.subtract))
                        b.drain("vector")
                        b.wait("vector", sWA, du)
                        uq = b.op("vector", lambda e, ib=ib: e.tensor_scalar(
                            wub[ib][:], wstu[ib][:], 128.0, None, OP.subtract), sV)
                        stg_free[ib] = (uq, uq)
                        b.wait("tensor", sV, uq)
                        for which, base, wsrc, scol, func in (
                                ("g", 0, wgb[ib], sgc, AF.Silu), ("u", 2, wub[ib], suc, AF.Identity)):
                            for tb in range(2):
                                pg = base + tb
                                b.wait("tensor", sS, psfree[pg])
                                mmv = None
                                for cc in range(NC):
                                    mmv = b.op("tensor", lambda e, cc=cc, tb=tb, pg=pg, wsrc=wsrc: e.matmul(
                                        ps[pg][:], wsrc[:, cc * 128:(cc + 1) * 128],
                                        xnt[:, cc * TH + tb * 512: cc * TH + tb * 512 + 512],
                                        start=(cc == 0), stop=(cc == NC - 1)), sT if cc == NC - 1 else None)
                                b.wait("scalar", sT, mmv)
                                dst = grow if which == "g" else urow
                                b.wait("scalar", sV, growv[tb])
                                psfree[pg] = b.op("scalar", lambda e, tb=tb, pg=pg, dst=dst, scol=scol, hc=hc, func=func:
                                    e.activation(dst[:, tb * 512:(tb + 1) * 512], ps[pg][:], func,
                                                 bias=0.0, scale=scol[:, hc:hc + 1]), sS)
                        wgb_free[ib] = sT.n
                        for tb in range(2):
                            b.wait("vector", sS, psfree[2 + tb])
                            growv[tb] = b.op("vector", lambda e, hc=hc, tb=tb: e.tensor_tensor(
                                gu[:, hc * TH + tb * 512: hc * TH + tb * 512 + 512],
                                grow[:, tb * 512:(tb + 1) * 512], urow[:, tb * 512:(tb + 1) * 512],
                                OP.mult), sV)
                            b.wait("tensor", sV, growv[tb])
                            b.op("tensor", lambda e, hc=hc, tb=tb: e.matmul(
                                pgu[tb][:], onesc[:], gu[:, hc * TH + tb * 512: hc * TH + tb * 512 + 512],
                                start=(hc == 0), stop=(hc == NHC - 1)), sT if hc == NHC - 1 else None)
                    gusum_mm = sT.n
                    b.flush()
            b.wait("vector", sT, gusum_mm)
            for tb in range(2):
                b.op("vector", lambda e, tb=tb: e.tensor_copy(gurow[:, tb * 512:(tb + 1) * 512], pgu[tb][:]))
            b.drain("vector")
            grv = b.op("vector", lambda e: e.engine_nop(), sV)
            b.wait("tensor", sV, grv)
            for j in range(8):
                b.op("tensor", lambda e, j=j: e.matmul(
                    ps[0][:, j:j + 1], gurow[:, j * 128:(j + 1) * 128], c["ones1"][0:1, 0:1],
                    start=True, stop=True), sT if j == 7 else None)
            gtv = sT.n
            b.wait("vector", sT, gtv)
            b.op("vector", lambda e: e.tensor_scalar(m128g[:], ps[0][:, 0:8], -128.0, None, OP.mult))
            b.drain("vector")
            m128v = b.op("vector", lambda e: e.engine_nop(), sV)
            b.flush()

            with ExitStack() as des:
                wdb = [b.sb(f"wdb{half}{i}", [128, 44 * 512], BF16, des) for i in range(2)]
                tmp = [b.sb(f"dtmp{half}{i}", [128, 512], F32, des) for i in range(2)]
                wd_used = [0, 0]
                tmp_free = [0, 0]
                dmad = []
                for db in range(4):
                    ibw = db % 2
                    b.wait("gpsimd", sT, wd_used[ibw])
                    dws = []
                    for q in range(4):
                        lo = q * 11
                        dws.append(b.dma(
                            "gpsimd",
                            wdb[ibw][:].rearrange("p (hc j) -> p hc j", j=512)[:, lo:lo + 11, :],
                            A["l1_wd"][db * 5632 + lo * 128: db * 5632 + (lo + 11) * 128, :]
                            .rearrange("(hc p) j -> p hc j", p=128),
                            sWB))
                    for tt in range(off, off + 8):
                        ti = tt - off
                        pg = 2 + ti % 2
                        b.wait("tensor", sWB, dws[-1])
                        if len(dmad) >= 2:
                            b.wait("tensor", sV, dmad[-2][1])
                        b.wait("tensor", sV, grv)
                        mmv = None
                        for hc in range(NHC):
                            mmv = b.op("tensor", lambda e, hc=hc, ti=ti, pg=pg, ibw=ibw: e.matmul(
                                ps[pg][:], gu[:, hc * TH + ti * 128: hc * TH + (ti + 1) * 128],
                                wdb[ibw][:, hc * 512:(hc + 1) * 512],
                                start=(hc == 0), stop=(hc == NHC - 1)), sT if hc == NHC - 1 else None)
                        wd_used[ibw] = mmv
                        ibt = ti % 2
                        b.wait("vector", sT, mmv)
                        b.wait("vector", sV, m128v)
                        b.wait("vector", sDO, tmp_free[ibt])
                        vv = b.op("vector", lambda e, ti=ti, pg=pg, ibt=ibt, db=db: e.scalar_tensor_tensor(
                            tmp[ibt][:], ps[pg][:], m128g[:, ti:ti + 1], sdb[:, db * 512:(db + 1) * 512],
                            OP.add, OP.mult), sV)
                        dmad.append((None, vv))
                        b.wait("scalar", sV, vv)
                        tmp_free[ibt] = b.dma(
                            "scalar", ddram[tt * 128:(tt + 1) * 128, db * 512:(db + 1) * 512],
                            tmp[ibt][:], sDO)
                b.flush()

        with ExitStack() as fes:
            dxt = [b.sb(f"fx{half}{i}", [128, D], F32, fes) for i in range(2)]
            ddt = [b.sb(f"fd{half}{i}", [128, D], F32, fes) for i in range(2)]
            x4t = [b.sb(f"fo{half}{i}", [128, D], F32, fes) for i in range(2)]
            bfree = [0, 0]
            ofree = [0, 0]
            b.wait("sync", sDO, sDO.n)
            for tt in range(off, off + 8):
                ib = tt % 2
                b.wait("sync", sV, bfree[ib])
                d1 = b.dma("sync", dxt[ib][:], xin[tt * 128:(tt + 1) * 128, :], sDX)
                d2 = b.dma("sync", ddt[ib][:], ddram[tt * 128:(tt + 1) * 128, :], sDX)
                b.wait("vector", sDX, d2)
                b.wait("vector", sDO, ofree[ib])
                bfree[ib] = b.op("vector", lambda e, ib=ib: e.tensor_tensor(
                    x4t[ib][:], dxt[ib][:], ddt[ib][:], OP.add), sV)
                b.wait("scalar", sV, bfree[ib])
                ofree[ib] = b.dma("scalar", xout[tt * 128:(tt + 1) * 128, :], x4t[ib][:], sDO)
            b.wait("scalar", sDO, sDO.n)
            b.flush()


P4_INPUTS = ["ident_bf", "ident_f32", "l1_wg", "l1_wu", "l1_wd", "l1_sg", "l1_su", "l1_sd"]


def build_p4_standalone():
    nc = bass.Bass("TRN2", target_bir_lowering=False)
    A = declare_shared(nc, P4_INPUTS)
    xin = nc.dram_tensor("xin", [T, D], F32, kind="ExternalInput").ap()
    xout = nc.dram_tensor("xout", [T, D], F32, kind="ExternalOutput").ap()
    ddram = nc.dram_tensor("ddram", [T, D], F32, kind="Internal").ap()
    with ExitStack() as es:
        emit_p4_l1ffn(nc, es, A, xin, xout, ddram)
    return nc


P3_INPUTS = ["ident_bf", "ident_f32", "masks", "l1_wqkv", "l1_wproj"]


def build_p3_standalone(ntiles=NT):
    nc = bass.Bass("TRN2", target_bir_lowering=False)
    A = declare_shared(nc, P3_INPUTS)
    xin = nc.dram_tensor("xin", [T, D], F32, kind="ExternalInput").ap()
    xout = nc.dram_tensor("xout", [T, D], F32, kind="ExternalOutput").ap()
    with ExitStack() as es:
        emit_p3_l1attn(nc, es, A, xin, xout, ntiles=ntiles)
    return nc


P1_INPUTS = ["ident_bf", "ident_f32", "masks", "l0_wq", "l0_wk", "l0_wv", "l0_wo",
             "l0_lng", "l0_lnb", "l0_bq", "l0_bk", "l0_bv", "l0_bo"]


def build_p1_standalone(ntiles=NT):
    nc = bass.Bass("TRN2", target_bir_lowering=False)
    A = declare_shared(nc, P1_INPUTS)
    xin = nc.dram_tensor("xin", [T, D], F32, kind="ExternalInput").ap()
    xout = nc.dram_tensor("xout", [T, D], F32, kind="ExternalOutput").ap()
    with ExitStack() as es:
        emit_p1_l0attn(nc, es, A, xin, xout, ntiles=ntiles)
    return nc



def _prep_shared(inp):
    A = {}
    A["ident_bf"] = np.eye(128, dtype=BFNP)
    A["ident_f32"] = np.eye(128, dtype=np.float32)
    p = np.arange(128)[:, None]
    mcol = np.arange(896)[None, :]
    A["masks"] = np.where(mcol <= p + 384, 0.0, NEG).astype(np.float32)

    def colchunks(v):
        return np.ascontiguousarray(v.reshape(-1, 128).T)

    A["l0_wq"] = np.ascontiguousarray(
        inp["L0_attn_Wq"].reshape(16, 128, 16, 128).transpose(2, 1, 0, 3).reshape(2048, 2048))
    A["l0_wk"] = np.ascontiguousarray(
        inp["L0_attn_Wk"].reshape(16, 128, 16, 128).transpose(2, 1, 0, 3).reshape(2048, 2048))
    A["l0_wv"] = np.ascontiguousarray(
        inp["L0_attn_Wv"].reshape(16, 128, 4, 512).transpose(2, 1, 0, 3).reshape(512, 8192))
    A["l0_wo"] = np.ascontiguousarray(inp["L0_attn_Wo"])
    A["l0_lng"] = colchunks(inp["L0_attn_ln_g"])
    A["l0_lnb"] = colchunks(inp["L0_attn_ln_b"])
    A["l0_bq"] = colchunks(inp["L0_attn_bq"])
    A["l0_bk"] = colchunks(inp["L0_attn_bk"])
    A["l0_bv"] = colchunks(inp["L0_attn_bv"])
    A["l0_bo"] = np.ascontiguousarray(inp["L0_attn_bo"].reshape(1, 2048))
    A["l0f_w"] = np.ascontiguousarray(inp["L0_ffn_weight"].T)
    A["l0f_gam"] = colchunks(inp["L0_ffn_ln_gamma"])
    A["l0f_bet"] = colchunks(inp["L0_ffn_ln_beta"])
    A["l0f_bias"] = np.ascontiguousarray(inp["L0_ffn_bias"].reshape(1, 2048))
    A["l1_wqkv"] = np.ascontiguousarray(
        inp["L1_attn_weight_qkv"].T.reshape(16, 128, 24, 128).transpose(2, 1, 0, 3).reshape(3072, 2048))
    A["l1_wproj"] = np.ascontiguousarray(inp["L1_attn_weight_proj"].T)
    A["l1_wg"] = np.ascontiguousarray(
        inp["L1_ffn_wq_gate"].T.reshape(16, 128, 44, 128).transpose(2, 1, 0, 3).reshape(5632, 2048))
    A["l1_wu"] = np.ascontiguousarray(
        inp["L1_ffn_wq_up"].T.reshape(16, 128, 44, 128).transpose(2, 1, 0, 3).reshape(5632, 2048))
    A["l1_wd"] = np.ascontiguousarray(
        inp["L1_ffn_wq_down"].T.reshape(44, 128, 4, 512).transpose(2, 0, 1, 3).reshape(22528, 512))
    A["l1_sg"] = colchunks(inp["L1_ffn_s_gate"])
    A["l1_su"] = colchunks(inp["L1_ffn_s_up"])
    A["l1_sd"] = np.ascontiguousarray(inp["L1_ffn_s_down"].reshape(1, 2048))
    return A


SHARED_SPECS = {
    "ident_bf": ([128, 128], BF16), "ident_f32": ([128, 128], F32),
    "masks": ([128, 896], F32),
    "l0_wq": ([2048, 2048], F32), "l0_wk": ([2048, 2048], F32),
    "l0_wv": ([512, 8192], F32), "l0_wo": ([2048, 2048], F32),
    "l0_lng": ([128, 16], F32), "l0_lnb": ([128, 16], F32),
    "l0_bq": ([128, 16], F32), "l0_bk": ([128, 16], F32), "l0_bv": ([128, 16], F32),
    "l0_bo": ([1, 2048], F32),
    "l0f_w": ([2048, 2048], F32), "l0f_gam": ([128, 16], F32),
    "l0f_bet": ([128, 16], F32), "l0f_bias": ([1, 2048], F32),
    "l1_wqkv": ([3072, 2048], F32), "l1_wproj": ([2048, 2048], F32),
    "l1_wg": ([5632, 2048], I32), "l1_wu": ([5632, 2048], I32), "l1_wd": ([22528, 512], I32),
    "l1_sg": ([128, 44], F32), "l1_su": ([128, 44], F32), "l1_sd": ([1, 2048], F32),
}


def declare_shared(nc, names):
    aps = {}
    for n in names:
        shape, dt = SHARED_SPECS[n]
        aps[n] = nc.dram_tensor(n, shape, dt, kind="ExternalInput").ap()
    return aps


P2_INPUTS = ["ident_bf", "ident_f32", "l0f_w", "l0f_gam", "l0f_bet", "l0f_bias"]


def build_p2_standalone(debug=False, ntiles=NT):
    nc = bass.Bass("TRN2", target_bir_lowering=False)
    A = declare_shared(nc, P2_INPUTS)
    xin = nc.dram_tensor("xin", [T, D], F32, kind="ExternalInput").ap()
    xout = nc.dram_tensor("xout", [T, D], F32, kind="ExternalOutput").ap()
    dbg = None
    if debug:
        dbg = {
            "rinv": nc.dram_tensor("d_rinv", [128, 1], F32, kind="ExternalOutput").ap(),
            "wq0": nc.dram_tensor("d_wq0", [128, D], BF16, kind="ExternalOutput").ap(),
            "u0": nc.dram_tensor("d_u0", [128, D], BF16, kind="ExternalOutput").ap(),
            "xnt0": nc.dram_tensor("d_xnt0", [128, D], BF16, kind="ExternalOutput").ap(),
            "biasb": nc.dram_tensor("d_biasb", [128, D], F32, kind="ExternalOutput").ap(),
            "coef": nc.dram_tensor("d_coef", [128, 8], F32, kind="ExternalOutput").ap(),
            "nb": nc.dram_tensor("d_nb", [128, 1], F32, kind="ExternalOutput").ap(),
        }
    with ExitStack() as es:
        emit_p2_l0ffn(nc, es, A, xin, xout, dbg=dbg, ntiles=ntiles)
    return nc


def build_fused():
    nc = bass.Bass("TRN2", target_bir_lowering=False)
    names = sorted(set(P1_INPUTS) | set(P2_INPUTS) | set(P3_INPUTS) | set(P4_INPUTS))
    A = declare_shared(nc, names)
    x0 = nc.dram_tensor("xin", [T, D], F32, kind="ExternalInput").ap()
    x4 = nc.dram_tensor("xout", [T, D], F32, kind="ExternalOutput").ap()
    x1 = nc.dram_tensor("x1s", [T, D], F32, kind="Internal").ap()
    x2 = nc.dram_tensor("x2s", [T, D], F32, kind="Internal").ap()
    x3 = nc.dram_tensor("x3s", [T, D], F32, kind="Internal").ap()
    dd = nc.dram_tensor("ddram", [T, D], F32, kind="Internal").ap()
    sems = {}
    with ExitStack() as ses:
        with ExitStack() as es:
            emit_p1_l0attn(nc, es, A, x0, x1, tag="p1", sems=sems, sems_es=ses)
        with ExitStack() as es:
            emit_p2_l0ffn(nc, es, A, x1, x2, tag="p2", sems=sems, sems_es=ses)
        with ExitStack() as es:
            emit_p3_l1attn(nc, es, A, x2, x3, tag="p3", sems=sems, sems_es=ses)
        with ExitStack() as es:
            emit_p4_l1ffn(nc, es, A, x3, x4, dd, tag="p4", sems=sems, sems_es=ses)
    return nc, names


_BUILD_CACHE = {}


def _get_phase_programs():
    if "phases" not in _BUILD_CACHE:
        _BUILD_CACHE["phases"] = [
            (build_p1_standalone(), P1_INPUTS),
            (build_p2_standalone(), P2_INPUTS),
            (build_p3_standalone(), P3_INPUTS),
            (build_p4_standalone(), P4_INPUTS),
        ]
    return _BUILD_CACHE["phases"]


def kernel(**inputs):
    A = _prep_shared(inputs)
    if "fused" not in _BUILD_CACHE:
        _BUILD_CACHE["fused"] = build_fused()
    nc, names = _BUILD_CACHE["fused"]
    in_maps = []
    for i in range(NCORES):
        m = {k: A[k] for k in names}
        m["xin"] = np.ascontiguousarray(inputs["L0_attn_X"][i])
        in_maps.append(m)
    res = run_bass_kernel_spmd(nc, in_maps, core_ids=list(range(NCORES)))
    return np.stack([res.results[i]["xout"] for i in range(NCORES)], axis=0)


# version 06
# speedup vs baseline: 1.4143x; 1.1935x over previous
import numpy as np
import ml_dtypes
from contextlib import ExitStack
import concourse.bass as bass
import concourse.mybir as mybir
from concourse.bass_utils import run_bass_kernel_spmd
from concourse.alu_op_type import AluOpType as OP

F32 = mybir.dt.float32
BF16 = mybir.dt.bfloat16
I32 = mybir.dt.int32
AF = mybir.ActivationFunctionType
AX = mybir.AxisListType.X
BFNP = ml_dtypes.bfloat16

ENGINES = ("sync", "tensor", "vector", "scalar", "gpsimd")
NCORES = 8
T = 2048
D = 2048
NT = T // 128
NC = D // 128
HFF = 5632
NHC = HFF // 128
INVD = 1.0 / D
EPS = 1e-5
ASCALE = 0.08838834764831845
NEG = -1.0e30


class Sem:
    __slots__ = ("h", "n")

    def __init__(self, h):
        self.h = h
        self.n = 0


class Bld:

    def __init__(self, nc, es, tag, sems=None, sems_es=None):
        self.nc, self.es, self.tag = nc, es, tag
        self.q = {e: [] for e in ENGINES}
        self._sems = sems if sems is not None else {}
        self._sems_es = sems_es
        self._uid = 0

    def fresh(self):
        self._uid += 1
        return self._uid

    def sem(self, name):
        if name not in self._sems:
            es = self._sems_es or self.es
            self._sems[name] = Sem(es.enter_context(self.nc.semaphore(f"sem_{name}_{self.tag}")))
        return self._sems[name]

    def sb(self, name, shape, dt, es=None):
        return (es or self.es).enter_context(self.nc.sbuf_tensor(f"{self.tag}_{name}", shape, dt))

    def ps(self, name, shape, dt, es=None):
        return (es or self.es).enter_context(self.nc.psum_tensor(f"{self.tag}_{name}", shape, dt))

    def op(self, eng, fn, sem=None, k=1):
        if sem is None:
            self.q[eng].append(fn)
            return None
        sem.n += k
        v = sem.n
        self.q[eng].append(lambda e, fn=fn, s=sem.h, k=k: fn(e).then_inc(s, k))
        return v

    def dma(self, eng, dst, src, sem):
        return self.op(eng, lambda e, d=dst, s=src: e.dma_start(d, s), sem, 16)

    def wait(self, eng, sem, v):
        if v:
            self.q[eng].append(lambda e, s=sem.h, v=v: e.wait_ge(s, v))

    def drain(self, eng):
        self.q[eng].append(lambda e: e.drain())

    def flush(self):
        q = self.q
        self.q = {e: [] for e in ENGINES}
        with self.nc.Block() as blk:
            for name in ENGINES:
                ops = q[name]
                if ops:
                    getattr(blk, name)(lambda eng, ops=ops: [f(eng) for f in ops])


def emit_consts(b, A):
    c = {}
    c["idb"] = b.sb("idb", [128, 128], BF16)
    c["idf"] = b.sb("idf", [128, 128], F32)
    c["ones1"] = b.sb("ones1", [1, 128], F32)
    c["epsc"] = b.sb("epsc", [128, 1], F32)
    c["onescol"] = b.sb("onescol", [128, 1], BF16)
    sDC = b.sem("dc")
    b.dma("sync", c["idb"][:], A["ident_bf"], sDC)
    b.dma("sync", c["idf"][:], A["ident_f32"], sDC)
    b.op("vector", lambda e: e.memset(c["ones1"][:], 1.0))
    b.op("vector", lambda e: e.memset(c["epsc"][:], EPS))
    b.op("vector", lambda e: e.memset(c["onescol"][:], 1.0))
    b.wait("sync", sDC, sDC.n)
    return c


def emit_bcast_row(b, c, row_sb, dst_sb, pa, width=D):
    sT, sV = b.sem("t"), b.sem("v")
    for ob in range(width // 512):
        vprev = sV.n
        b.wait("tensor", sV, vprev)
        b.op("tensor", lambda e, ob=ob: e.matmul(
            pa[:], c["ones1"][:], row_sb[:, ob * 512:(ob + 1) * 512], start=True, stop=True), sT)
        b.wait("vector", sT, sT.n)
        b.op("vector", lambda e, ob=ob: e.tensor_copy(dst_sb[:, ob * 512:(ob + 1) * 512], pa[:]), sV)


def emit_ln_stats(b, c, x_ap, bn6, mv, sc, nb, double_ln):
    sV, sS = b.sem("v"), b.sem("s")
    for i in range(4):
        b.op("vector", lambda e, i=i: e.bn_stats(bn6[:, i * 6:(i + 1) * 6], x_ap[:, i * 512:(i + 1) * 512]))
    b.drain("vector")
    vs = b.op("vector", lambda e: e.bn_aggr(mv[:], bn6[:]), sV)
    b.wait("scalar", sV, vs)
    ss = b.op("scalar", lambda e: e.activation(sc[:, 2:3], mv[:, 1:2], AF.Sqrt, bias=c["epsc"][:], scale=1.0), sS)
    b.wait("vector", sS, ss)
    b.op("vector", lambda e: e.reciprocal(sc[:, 3:4], sc[:, 2:3]))
    b.drain("vector")
    if double_ln:
        b.op("vector", lambda e: e.tensor_scalar(sc[:, 4:5], sc[:, 3:4], sc[:, 3:4], None, OP.mult))
        b.drain("vector")
        vv = b.op("vector", lambda e: e.tensor_scalar(sc[:, 5:6], mv[:, 1:2], sc[:, 4:5], None, OP.mult), sV)
        b.wait("scalar", sV, vv)
        ss = b.op("scalar", lambda e: e.activation(sc[:, 6:7], sc[:, 5:6], AF.Sqrt, bias=c["epsc"][:], scale=1.0), sS)
        b.wait("vector", sS, ss)
        b.op("vector", lambda e: e.reciprocal(sc[:, 7:8], sc[:, 6:7]))
        b.drain("vector")
        b.op("vector", lambda e: e.tensor_scalar(sc[:, 0:1], sc[:, 3:4], sc[:, 7:8], None, OP.mult))
    else:
        b.op("vector", lambda e: e.tensor_copy(sc[:, 0:1], sc[:, 3:4]))
    b.drain("vector")
    vv = b.op("vector", lambda e: e.scalar_tensor_tensor(nb[:], mv[:, 0:1], -1.0, sc[:, 0:1], OP.mult, OP.mult), sV)
    return vv


def emit_p2_l0ffn(nc, es, A, xin, xout, tag="p2", dbg=None, ntiles=NT, sems=None, sems_es=None):
    b = Bld(nc, es, tag, sems=sems, sems_es=sems_es)
    sV, sS, sT, sG = b.sem("v"), b.sem("s"), b.sem("t"), b.sem("g")
    sDC, sDW, sDX, sDO = b.sem("dc"), b.sem("dw"), b.sem("dx"), b.sem("do")

    c = emit_consts(b, A)
    gam = b.sb("gam", [128, 16], F32)
    bet = b.sb("bet", [128, 16], F32)
    brow = b.sb("brow", [1, D], F32)
    bias_b = b.sb("bias_b", [128, D], F32)
    wq = b.sb("wq", [128, NC * D], BF16)
    pa = b.ps("pa", [128, 512], F32)
    pb = b.ps("pb", [128, 512], F32)
    pta = b.ps("pta", [128, 512], BF16)
    ptb = b.ps("ptb", [128, 512], BF16)

    b.dma("sync", gam[:], A["l0f_gam"], sDC)
    b.dma("sync", bet[:], A["l0f_bet"], sDC)
    b.dma("sync", brow[:], A["l0f_bias"], sDC)
    b.wait("sync", sDC, sDC.n)
    b.flush()

    emit_bcast_row(b, c, brow, bias_b, pa)
    b.flush()

    with ExitStack() as wes:
        stg = [b.sb(f"stg{i}", [128, D], F32, wes) for i in range(2)]
        acc = b.sb("acc", [128, 16], F32, wes)
        dump = b.sb("dump", [128, D], F32, wes)
        small = b.sb("small", [1, 4], F32, wes)
        rinv = b.sb("rinv", [128, 1], F32, wes)
        tmp_col = b.sb("tmpcol", [128, 1], F32, wes)
        t1 = b.sb("t1", [128, D], BF16, wes)
        t2 = b.sb("t2", [128, D], BF16, wes)
        for cc in range(NC):
            if cc >= 2:
                b.wait("sync", sS, sS.n)
            dv = b.dma("sync", stg[cc % 2][:], A["l0f_w"][cc * 128:(cc + 1) * 128, :], sDW)
            b.wait("scalar", sDW, dv)
            b.op("scalar", lambda e, cc=cc: e.activation(
                dump[:], stg[cc % 2][:], AF.Abs, accum_out=acc[:, cc:cc + 1]), sS)
        b.flush()
        b.op("vector", lambda e: e.reduce_sum(tmp_col[:], acc[:], axis=AX))
        b.drain("vector")
        vv = b.op("vector", lambda e: e.engine_nop(), sV)
        b.wait("tensor", sV, vv)
        tv = b.op("tensor", lambda e: e.transpose(pa[0:1, 0:128], tmp_col[:], c["idf"][:]), sT)
        b.wait("vector", sT, tv)
        b.op("vector", lambda e: e.reduce_sum(small[:, 0:1], pa[0:1, 0:128], axis=AX))
        b.drain("vector")
        b.op("vector", lambda e: e.tensor_scalar(small[:, 1:2], small[:, 0:1], 1.0 / (D * D), EPS, OP.mult, OP.add))
        b.drain("vector")
        vv = b.op("vector", lambda e: e.reciprocal(small[:, 2:3], small[:, 1:2]), sV)
        b.wait("tensor", sV, vv)
        tv = b.op("tensor", lambda e: e.matmul(pb[:, 0:1], c["ones1"][:], small[:, 2:3], start=True, stop=True), sT)
        b.wait("vector", sT, tv)
        b.op("vector", lambda e: e.tensor_copy(rinv[:], pb[:, 0:1]))
        b.drain("vector")
        b.flush()
        for cc in range(NC):
            if cc >= 2:
                b.wait("sync", sV, sV.n)
            dv = b.dma("sync", stg[cc % 2][:], A["l0f_w"][cc * 128:(cc + 1) * 128, :], sDW)
            b.wait("vector", sDW, dv)
            b.op("vector", lambda e, cc=cc: e.tensor_scalar(
                t1[:], stg[cc % 2][:], rinv[:], 0.5, OP.mult, OP.is_gt))
            b.drain("vector")
            b.op("vector", lambda e, cc=cc: e.tensor_scalar(
                t2[:], stg[cc % 2][:], rinv[:], -0.5, OP.mult, OP.is_lt))
            b.drain("vector")
            b.op("vector", lambda e, cc=cc: e.tensor_tensor(
                wq[:, cc * D:(cc + 1) * D], t1[:], t2[:], OP.subtract), sV)
        b.flush()
        if dbg is not None:
            b.dma("scalar", dbg["rinv"], rinv[:], sDO)
            b.dma("scalar", dbg["wq0"], wq[:, 0:D], sDO)
            b.wait("scalar", sDO, sDO.n)
            b.flush()

    xt = [b.sb(f"xt{i}", [128, D], F32) for i in range(2)]
    ub = [b.sb(f"u{i}", [128, D], BF16) for i in range(2)]
    xnt = [b.sb(f"xnt{i}", [128, D], BF16) for i in range(2)]
    xb = [b.sb(f"xb{i}", [128, D], F32) for i in range(2)]
    x2t = [b.sb(f"x2t{i}", [128, D], F32) for i in range(2)]
    bn6 = b.sb("bn6", [128, 24], F32)
    mv = b.sb("mv", [128, 2], F32)
    scc = b.sb("scc", [128, 8], F32)
    nb = b.sb("nb", [128, 1], F32)
    xt_free = [(0, 0, 0), (0, 0, 0)]
    u_free = [0, 0]
    xnt_free = [0, 0]
    x2_free = [0, 0]
    xb_free = [0, 0]
    pmm = [pa, pb]
    pmm_free = [0, 0]
    ptr = [pta, ptb]
    ptr_free = [0, 0]
    last_uv = 0

    for tt in range(ntiles):
        ib = tt % 2
        b.wait("sync", sG, xt_free[ib][0])
        b.wait("sync", sV, xt_free[ib][1])
        b.wait("sync", sS, xt_free[ib][2])
        dv = b.dma("sync", xt[ib][:], xin[tt * 128:(tt + 1) * 128, :], sDX)
        b.wait("vector", sDX, dv)
        b.wait("vector", sS, last_uv)
        coeff_v = emit_ln_stats(b, c, xt[ib], bn6, mv, scc, nb, double_ln=True)
        b.wait("scalar", sDX, dv)
        b.wait("scalar", sV, coeff_v)
        b.wait("scalar", sT, u_free[ib])
        uv = b.op("scalar", lambda e, ib=ib: e.activation(
            ub[ib][:], xt[ib][:], AF.Identity, bias=nb[:], scale=scc[:, 0:1]), sS)
        last_uv = uv
        b.wait("gpsimd", sDX, dv)
        b.wait("gpsimd", sV, xb_free[ib])
        gv = b.op("gpsimd", lambda e, ib=ib: e.tensor_tensor(
            xb[ib][:], xt[ib][:], bias_b[:], OP.add), sG)
        xt_free[ib] = (gv, coeff_v, uv)
        b.wait("tensor", sS, uv)
        b.wait("scalar", sT, xnt_free[ib])
        for grp in range(4):
            pg = grp % 2
            b.wait("tensor", sS, ptr_free[pg])
            for j in range(4):
                cc = grp * 4 + j
                tv = b.op("tensor", lambda e, cc=cc, pg=pg, j=j, ib=ib: e.transpose(
                    ptr[pg][:, j * 128:(j + 1) * 128], ub[ib][:, cc * 128:(cc + 1) * 128], c["idb"][:]),
                    sT if j == 3 else None)
            if grp == 3:
                u_free[ib] = sT.n
            b.wait("scalar", sT, sT.n)
            for j in range(4):
                cc = grp * 4 + j
                ev = b.op("scalar", lambda e, cc=cc, pg=pg, j=j, ib=ib: e.activation(
                    xnt[ib][:, cc * 128:(cc + 1) * 128], ptr[pg][:, j * 128:(j + 1) * 128],
                    AF.Identity, bias=bet[:, cc:cc + 1], scale=gam[:, cc:cc + 1]), sS)
            ptr_free[pg] = sS.n
        xnt_ready = sS.n
        b.wait("tensor", sS, xnt_ready)
        for ob in range(4):
            pg = ob % 2
            b.wait("tensor", sV, pmm_free[pg])
            for cc in range(NC):
                mmv = b.op("tensor", lambda e, cc=cc, ob=ob, pg=pg, ib=ib: e.matmul(
                    pmm[pg][:], xnt[ib][:, cc * 128:(cc + 1) * 128],
                    wq[:, cc * D + ob * 512: cc * D + (ob + 1) * 512],
                    start=(cc == 0), stop=(cc == NC - 1)), sT if cc == NC - 1 else None)
            b.wait("vector", sT, mmv)
            b.wait("vector", sG, xt_free[ib][0])
            b.wait("vector", sDO, x2_free[ib])
            pmm_free[pg] = b.op("vector", lambda e, ob=ob, pg=pg, ib=ib: e.tensor_tensor(
                x2t[ib][:, ob * 512:(ob + 1) * 512], pmm[pg][:], xb[ib][:, ob * 512:(ob + 1) * 512], OP.add), sV)
        xb_free[ib] = sV.n
        xnt_free[ib] = sT.n
        b.wait("scalar", sV, sV.n)
        x2_free[ib] = b.dma("scalar", xout[tt * 128:(tt + 1) * 128, :], x2t[ib][:], sDO)
        if dbg is not None and tt == 0:
            b.dma("scalar", dbg["u0"], ub[0][:], sDO)
            b.dma("scalar", dbg["xnt0"], xnt[0][:], sDO)
            b.dma("scalar", dbg["biasb"], bias_b[:], sDO)
            b.dma("scalar", dbg["coef"], scc[:], sDO)
            b.dma("scalar", dbg["nb"], nb[:], sDO)
    b.wait("scalar", sDO, sDO.n)
    b.flush()


def emit_ln_phase(b, c, xin_ap, col_of, xnt_sb, gamcol, betcol, double_ln, tiles, les, ptpair):
    sV, sS, sT, sDX = b.sem("v"), b.sem("s"), b.sem("t"), b.sem("dx")
    u_ = b.fresh()
    xt = [b.sb(f"lnxt{i}_{u_}", [128, D], F32, les) for i in range(2)]
    ub = [b.sb(f"lnu{i}_{u_}", [128, D], BF16, les) for i in range(2)]
    bn6 = b.sb(f"lnbn6_{u_}", [128, 24], F32, les)
    mv = b.sb(f"lnmv_{u_}", [128, 2], F32, les)
    scc = b.sb(f"lnscc_{u_}", [128, 8], F32, les)
    nb = b.sb(f"lnnb_{u_}", [128, 1], F32, les)
    ptr = ptpair
    ptr_free = [0, 0]
    xt_free = [(0, 0), (0, 0)]
    u_free = [0, 0]
    last_uv = 0
    for ti, tt in enumerate(tiles):
        ib = ti % 2
        b.wait("sync", sV, xt_free[ib][0])
        b.wait("sync", sS, xt_free[ib][1])
        dv = b.dma("sync", xt[ib][:], xin_ap[tt * 128:(tt + 1) * 128, :], sDX)
        b.wait("vector", sDX, dv)
        b.wait("vector", sS, last_uv)
        coeff_v = emit_ln_stats(b, c, xt[ib], bn6, mv, scc, nb, double_ln=double_ln)
        b.wait("scalar", sDX, dv)
        b.wait("scalar", sV, coeff_v)
        b.wait("scalar", sT, u_free[ib])
        uv = b.op("scalar", lambda e, ib=ib: e.activation(
            ub[ib][:], xt[ib][:], AF.Identity, bias=nb[:], scale=scc[:, 0:1]), sS)
        last_uv = uv
        xt_free[ib] = (coeff_v, uv)
        b.wait("tensor", sS, uv)
        for grp in range(4):
            pg = grp % 2
            b.wait("tensor", sS, ptr_free[pg])
            for j in range(4):
                cc = grp * 4 + j
                b.op("tensor", lambda e, cc=cc, pg=pg, j=j, ib=ib: e.transpose(
                    ptr[pg][:, j * 128:(j + 1) * 128], ub[ib][:, cc * 128:(cc + 1) * 128], c["idb"][:]),
                    sT if j == 3 else None)
            if grp == 3:
                u_free[ib] = sT.n
            b.wait("scalar", sT, sT.n)
            for j in range(4):
                cc = grp * 4 + j
                if gamcol is not None:
                    b.op("scalar", lambda e, cc=cc, pg=pg, j=j, tt=tt: e.activation(
                        xnt_sb[:, col_of(cc, tt):col_of(cc, tt) + 128], ptr[pg][:, j * 128:(j + 1) * 128],
                        AF.Identity, bias=betcol(cc), scale=gamcol(cc)), sS)
                else:
                    b.op("scalar", lambda e, cc=cc, pg=pg, j=j, tt=tt: e.activation(
                        xnt_sb[:, col_of(cc, tt):col_of(cc, tt) + 128], ptr[pg][:, j * 128:(j + 1) * 128],
                        AF.Copy), sS)
            ptr_free[pg] = sS.n
    return sS.n


def emit_head_bound(b, c, st, qh, kh, sqpair, nmx, mcol, bufs, ready):
    sV, sS, sT = b.sem("v"), b.sem("s"), b.sem("t")
    pbc, pav = bufs["pbc"], bufs["pav"]
    sqa, sqb_ = sqpair
    b.wait("scalar", sS, ready)
    b.wait("scalar", sT, st["nmm"])
    sqv = b.op("scalar", lambda e: e.activation(sqa[:], qh[:], AF.Square), sS)
    sqv = b.op("scalar", lambda e: e.activation(sqb_[:], kh[:], AF.Square), sS)
    pp = [pbc, pav]
    srcs = [(sqa, 0), (sqa, 1), (sqa, 2), (sqa, 3), (sqb_, 0), (sqb_, 1), (sqb_, 2), (sqb_, 3)]
    b.wait("tensor", sS, sqv)
    b.wait("tensor", sV, st["pav_ev"])
    b.wait("tensor", sV, st["linv_ev"])
    b.wait("tensor", sV, st["nred"])
    for i in range(0, 8, 2):
        for j in range(2):
            sq_, tb = srcs[i + j]
            st["nmm"] = b.op("tensor", lambda e, sq_=sq_, tb=tb, j=j: e.matmul(
                pp[j][0:1, 0:512], c["onescol"][:], sq_[:, tb * 512:(tb + 1) * 512],
                start=True, stop=True), sT)
        b.wait("vector", sT, st["nmm"])
        for j in range(2):
            st["nred"] = b.op("vector", lambda e, i=i, j=j: e.reduce_max(
                nmx[:, i + j:i + j + 1], pp[j][0:1, 0:512], axis=AX), sV)
        if i < 6:
            b.wait("tensor", sV, st["nred"])
    b.wait("vector", sV, st["nred"])
    b.op("vector", lambda e: e.reduce_max(nmx[:, 8:9], nmx[:, 0:4], axis=AX))
    b.drain("vector")
    b.op("vector", lambda e: e.reduce_max(nmx[:, 9:10], nmx[:, 4:8], axis=AX))
    b.drain("vector")
    vv = b.op("vector", lambda e: e.tensor_tensor(nmx[:, 10:11], nmx[:, 8:9], nmx[:, 9:10], OP.mult), sV)
    b.wait("scalar", sV, vv)
    sv = b.op("scalar", lambda e: e.activation(nmx[:, 11:12], nmx[:, 10:11], AF.Sqrt), sS)
    b.wait("vector", sS, sv)
    vv = b.op("vector", lambda e: e.tensor_scalar(nmx[:, 12:13], nmx[:, 11:12], -ASCALE, None, OP.mult), sV)
    b.wait("tensor", sV, vv)
    tv = b.op("tensor", lambda e: e.matmul(pbc[:, 0:1], c["ones1"][:], nmx[:, 12:13],
                                           start=True, stop=True), sT)
    b.wait("vector", sT, tv)
    mv = b.op("vector", lambda e: e.tensor_copy(mcol[:], pbc[:, 0:1]), sV)
    st["nred"] = mv
    return mv


def emit_attn_head(b, c, st, qh, kh, vtile, avt_slice, bufs, ready, mcol, mrdy):
    sV, sS, sT = b.sem("v"), b.sem("s"), b.sem("t")
    psr, pav, pbc = bufs["psr"], bufs["pav"], bufs["pbc"]
    nb_ = len(psr)
    PT, mskT = bufs["PT"], bufs["mskT"]
    lrow, invl, invlb = bufs["lrow"], bufs["invl"], bufs["invlb"]
    b.wait("tensor", sS, ready)
    first_exp = True
    for qb in range(4):
        nk = 4 * (qb + 1)
        mmv = {}
        for kk in range(nk + 2):
            if kk < nk:
                bk = kk % nb_
                b.wait("tensor", sS, st["psfree"][bk])
                if kk == 0:
                    b.wait("tensor", sV, st["linv_ev"])
                    b.wait("tensor", sV, st["pav_ev"])
                v = b.op("tensor", lambda e, bk=bk, kk=kk, qb=qb: e.matmul(
                    psr[bk][:], kh[:, kk * 128:(kk + 1) * 128], qh[:, qb * 512:(qb + 1) * 512],
                    start=True, stop=True), sT)
                mmv[kk] = v
                diag = kk >= 4 * qb
                if diag:
                    r = kk - 4 * qb
                    b.wait("vector", sT, v)
                    mmv[kk] = ("v", b.op("vector", lambda e, bk=bk, r=r: e.tensor_tensor(
                        psr[bk][:], psr[bk][:], mskT[:, (3 - r) * 128:(3 - r) * 128 + 512], OP.add), sV))
                if first_exp:
                    b.wait("scalar", sV, mrdy)
                    first_exp = False
                m = mmv[kk]
                if isinstance(m, tuple):
                    b.wait("scalar", sV, m[1])
                else:
                    b.wait("scalar", sT, m)
                b.wait("scalar", sT, st["PTmm"][kk % 3])
                st["psfree"][bk] = b.op("scalar", lambda e, bk=bk, kk=kk: e.activation(
                    PT[kk % 3][:], psr[bk][:], AF.Exp, bias=mcol[:], scale=ASCALE), sS)
                st["pexp"][kk % 3] = st["psfree"][bk]
            j = kk - 2
            if 0 <= j < nk:
                b.wait("tensor", sS, st["pexp"][j % 3])
                b.op("tensor", lambda e, j=j, nk=nk: e.matmul(
                    pav[:], vtile(j), PT[j % 3][:], start=(j == 0), stop=(j == nk - 1)))
                st["PTmm"][j % 3] = b.op("tensor", lambda e, j=j, nk=nk: e.matmul(
                    pbc[0:1, 0:512], c["onescol"][:], PT[j % 3][:],
                    start=(j == 0), stop=(j == nk - 1)), sT)
        lv = st["PTmm"][(nk - 1) % 3]
        b.wait("vector", sT, lv)
        b.op("vector", lambda e: e.tensor_copy(lrow[:], pbc[0:1, 0:512]))
        b.drain("vector")
        st["lrow_ev"] = b.op("vector", lambda e: e.reciprocal(invl[:], lrow[:]), sV)
        b.wait("tensor", sV, st["lrow_ev"])
        bcv = b.op("tensor", lambda e: e.matmul(pbc[:], c["ones1"][:], invl[:],
                                                start=True, stop=True), sT)
        b.wait("vector", sT, bcv)
        st["linv_ev"] = b.op("vector", lambda e: e.tensor_copy(invlb[:], pbc[:]), sV)
        st["pav_ev"] = b.op("vector", lambda e, qb=qb: e.tensor_tensor(
            avt_slice(qb), pav[:], invlb[:], OP.mult), sV)


def emit_p1_l0attn(nc, es, A, xin, xout, tag="p1", ntiles=NT, sems=None, sems_es=None):
    b = Bld(nc, es, tag, sems=sems, sems_es=sems_es)
    sV, sS, sT, sG = b.sem("v"), b.sem("s"), b.sem("t"), b.sem("g")
    sDC, sDX, sDO = b.sem("dc"), b.sem("dx"), b.sem("do")
    sWA = b.sem("wa")

    c = emit_consts(b, A)
    lng = b.sb("lng", [128, 16], F32)
    lnb = b.sb("lnb", [128, 16], F32)
    bqc = b.sb("bqc", [128, 16], F32)
    bkc = b.sb("bkc", [128, 16], F32)
    bvb = b.sb("bvb", [128, 16], BF16)
    msk = b.sb("msk", [128, 896], F32)
    for dst, src_ in [(lng, A["l0_lng"]), (lnb, A["l0_lnb"]), (bqc, A["l0_bq"]),
                      (bkc, A["l0_bk"]), (msk, A["masksT"])]:
        b.dma("sync", dst[:], src_, sDC)
    b.dma("gpsimd", bvb[:], A["l0_bv"], sDC)
    b.wait("sync", sDC, sDC.n)
    b.flush()

    pbc = b.ps("pbc", [128, 512], F32)
    pav = b.ps("pav", [128, 512], F32)

    AVT = b.sb("avt", [128, NC * T], BF16)

    st = dict(psfree=[0] * 6, PTmm=[0, 0, 0], pexp=[0, 0, 0], pav_ev=0,
              lrow_ev=0, linv_ev=0, nmm=0, nred=0, ptevac=[0, 0])

    with ExitStack() as xes:
        xnt = b.sb("xnt", [128, NC * T], BF16, xes)
        with ExitStack() as les:
            pta = b.ps("lnpta", [128, 512], BF16, les)
            ptb = b.ps("lnptb", [128, 512], BF16, les)
            xnt_done = emit_ln_phase(
                b, c, xin, lambda cc, tt: cc * T + tt * 128, xnt,
                lambda cc: lng[:, cc:cc + 1], lambda cc: lnb[:, cc:cc + 1],
                True, range(ntiles), les, (pta, ptb))
            b.flush()

        with ExitStack() as aes:
            psr = [b.ps(f"psr{i}", [128, 512], F32, aes) for i in range(6)]
            ps = psr
            nmx = b.sb("nmx", [1, 16], F32, aes)
            mcol = b.sb("mcol", [128, 1], F32, aes)
            sqa = b.sb("sqa", [128, T], BF16, aes)
            sqb_ = b.sb("sqb", [128, T], BF16, aes)
            lrowb = b.sb("lrowb", [1, 1024], F32, aes)
            lrow, invl = lrowb[:, 0:512], lrowb[:, 512:1024]
            invlb = b.sb("invlb", [128, 512], F32, aes)
            PT = [b.sb(f"PT{i}", [128, 512], BF16, aes) for i in range(3)]
            V_sb = b.sb("vsb", [128, 16 * 512], BF16, aes)
            qh = b.sb("qh", [128, T], BF16, aes)
            kh = b.sb("kh", [128, T], BF16, aes)
            wq_b = b.sb("wqb", [128, D], BF16, aes)
            wk_b = b.sb("wkb", [128, D], BF16, aes)
            bufs = dict(psr=psr, pbc=pbc, pav=pav, PT=PT, mskT=msk,
                        lrow=lrow, invl=invl, invlb=invlb)

            qk_done = 0
            for g in range(4):
                with ExitStack() as ges:
                    wvg = b.sb(f"wvg{g}", [128, 8192], BF16, ges)
                    dvv = b.dma("gpsimd", wvg[:], A["l0_wv"][g * 128:(g + 1) * 128, :], sWA)
                    b.wait("tensor", sWA, dvv)
                    b.wait("tensor", sS, xnt_done)
                    vev = [st["psfree"][0], st["psfree"][1]]
                    for tt in range(16):
                        pg = tt % 2
                        b.wait("tensor", sS, vev[pg])
                        mmv = None
                        for cc in range(NC):
                            mmv = b.op("tensor", lambda e, cc=cc, tt=tt, pg=pg, wvg=wvg: e.matmul(
                                ps[pg][:], xnt[:, cc * T + tt * 128: cc * T + (tt + 1) * 128],
                                wvg[:, cc * 512:(cc + 1) * 512],
                                start=(cc == 0), stop=(cc == NC - 1)), sT if cc == NC - 1 else None)
                        b.wait("scalar", sT, mmv)
                        vev[pg] = b.op("scalar", lambda e, tt=tt, pg=pg: e.activation(
                            V_sb[:, tt * 512:(tt + 1) * 512], ps[pg][:], AF.Copy), sS)
                    st["psfree"][0], st["psfree"][1] = vev
                    vprod_ev = sS.n
                    for hh in range(4):
                        h = g * 4 + hh
                        b.wait("gpsimd", sT, qk_done)
                        dq = b.dma("gpsimd", wq_b[:], A["l0_wq"][h * 128:(h + 1) * 128, :], sWA)
                        dk = b.dma("gpsimd", wk_b[:], A["l0_wk"][h * 128:(h + 1) * 128, :], sWA)
                        b.wait("tensor", sWA, dq)
                        kv = None
                        for (which, wbuf, dst, bias) in (("q", wq_b, qh, bqc), ("k", wk_b, kh, bkc)):
                            if which == "k":
                                b.wait("tensor", sWA, dk)
                            for tb in range(4):
                                pg = tb % 2
                                b.wait("tensor", sS, st["psfree"][pg])
                                mmv = None
                                for cc in range(NC):
                                    mmv = b.op("tensor", lambda e, cc=cc, tb=tb, pg=pg, wbuf=wbuf: e.matmul(
                                        ps[pg][:], wbuf[:, cc * 128:(cc + 1) * 128],
                                        xnt[:, cc * T + tb * 512: cc * T + tb * 512 + 512],
                                        start=(cc == 0), stop=(cc == NC - 1)), sT if cc == NC - 1 else None)
                                b.wait("scalar", sT, mmv)
                                st["psfree"][pg] = b.op("scalar", lambda e, tb=tb, pg=pg, dst=dst, bias=bias, h=h:
                                    e.activation(dst[:, tb * 512:(tb + 1) * 512], ps[pg][:], AF.Identity,
                                                 bias=bias[:, h:h + 1], scale=1.0), sS)
                            if which == "k":
                                kv = sS.n
                        qk_done = sT.n
                        mrdy = emit_head_bound(b, c, st, qh, kh, (sqa, sqb_), nmx, mcol, bufs, max(kv, vprod_ev))
                        emit_attn_head(
                            b, c, st, qh, kh,
                            lambda kk, hh=hh, V_sb=V_sb: V_sb[:, kk * 512 + hh * 128: kk * 512 + hh * 128 + 128],
                            lambda qb, h=h: AVT[:, h * T + qb * 512: h * T + (qb + 1) * 512],
                            bufs, max(kv, vprod_ev), mcol, mrdy)
                    b.flush()

    with ExitStack() as oes:
        ps = [b.ps(f"pso{i}", [128, 512], F32, oes) for i in range(2)]
        wo = b.sb("wo", [128, NC * D], BF16, oes)
        borow = b.sb("borow", [1, D], F32, oes)
        boeff = b.sb("boeff", [1, D], F32, oes)
        bob = b.sb("bob", [128, D], F32, oes)
        xt = [b.sb(f"oxt{i}", [128, D], F32, oes) for i in range(2)]
        xbb = [b.sb(f"oxb{i}", [128, D], F32, oes) for i in range(2)]
        x1t = [b.sb(f"ox1{i}", [128, D], F32, oes) for i in range(2)]
        b.dma("sync", borow[:], A["l0_bo"], sDC)
        b.wait("sync", sDC, sDC.n)
        for cc in range(NC):
            b.dma("gpsimd", wo[:, cc * D:(cc + 1) * D], A["l0_wo"][cc * 128:(cc + 1) * 128, :], sWA)
        b.wait("tensor", sWA, sWA.n)
        b.flush()
        for ob in range(4):
            mmv = None
            for cc in range(NC):
                mmv = b.op("tensor", lambda e, cc=cc, ob=ob: e.matmul(
                    pbc[0:1, 0:512], bvb[:, cc:cc + 1], wo[:, cc * D + ob * 512: cc * D + (ob + 1) * 512],
                    start=(cc == 0), stop=(cc == NC - 1)), sT if cc == NC - 1 else None)
            b.wait("vector", sT, mmv)
            ev = b.op("vector", lambda e, ob=ob: e.tensor_tensor(
                boeff[:, ob * 512:(ob + 1) * 512], pbc[0:1, 0:512], borow[:, ob * 512:(ob + 1) * 512],
                OP.add), sV)
            b.wait("tensor", sV, ev)
        b.flush()
        emit_bcast_row(b, c, boeff, bob, pbc)
        b.flush()
        xt_free = [0, 0]
        x1_free = [0, 0]
        xb_free = [0, 0]
        pmm_free = [0, 0]
        for tt in range(ntiles):
            ib = tt % 2
            b.wait("sync", sG, xt_free[ib])
            dv = b.dma("sync", xt[ib][:], xin[tt * 128:(tt + 1) * 128, :], sDX)
            b.wait("gpsimd", sDX, dv)
            b.wait("gpsimd", sV, xb_free[ib])
            gv = b.op("gpsimd", lambda e, ib=ib: e.tensor_tensor(
                xbb[ib][:], xt[ib][:], bob[:], OP.add), sG)
            xt_free[ib] = gv
            for ob in range(4):
                pg = ob % 2
                b.wait("tensor", sV, pmm_free[pg])
                mmv = None
                for cc in range(NC):
                    mmv = b.op("tensor", lambda e, cc=cc, tt=tt, ob=ob, pg=pg: e.matmul(
                        ps[pg][:], AVT[:, cc * T + tt * 128: cc * T + (tt + 1) * 128],
                        wo[:, cc * D + ob * 512: cc * D + (ob + 1) * 512],
                        start=(cc == 0), stop=(cc == NC - 1)), sT if cc == NC - 1 else None)
                b.wait("vector", sT, mmv)
                b.wait("vector", sG, gv)
                b.wait("vector", sDO, x1_free[ib])
                pmm_free[pg] = b.op("vector", lambda e, ob=ob, pg=pg, ib=ib: e.tensor_tensor(
                    x1t[ib][:, ob * 512:(ob + 1) * 512], ps[pg][:], xbb[ib][:, ob * 512:(ob + 1) * 512],
                    OP.add), sV)
            xb_free[ib] = sV.n
            b.wait("scalar", sV, sV.n)
            x1_free[ib] = b.dma("scalar", xout[tt * 128:(tt + 1) * 128, :], x1t[ib][:], sDO)
        b.wait("scalar", sDO, sDO.n)
        b.flush()


def emit_p3_l1attn(nc, es, A, xin, xout, tag="p3", ntiles=NT, sems=None, sems_es=None):
    b = Bld(nc, es, tag, sems=sems, sems_es=sems_es)
    sV, sS, sT, sG = b.sem("v"), b.sem("s"), b.sem("t"), b.sem("g")
    sDC, sDX, sDO = b.sem("dc"), b.sem("dx"), b.sem("do")
    sWA = b.sem("wa")

    c = emit_consts(b, A)
    msk = b.sb("msk", [128, 896], F32)
    b.dma("sync", msk[:], A["masksT"], sDC)
    b.wait("sync", sDC, sDC.n)
    b.flush()

    pbc = b.ps("pbc", [128, 512], F32)
    pav = b.ps("pav", [128, 512], F32)

    AVT = b.sb("avt", [128, NC * T], BF16)

    st = dict(psfree=[0] * 4, PTmm=[0, 0, 0], pexp=[0, 0, 0], pav_ev=0,
              lrow_ev=0, linv_ev=0, nmm=0, nred=0, ptevac=[0, 0])

    with ExitStack() as aes:
        psr = [b.ps(f"psr{i}", [128, 512], F32, aes) for i in range(4)]
        ps = psr
        pta = b.ps("pta", [128, 512], BF16, aes)
        ptb = b.ps("ptb", [128, 512], BF16, aes)
        nmx = b.sb("nmx", [1, 16], F32, aes)
        mcol = b.sb("mcol", [128, 1], F32, aes)
        sqa = b.sb("sqa", [128, T], BF16, aes)
        sqb_ = b.sb("sqb", [128, T], BF16, aes)
        lrowb = b.sb("lrowb", [1, 1024], F32, aes)
        lrow, invl = lrowb[:, 0:512], lrowb[:, 512:1024]
        invlb = b.sb("invlb", [128, 512], F32, aes)
        PT = [b.sb(f"PT{i}", [128, 512], BF16, aes) for i in range(3)]
        bufs = dict(psr=psr, pbc=pbc, pav=pav, PT=PT, mskT=msk,
                    lrow=lrow, invl=invl, invlb=invlb)
        with ExitStack() as xes:
            xnt = b.sb("xnt", [128, NC * T], BF16, xes)
            with ExitStack() as les:
                xnt_done = emit_ln_phase(
                    b, c, xin, lambda cc, tt: cc * T + tt * 128, xnt,
                    None, None, False, range(ntiles), les, (pta, ptb))
                b.flush()

            wdone = 0
            for g in range(4):
                with ExitStack() as ges:
                    Qg = b.sb(f"qg{g}", [128, 4 * T], BF16, ges)
                    Kg = b.sb(f"kg{g}", [128, T], BF16, ges)
                    Vt = b.sb(f"vt{g}", [128, T], BF16, ges)
                    Vg = b.sb(f"vg{g}", [128, T], BF16, ges)
                    wb = [b.sb(f"wb{g}{i}", [128, D], BF16, ges) for i in range(2)]
                    vtv = 0
                    for r in range(6):
                        pc = 6 * g + r
                        ib = r % 2
                        b.wait("gpsimd", sT, wdone if r < 2 else wdone)
                        dq = b.dma("gpsimd", wb[ib][:], A["l1_wqkv"][pc * 128:(pc + 1) * 128, :], sWA)
                        b.wait("tensor", sWA, dq)
                        dst = (Qg[:, r * T:(r + 1) * T] if r < 4 else (Kg[:] if r == 4 else Vt[:]))
                        for tb in range(4):
                            pg = tb % 2
                            b.wait("tensor", sS, st["psfree"][pg])
                            mmv = None
                            for cc in range(NC):
                                mmv = b.op("tensor", lambda e, cc=cc, tb=tb, pg=pg, ib=ib: e.matmul(
                                    ps[pg][:], wb[ib][:, cc * 128:(cc + 1) * 128],
                                    xnt[:, cc * T + tb * 512: cc * T + tb * 512 + 512],
                                    start=(cc == 0), stop=(cc == NC - 1)), sT if cc == NC - 1 else None)
                            wdone = mmv
                            b.wait("scalar", sT, mmv)
                            st["psfree"][pg] = b.op("scalar", lambda e, tb=tb, pg=pg, dst=dst: e.activation(
                                dst[:, tb * 512:(tb + 1) * 512], ps[pg][:], AF.Copy), sS)
                        if r == 5:
                            vtv = sS.n
                    b.wait("tensor", sS, vtv)
                    ptp = [pta, ptb]
                    for g4 in range(4):
                        pg = g4 % 2
                        b.wait("tensor", sS, st["ptevac"][pg])
                        for j in range(4):
                            kk = g4 * 4 + j
                            b.op("tensor", lambda e, kk=kk, j=j, pg=pg: e.transpose(
                                ptp[pg][:, j * 128:(j + 1) * 128], Vt[:, kk * 128:(kk + 1) * 128],
                                c["idb"][:]), sT if j == 3 else None)
                        b.wait("scalar", sT, sT.n)
                        st["ptevac"][pg] = b.op("scalar", lambda e, g4=g4, pg=pg: e.activation(
                            Vg[:, g4 * 512:(g4 + 1) * 512], ptp[pg][:], AF.Copy), sS)
                    vgv = sS.n
                    for j in range(4):
                        h = g * 4 + j
                        mrdy = emit_head_bound(b, c, st, Qg[:, j * T:(j + 1) * T], Kg,
                                               (sqa, sqb_), nmx, mcol, bufs, vgv)
                        emit_attn_head(
                            b, c, st, Qg[:, j * T:(j + 1) * T], Kg,
                            lambda kk, Vg=Vg: Vg[:, kk * 128:(kk + 1) * 128],
                            lambda qb, h=h: AVT[:, h * T + qb * 512: h * T + (qb + 1) * 512],
                            bufs, vgv, mcol, mrdy)
                    b.flush()

    with ExitStack() as oes:
        ps = [b.ps(f"pso{i}", [128, 512], F32, oes) for i in range(2)]
        wo = b.sb("wo", [128, NC * D], BF16, oes)
        xt = [b.sb(f"oxt{i}", [128, D], F32, oes) for i in range(2)]
        x1t = [b.sb(f"ox1{i}", [128, D], F32, oes) for i in range(2)]
        for cc in range(NC):
            b.dma("gpsimd", wo[:, cc * D:(cc + 1) * D], A["l1_wproj"][cc * 128:(cc + 1) * 128, :], sWA)
        b.wait("tensor", sWA, sWA.n)
        b.flush()
        xt_free = [0, 0]
        x1_free = [0, 0]
        pmm_free = [0, 0]
        for tt in range(ntiles):
            ib = tt % 2
            b.wait("sync", sV, xt_free[ib])
            dv = b.dma("sync", xt[ib][:], xin[tt * 128:(tt + 1) * 128, :], sDX)
            for ob in range(4):
                pg = ob % 2
                b.wait("tensor", sV, pmm_free[pg])
                mmv = None
                for cc in range(NC):
                    mmv = b.op("tensor", lambda e, cc=cc, tt=tt, ob=ob, pg=pg: e.matmul(
                        ps[pg][:], AVT[:, cc * T + tt * 128: cc * T + (tt + 1) * 128],
                        wo[:, cc * D + ob * 512: cc * D + (ob + 1) * 512],
                        start=(cc == 0), stop=(cc == NC - 1)), sT if cc == NC - 1 else None)
                b.wait("vector", sT, mmv)
                b.wait("vector", sDX, dv)
                b.wait("vector", sDO, x1_free[ib])
                pmm_free[pg] = b.op("vector", lambda e, ob=ob, pg=pg, ib=ib: e.tensor_tensor(
                    x1t[ib][:, ob * 512:(ob + 1) * 512], ps[pg][:], xt[ib][:, ob * 512:(ob + 1) * 512],
                    OP.add), sV)
            xt_free[ib] = sV.n
            b.wait("scalar", sV, sV.n)
            x1_free[ib] = b.dma("scalar", xout[tt * 128:(tt + 1) * 128, :], x1t[ib][:], sDO)
        b.wait("scalar", sDO, sDO.n)
        b.flush()


def emit_p4_l1ffn(nc, es, A, xin, xout, ddram, tag="p4", sems=None, sems_es=None):
    b = Bld(nc, es, tag, sems=sems, sems_es=sems_es)
    sV, sS, sT, sG = b.sem("v"), b.sem("s"), b.sem("t"), b.sem("g")
    sDC, sDX, sDO, sWA, sWB = b.sem("dc"), b.sem("dx"), b.sem("do"), b.sem("wa"), b.sem("wb")
    TH = 1024

    c = emit_consts(b, A)
    sgc = b.sb("sgc", [128, 44], F32)
    suc = b.sb("suc", [128, 44], F32)
    sdrow = b.sb("sdrow", [1, D], F32)
    sdb = b.sb("sdb", [128, D], F32)
    onesc = b.sb("onesc", [128, 1], BF16)
    b.dma("sync", sgc[:], A["l1_sg"], sDC)
    b.dma("sync", suc[:], A["l1_su"], sDC)
    b.dma("sync", sdrow[:], A["l1_sd"], sDC)
    b.op("vector", lambda e: e.memset(onesc[:], 1.0))
    b.wait("sync", sDC, sDC.n)
    b.flush()

    pta = b.ps("pta", [128, 512], BF16)
    ptb = b.ps("ptb", [128, 512], BF16)
    ps = [b.ps(f"ps{i}", [128, 512], F32) for i in range(4)]
    pgu = [b.ps(f"pgu{i}", [1, 512], F32) for i in range(2)]

    emit_bcast_row(b, c, sdrow, sdb, ps[0])
    b.flush()

    for half in range(2):
        off = half * 8
        with ExitStack() as hes:
            gu = b.sb(f"gu{half}", [128, 44 * TH], BF16, hes)
            gurow = b.sb(f"gurow{half}", [1, TH], F32, hes)
            m128g = b.sb(f"m128g{half}", [128, 8], F32, hes)
            with ExitStack() as xes:
                xnt = b.sb(f"xnt{half}", [128, NC * TH], BF16, xes)
                with ExitStack() as les:
                    emit_ln_phase(
                        b, c, xin, lambda cc, tt: cc * TH + (tt - off) * 128, xnt,
                        None, None, False, range(off, off + 8), les, (pta, ptb))
                    b.flush()
                with ExitStack() as ges:
                    wstg = [b.sb(f"wsg{half}{i}", [128, D], BF16, ges) for i in range(2)]
                    wstu = [b.sb(f"wsu{half}{i}", [128, D], BF16, ges) for i in range(2)]
                    wgb = [b.sb(f"wgb{half}{i}", [128, D], BF16, ges) for i in range(2)]
                    wub = [b.sb(f"wub{half}{i}", [128, D], BF16, ges) for i in range(2)]
                    grow = b.sb(f"grow{half}", [128, TH], BF16, ges)
                    urow = b.sb(f"urow{half}", [128, TH], BF16, ges)
                    stg_free = [(0, 0), (0, 0)]
                    wgb_free = [0, 0]
                    ev_free = [0, 0, 0, 0]
                    psfree = [0, 0, 0, 0]
                    growv = [0, 0]
                    for hc in range(NHC):
                        ib = hc % 2
                        b.wait("gpsimd", sV, stg_free[ib][0])
                        dg = b.dma("gpsimd", wstg[ib][:], A["l1_wg"][hc * 128:(hc + 1) * 128, :], sWA)
                        du = b.dma("gpsimd", wstu[ib][:], A["l1_wu"][hc * 128:(hc + 1) * 128, :], sWA)
                        b.wait("vector", sWA, dg)
                        b.wait("vector", sT, wgb_free[ib])
                        b.op("vector", lambda e, ib=ib: e.tensor_scalar(
                            wgb[ib][:], wstg[ib][:], 128.0, None, OP.subtract))
                        b.drain("vector")
                        b.wait("vector", sWA, du)
                        uq = b.op("vector", lambda e, ib=ib: e.tensor_scalar(
                            wub[ib][:], wstu[ib][:], 128.0, None, OP.subtract), sV)
                        stg_free[ib] = (uq, uq)
                        b.wait("tensor", sV, uq)
                        for which, base, wsrc, scol, func in (
                                ("g", 0, wgb[ib], sgc, AF.Silu), ("u", 2, wub[ib], suc, AF.Identity)):
                            for tb in range(2):
                                pg = base + tb
                                b.wait("tensor", sS, psfree[pg])
                                mmv = None
                                for cc in range(NC):
                                    mmv = b.op("tensor", lambda e, cc=cc, tb=tb, pg=pg, wsrc=wsrc: e.matmul(
                                        ps[pg][:], wsrc[:, cc * 128:(cc + 1) * 128],
                                        xnt[:, cc * TH + tb * 512: cc * TH + tb * 512 + 512],
                                        start=(cc == 0), stop=(cc == NC - 1)), sT if cc == NC - 1 else None)
                                b.wait("scalar", sT, mmv)
                                dst = grow if which == "g" else urow
                                b.wait("scalar", sV, growv[tb])
                                psfree[pg] = b.op("scalar", lambda e, tb=tb, pg=pg, dst=dst, scol=scol, hc=hc, func=func:
                                    e.activation(dst[:, tb * 512:(tb + 1) * 512], ps[pg][:], func,
                                                 bias=0.0, scale=scol[:, hc:hc + 1]), sS)
                        wgb_free[ib] = sT.n
                        for tb in range(2):
                            b.wait("vector", sS, psfree[2 + tb])
                            growv[tb] = b.op("vector", lambda e, hc=hc, tb=tb: e.tensor_tensor(
                                gu[:, hc * TH + tb * 512: hc * TH + tb * 512 + 512],
                                grow[:, tb * 512:(tb + 1) * 512], urow[:, tb * 512:(tb + 1) * 512],
                                OP.mult), sV)
                            b.wait("tensor", sV, growv[tb])
                            b.op("tensor", lambda e, hc=hc, tb=tb: e.matmul(
                                pgu[tb][:], onesc[:], gu[:, hc * TH + tb * 512: hc * TH + tb * 512 + 512],
                                start=(hc == 0), stop=(hc == NHC - 1)), sT if hc == NHC - 1 else None)
                    gusum_mm = sT.n
                    b.flush()
            b.wait("vector", sT, gusum_mm)
            for tb in range(2):
                b.op("vector", lambda e, tb=tb: e.tensor_copy(gurow[:, tb * 512:(tb + 1) * 512], pgu[tb][:]))
            b.drain("vector")
            grv = b.op("vector", lambda e: e.engine_nop(), sV)
            b.wait("tensor", sV, grv)
            for j in range(8):
                b.op("tensor", lambda e, j=j: e.matmul(
                    ps[0][:, j:j + 1], gurow[:, j * 128:(j + 1) * 128], c["ones1"][0:1, 0:1],
                    start=True, stop=True), sT if j == 7 else None)
            gtv = sT.n
            b.wait("vector", sT, gtv)
            b.op("vector", lambda e: e.tensor_scalar(m128g[:], ps[0][:, 0:8], -128.0, None, OP.mult))
            b.drain("vector")
            m128v = b.op("vector", lambda e: e.engine_nop(), sV)
            b.flush()

            with ExitStack() as des:
                wdb = [b.sb(f"wdb{half}{i}", [128, 44 * 512], BF16, des) for i in range(2)]
                tmp = [b.sb(f"dtmp{half}{i}", [128, 512], F32, des) for i in range(2)]
                wd_used = [0, 0]
                tmp_free = [0, 0]
                dmad = []
                for db in range(4):
                    ibw = db % 2
                    b.wait("gpsimd", sT, wd_used[ibw])
                    dws = []
                    for q in range(4):
                        lo = q * 11
                        dws.append(b.dma(
                            "gpsimd",
                            wdb[ibw][:].rearrange("p (hc j) -> p hc j", j=512)[:, lo:lo + 11, :],
                            A["l1_wd"][db * 5632 + lo * 128: db * 5632 + (lo + 11) * 128, :]
                            .rearrange("(hc p) j -> p hc j", p=128),
                            sWB))
                    for tt in range(off, off + 8):
                        ti = tt - off
                        pg = 2 + ti % 2
                        b.wait("tensor", sWB, dws[-1])
                        if len(dmad) >= 2:
                            b.wait("tensor", sV, dmad[-2][1])
                        b.wait("tensor", sV, grv)
                        mmv = None
                        for hc in range(NHC):
                            mmv = b.op("tensor", lambda e, hc=hc, ti=ti, pg=pg, ibw=ibw: e.matmul(
                                ps[pg][:], gu[:, hc * TH + ti * 128: hc * TH + (ti + 1) * 128],
                                wdb[ibw][:, hc * 512:(hc + 1) * 512],
                                start=(hc == 0), stop=(hc == NHC - 1)), sT if hc == NHC - 1 else None)
                        wd_used[ibw] = mmv
                        ibt = ti % 2
                        b.wait("vector", sT, mmv)
                        b.wait("vector", sV, m128v)
                        b.wait("vector", sDO, tmp_free[ibt])
                        vv = b.op("vector", lambda e, ti=ti, pg=pg, ibt=ibt, db=db: e.scalar_tensor_tensor(
                            tmp[ibt][:], ps[pg][:], m128g[:, ti:ti + 1], sdb[:, db * 512:(db + 1) * 512],
                            OP.add, OP.mult), sV)
                        dmad.append((None, vv))
                        b.wait("scalar", sV, vv)
                        tmp_free[ibt] = b.dma(
                            "scalar", ddram[tt * 128:(tt + 1) * 128, db * 512:(db + 1) * 512],
                            tmp[ibt][:], sDO)
                b.flush()

        with ExitStack() as fes:
            dxt = [b.sb(f"fx{half}{i}", [128, D], F32, fes) for i in range(2)]
            ddt = [b.sb(f"fd{half}{i}", [128, D], F32, fes) for i in range(2)]
            x4t = [b.sb(f"fo{half}{i}", [128, D], F32, fes) for i in range(2)]
            bfree = [0, 0]
            ofree = [0, 0]
            b.wait("sync", sDO, sDO.n)
            for tt in range(off, off + 8):
                ib = tt % 2
                b.wait("sync", sV, bfree[ib])
                d1 = b.dma("sync", dxt[ib][:], xin[tt * 128:(tt + 1) * 128, :], sDX)
                d2 = b.dma("sync", ddt[ib][:], ddram[tt * 128:(tt + 1) * 128, :], sDX)
                b.wait("vector", sDX, d2)
                b.wait("vector", sDO, ofree[ib])
                bfree[ib] = b.op("vector", lambda e, ib=ib: e.tensor_tensor(
                    x4t[ib][:], dxt[ib][:], ddt[ib][:], OP.add), sV)
                b.wait("scalar", sV, bfree[ib])
                ofree[ib] = b.dma("scalar", xout[tt * 128:(tt + 1) * 128, :], x4t[ib][:], sDO)
            b.wait("scalar", sDO, sDO.n)
            b.flush()


P4_INPUTS = ["ident_bf", "ident_f32", "l1_wg", "l1_wu", "l1_wd", "l1_sg", "l1_su", "l1_sd"]


def build_p4_standalone():
    nc = bass.Bass("TRN2", target_bir_lowering=False)
    A = declare_shared(nc, P4_INPUTS)
    xin = nc.dram_tensor("xin", [T, D], F32, kind="ExternalInput").ap()
    xout = nc.dram_tensor("xout", [T, D], F32, kind="ExternalOutput").ap()
    ddram = nc.dram_tensor("ddram", [T, D], F32, kind="Internal").ap()
    with ExitStack() as es:
        emit_p4_l1ffn(nc, es, A, xin, xout, ddram)
    return nc


P3_INPUTS = ["ident_bf", "ident_f32", "masksT", "l1_wqkv", "l1_wproj"]


def build_p3_standalone(ntiles=NT):
    nc = bass.Bass("TRN2", target_bir_lowering=False)
    A = declare_shared(nc, P3_INPUTS)
    xin = nc.dram_tensor("xin", [T, D], F32, kind="ExternalInput").ap()
    xout = nc.dram_tensor("xout", [T, D], F32, kind="ExternalOutput").ap()
    with ExitStack() as es:
        emit_p3_l1attn(nc, es, A, xin, xout, ntiles=ntiles)
    return nc


P1_INPUTS = ["ident_bf", "ident_f32", "masksT", "l0_wq", "l0_wk", "l0_wv", "l0_wo",
             "l0_lng", "l0_lnb", "l0_bq", "l0_bk", "l0_bv", "l0_bo"]


def build_p1_standalone(ntiles=NT):
    nc = bass.Bass("TRN2", target_bir_lowering=False)
    A = declare_shared(nc, P1_INPUTS)
    xin = nc.dram_tensor("xin", [T, D], F32, kind="ExternalInput").ap()
    xout = nc.dram_tensor("xout", [T, D], F32, kind="ExternalOutput").ap()
    with ExitStack() as es:
        emit_p1_l0attn(nc, es, A, xin, xout, ntiles=ntiles)
    return nc



def _prep_shared(inp):
    A = {}
    A["ident_bf"] = np.eye(128, dtype=BFNP)
    A["ident_f32"] = np.eye(128, dtype=np.float32)
    p = np.arange(128)[:, None]
    mcol = np.arange(896)[None, :]
    A["masks"] = np.where(mcol <= p + 384, 0.0, NEG).astype(np.float32)
    A["masksT"] = np.where(mcol >= p + 384, 0.0, NEG).astype(np.float32)

    def colchunks(v):
        return np.ascontiguousarray(v.reshape(-1, 128).T)

    A["l0_wq"] = np.ascontiguousarray(
        inp["L0_attn_Wq"].reshape(16, 128, 16, 128).transpose(2, 1, 0, 3).reshape(2048, 2048))
    A["l0_wk"] = np.ascontiguousarray(
        inp["L0_attn_Wk"].reshape(16, 128, 16, 128).transpose(2, 1, 0, 3).reshape(2048, 2048))
    A["l0_wv"] = np.ascontiguousarray(
        inp["L0_attn_Wv"].reshape(16, 128, 4, 512).transpose(2, 1, 0, 3).reshape(512, 8192))
    A["l0_wo"] = np.ascontiguousarray(inp["L0_attn_Wo"])
    A["l0_lng"] = colchunks(inp["L0_attn_ln_g"])
    A["l0_lnb"] = colchunks(inp["L0_attn_ln_b"])
    A["l0_bq"] = colchunks(inp["L0_attn_bq"])
    A["l0_bk"] = colchunks(inp["L0_attn_bk"])
    A["l0_bv"] = colchunks(inp["L0_attn_bv"])
    A["l0_bo"] = np.ascontiguousarray(inp["L0_attn_bo"].reshape(1, 2048))
    A["l0f_w"] = np.ascontiguousarray(inp["L0_ffn_weight"].T)
    A["l0f_gam"] = colchunks(inp["L0_ffn_ln_gamma"])
    A["l0f_bet"] = colchunks(inp["L0_ffn_ln_beta"])
    A["l0f_bias"] = np.ascontiguousarray(inp["L0_ffn_bias"].reshape(1, 2048))
    A["l1_wqkv"] = np.ascontiguousarray(
        inp["L1_attn_weight_qkv"].T.reshape(16, 128, 24, 128).transpose(2, 1, 0, 3).reshape(3072, 2048))
    A["l1_wproj"] = np.ascontiguousarray(inp["L1_attn_weight_proj"].T)
    A["l1_wg"] = np.ascontiguousarray(
        inp["L1_ffn_wq_gate"].T.reshape(16, 128, 44, 128).transpose(2, 1, 0, 3).reshape(5632, 2048))
    A["l1_wu"] = np.ascontiguousarray(
        inp["L1_ffn_wq_up"].T.reshape(16, 128, 44, 128).transpose(2, 1, 0, 3).reshape(5632, 2048))
    A["l1_wd"] = np.ascontiguousarray(
        inp["L1_ffn_wq_down"].T.reshape(44, 128, 4, 512).transpose(2, 0, 1, 3).reshape(22528, 512))
    A["l1_sg"] = colchunks(inp["L1_ffn_s_gate"])
    A["l1_su"] = colchunks(inp["L1_ffn_s_up"])
    A["l1_sd"] = np.ascontiguousarray(inp["L1_ffn_s_down"].reshape(1, 2048))
    return A


SHARED_SPECS = {
    "ident_bf": ([128, 128], BF16), "ident_f32": ([128, 128], F32),
    "masks": ([128, 896], F32), "masksT": ([128, 896], F32),
    "l0_wq": ([2048, 2048], F32), "l0_wk": ([2048, 2048], F32),
    "l0_wv": ([512, 8192], F32), "l0_wo": ([2048, 2048], F32),
    "l0_lng": ([128, 16], F32), "l0_lnb": ([128, 16], F32),
    "l0_bq": ([128, 16], F32), "l0_bk": ([128, 16], F32), "l0_bv": ([128, 16], F32),
    "l0_bo": ([1, 2048], F32),
    "l0f_w": ([2048, 2048], F32), "l0f_gam": ([128, 16], F32),
    "l0f_bet": ([128, 16], F32), "l0f_bias": ([1, 2048], F32),
    "l1_wqkv": ([3072, 2048], F32), "l1_wproj": ([2048, 2048], F32),
    "l1_wg": ([5632, 2048], I32), "l1_wu": ([5632, 2048], I32), "l1_wd": ([22528, 512], I32),
    "l1_sg": ([128, 44], F32), "l1_su": ([128, 44], F32), "l1_sd": ([1, 2048], F32),
}


def declare_shared(nc, names):
    aps = {}
    for n in names:
        shape, dt = SHARED_SPECS[n]
        aps[n] = nc.dram_tensor(n, shape, dt, kind="ExternalInput").ap()
    return aps


P2_INPUTS = ["ident_bf", "ident_f32", "l0f_w", "l0f_gam", "l0f_bet", "l0f_bias"]


def build_p2_standalone(debug=False, ntiles=NT):
    nc = bass.Bass("TRN2", target_bir_lowering=False)
    A = declare_shared(nc, P2_INPUTS)
    xin = nc.dram_tensor("xin", [T, D], F32, kind="ExternalInput").ap()
    xout = nc.dram_tensor("xout", [T, D], F32, kind="ExternalOutput").ap()
    dbg = None
    if debug:
        dbg = {
            "rinv": nc.dram_tensor("d_rinv", [128, 1], F32, kind="ExternalOutput").ap(),
            "wq0": nc.dram_tensor("d_wq0", [128, D], BF16, kind="ExternalOutput").ap(),
            "u0": nc.dram_tensor("d_u0", [128, D], BF16, kind="ExternalOutput").ap(),
            "xnt0": nc.dram_tensor("d_xnt0", [128, D], BF16, kind="ExternalOutput").ap(),
            "biasb": nc.dram_tensor("d_biasb", [128, D], F32, kind="ExternalOutput").ap(),
            "coef": nc.dram_tensor("d_coef", [128, 8], F32, kind="ExternalOutput").ap(),
            "nb": nc.dram_tensor("d_nb", [128, 1], F32, kind="ExternalOutput").ap(),
        }
    with ExitStack() as es:
        emit_p2_l0ffn(nc, es, A, xin, xout, dbg=dbg, ntiles=ntiles)
    return nc


def build_fused():
    nc = bass.Bass("TRN2", target_bir_lowering=False)
    names = sorted(set(P1_INPUTS) | set(P2_INPUTS) | set(P3_INPUTS) | set(P4_INPUTS))
    A = declare_shared(nc, names)
    x0 = nc.dram_tensor("xin", [T, D], F32, kind="ExternalInput").ap()
    x4 = nc.dram_tensor("xout", [T, D], F32, kind="ExternalOutput").ap()
    x1 = nc.dram_tensor("x1s", [T, D], F32, kind="Internal").ap()
    x2 = nc.dram_tensor("x2s", [T, D], F32, kind="Internal").ap()
    x3 = nc.dram_tensor("x3s", [T, D], F32, kind="Internal").ap()
    dd = nc.dram_tensor("ddram", [T, D], F32, kind="Internal").ap()
    sems = {}
    with ExitStack() as ses:
        with ExitStack() as es:
            emit_p1_l0attn(nc, es, A, x0, x1, tag="p1", sems=sems, sems_es=ses)
        with ExitStack() as es:
            emit_p2_l0ffn(nc, es, A, x1, x2, tag="p2", sems=sems, sems_es=ses)
        with ExitStack() as es:
            emit_p3_l1attn(nc, es, A, x2, x3, tag="p3", sems=sems, sems_es=ses)
        with ExitStack() as es:
            emit_p4_l1ffn(nc, es, A, x3, x4, dd, tag="p4", sems=sems, sems_es=ses)
    return nc, names


_BUILD_CACHE = {}


def _get_phase_programs():
    if "phases" not in _BUILD_CACHE:
        _BUILD_CACHE["phases"] = [
            (build_p1_standalone(), P1_INPUTS),
            (build_p2_standalone(), P2_INPUTS),
            (build_p3_standalone(), P3_INPUTS),
            (build_p4_standalone(), P4_INPUTS),
        ]
    return _BUILD_CACHE["phases"]


def kernel(**inputs):
    A = _prep_shared(inputs)
    if "fused" not in _BUILD_CACHE:
        _BUILD_CACHE["fused"] = build_fused()
    nc, names = _BUILD_CACHE["fused"]
    in_maps = []
    for i in range(NCORES):
        m = {k: A[k] for k in names}
        m["xin"] = np.ascontiguousarray(inputs["L0_attn_X"][i])
        in_maps.append(m)
    res = run_bass_kernel_spmd(nc, in_maps, core_ids=list(range(NCORES)))
    return np.stack([res.results[i]["xout"] for i in range(NCORES)], axis=0)


# version 08
# speedup vs baseline: 1.4151x; 1.0005x over previous
import numpy as np
import ml_dtypes
from contextlib import ExitStack
import concourse.bass as bass
import concourse.mybir as mybir
from concourse.bass_utils import run_bass_kernel_spmd
from concourse.alu_op_type import AluOpType as OP

F32 = mybir.dt.float32
BF16 = mybir.dt.bfloat16
I32 = mybir.dt.int32
AF = mybir.ActivationFunctionType
AX = mybir.AxisListType.X
BFNP = ml_dtypes.bfloat16

ENGINES = ("sync", "tensor", "vector", "scalar", "gpsimd")
NCORES = 8
T = 2048
D = 2048
NT = T // 128
NC = D // 128
HFF = 5632
NHC = HFF // 128
INVD = 1.0 / D
EPS = 1e-5
ASCALE = 0.08838834764831845
NEG = -1.0e30


class Sem:
    __slots__ = ("h", "n")

    def __init__(self, h):
        self.h = h
        self.n = 0


class Bld:

    def __init__(self, nc, es, tag, sems=None, sems_es=None):
        self.nc, self.es, self.tag = nc, es, tag
        self.q = {e: [] for e in ENGINES}
        self._sems = sems if sems is not None else {}
        self._sems_es = sems_es
        self._uid = 0

    def fresh(self):
        self._uid += 1
        return self._uid

    def sem(self, name):
        if name not in self._sems:
            es = self._sems_es or self.es
            self._sems[name] = Sem(es.enter_context(self.nc.semaphore(f"sem_{name}_{self.tag}")))
        return self._sems[name]

    def sb(self, name, shape, dt, es=None):
        return (es or self.es).enter_context(self.nc.sbuf_tensor(f"{self.tag}_{name}", shape, dt))

    def ps(self, name, shape, dt, es=None):
        return (es or self.es).enter_context(self.nc.psum_tensor(f"{self.tag}_{name}", shape, dt))

    def op(self, eng, fn, sem=None, k=1):
        if sem is None:
            self.q[eng].append(fn)
            return None
        sem.n += k
        v = sem.n
        self.q[eng].append(lambda e, fn=fn, s=sem.h, k=k: fn(e).then_inc(s, k))
        return v

    def dma(self, eng, dst, src, sem):
        return self.op(eng, lambda e, d=dst, s=src: e.dma_start(d, s), sem, 16)

    def wait(self, eng, sem, v):
        if v:
            self.q[eng].append(lambda e, s=sem.h, v=v: e.wait_ge(s, v))

    def drain(self, eng):
        self.q[eng].append(lambda e: e.drain())

    def flush(self):
        q = self.q
        self.q = {e: [] for e in ENGINES}
        with self.nc.Block() as blk:
            for name in ENGINES:
                ops = q[name]
                if ops:
                    getattr(blk, name)(lambda eng, ops=ops: [f(eng) for f in ops])


def emit_consts(b, A):
    c = {}
    c["idb"] = b.sb("idb", [128, 128], BF16)
    c["idf"] = b.sb("idf", [128, 128], F32)
    c["ones1"] = b.sb("ones1", [1, 128], F32)
    c["epsc"] = b.sb("epsc", [128, 1], F32)
    c["onescol"] = b.sb("onescol", [128, 1], BF16)
    sDC = b.sem("dc")
    b.dma("sync", c["idb"][:], A["ident_bf"], sDC)
    b.dma("sync", c["idf"][:], A["ident_f32"], sDC)
    b.op("vector", lambda e: e.memset(c["ones1"][:], 1.0))
    b.op("vector", lambda e: e.memset(c["epsc"][:], EPS))
    b.op("vector", lambda e: e.memset(c["onescol"][:], 1.0))
    b.wait("sync", sDC, sDC.n)
    return c


def emit_bcast_row(b, c, row_sb, dst_sb, pa, width=D):
    sT, sV = b.sem("t"), b.sem("v")
    for ob in range(width // 512):
        vprev = sV.n
        b.wait("tensor", sV, vprev)
        b.op("tensor", lambda e, ob=ob: e.matmul(
            pa[:], c["ones1"][:], row_sb[:, ob * 512:(ob + 1) * 512], start=True, stop=True), sT)
        b.wait("vector", sT, sT.n)
        b.op("vector", lambda e, ob=ob: e.tensor_copy(dst_sb[:, ob * 512:(ob + 1) * 512], pa[:]), sV)


def emit_ln_stats(b, c, x_ap, bn6, mv, sc, nb, double_ln):
    sV, sS = b.sem("v"), b.sem("s")
    for i in range(4):
        b.op("vector", lambda e, i=i: e.bn_stats(bn6[:, i * 6:(i + 1) * 6], x_ap[:, i * 512:(i + 1) * 512]))
    b.drain("vector")
    vs = b.op("vector", lambda e: e.bn_aggr(mv[:], bn6[:]), sV)
    b.wait("scalar", sV, vs)
    ss = b.op("scalar", lambda e: e.activation(sc[:, 2:3], mv[:, 1:2], AF.Sqrt, bias=c["epsc"][:], scale=1.0), sS)
    b.wait("vector", sS, ss)
    b.op("vector", lambda e: e.reciprocal(sc[:, 3:4], sc[:, 2:3]))
    b.drain("vector")
    if double_ln:
        b.op("vector", lambda e: e.tensor_scalar(sc[:, 4:5], sc[:, 3:4], sc[:, 3:4], None, OP.mult))
        b.drain("vector")
        vv = b.op("vector", lambda e: e.tensor_scalar(sc[:, 5:6], mv[:, 1:2], sc[:, 4:5], None, OP.mult), sV)
        b.wait("scalar", sV, vv)
        ss = b.op("scalar", lambda e: e.activation(sc[:, 6:7], sc[:, 5:6], AF.Sqrt, bias=c["epsc"][:], scale=1.0), sS)
        b.wait("vector", sS, ss)
        b.op("vector", lambda e: e.reciprocal(sc[:, 7:8], sc[:, 6:7]))
        b.drain("vector")
        b.op("vector", lambda e: e.tensor_scalar(sc[:, 0:1], sc[:, 3:4], sc[:, 7:8], None, OP.mult))
    else:
        b.op("vector", lambda e: e.tensor_copy(sc[:, 0:1], sc[:, 3:4]))
    b.drain("vector")
    vv = b.op("vector", lambda e: e.scalar_tensor_tensor(nb[:], mv[:, 0:1], -1.0, sc[:, 0:1], OP.mult, OP.mult), sV)
    return vv


def emit_p2_l0ffn(nc, es, A, xin, xout, tag="p2", dbg=None, ntiles=NT, sems=None, sems_es=None):
    b = Bld(nc, es, tag, sems=sems, sems_es=sems_es)
    sV, sS, sT, sG = b.sem("v"), b.sem("s"), b.sem("t"), b.sem("g")
    sDC, sDW, sDX, sDO = b.sem("dc"), b.sem("dw"), b.sem("dx"), b.sem("do")

    c = emit_consts(b, A)
    gam = b.sb("gam", [128, 16], F32)
    bet = b.sb("bet", [128, 16], F32)
    brow = b.sb("brow", [1, D], F32)
    bias_b = b.sb("bias_b", [128, D], F32)
    wq = b.sb("wq", [128, NC * D], BF16)
    pa = b.ps("pa", [128, 512], F32)
    pb = b.ps("pb", [128, 512], F32)
    pta = b.ps("pta", [128, 512], BF16)
    ptb = b.ps("ptb", [128, 512], BF16)

    b.dma("sync", gam[:], A["l0f_gam"], sDC)
    b.dma("sync", bet[:], A["l0f_bet"], sDC)
    b.dma("sync", brow[:], A["l0f_bias"], sDC)
    b.wait("sync", sDC, sDC.n)
    b.flush()

    emit_bcast_row(b, c, brow, bias_b, pa)
    b.flush()

    with ExitStack() as wes:
        stg = [b.sb(f"stg{i}", [128, D], F32, wes) for i in range(2)]
        acc = b.sb("acc", [128, 16], F32, wes)
        dump = b.sb("dump", [128, D], F32, wes)
        small = b.sb("small", [1, 4], F32, wes)
        rinv = b.sb("rinv", [128, 1], F32, wes)
        tmp_col = b.sb("tmpcol", [128, 1], F32, wes)
        t1 = b.sb("t1", [128, D], BF16, wes)
        t2 = b.sb("t2", [128, D], BF16, wes)
        for cc in range(NC):
            if cc >= 2:
                b.wait("sync", sS, sS.n)
            dv = b.dma("sync", stg[cc % 2][:], A["l0f_w"][cc * 128:(cc + 1) * 128, :], sDW)
            b.wait("scalar", sDW, dv)
            b.op("scalar", lambda e, cc=cc: e.activation(
                dump[:], stg[cc % 2][:], AF.Abs, accum_out=acc[:, cc:cc + 1]), sS)
        b.flush()
        b.op("vector", lambda e: e.reduce_sum(tmp_col[:], acc[:], axis=AX))
        b.drain("vector")
        vv = b.op("vector", lambda e: e.engine_nop(), sV)
        b.wait("tensor", sV, vv)
        tv = b.op("tensor", lambda e: e.transpose(pa[0:1, 0:128], tmp_col[:], c["idf"][:]), sT)
        b.wait("vector", sT, tv)
        b.op("vector", lambda e: e.reduce_sum(small[:, 0:1], pa[0:1, 0:128], axis=AX))
        b.drain("vector")
        b.op("vector", lambda e: e.tensor_scalar(small[:, 1:2], small[:, 0:1], 1.0 / (D * D), EPS, OP.mult, OP.add))
        b.drain("vector")
        vv = b.op("vector", lambda e: e.reciprocal(small[:, 2:3], small[:, 1:2]), sV)
        b.wait("tensor", sV, vv)
        tv = b.op("tensor", lambda e: e.matmul(pb[:, 0:1], c["ones1"][:], small[:, 2:3], start=True, stop=True), sT)
        b.wait("vector", sT, tv)
        b.op("vector", lambda e: e.tensor_copy(rinv[:], pb[:, 0:1]))
        b.drain("vector")
        b.flush()
        for cc in range(NC):
            if cc >= 2:
                b.wait("sync", sV, sV.n)
            dv = b.dma("sync", stg[cc % 2][:], A["l0f_w"][cc * 128:(cc + 1) * 128, :], sDW)
            b.wait("vector", sDW, dv)
            b.op("vector", lambda e, cc=cc: e.tensor_scalar(
                t1[:], stg[cc % 2][:], rinv[:], 0.5, OP.mult, OP.is_gt))
            b.drain("vector")
            b.op("vector", lambda e, cc=cc: e.tensor_scalar(
                t2[:], stg[cc % 2][:], rinv[:], -0.5, OP.mult, OP.is_lt))
            b.drain("vector")
            b.op("vector", lambda e, cc=cc: e.tensor_tensor(
                wq[:, cc * D:(cc + 1) * D], t1[:], t2[:], OP.subtract), sV)
        b.flush()
        if dbg is not None:
            b.dma("scalar", dbg["rinv"], rinv[:], sDO)
            b.dma("scalar", dbg["wq0"], wq[:, 0:D], sDO)
            b.wait("scalar", sDO, sDO.n)
            b.flush()

    xt = [b.sb(f"xt{i}", [128, D], F32) for i in range(2)]
    ub = [b.sb(f"u{i}", [128, D], BF16) for i in range(2)]
    xnt = [b.sb(f"xnt{i}", [128, D], BF16) for i in range(2)]
    xb = [b.sb(f"xb{i}", [128, D], F32) for i in range(2)]
    x2t = [b.sb(f"x2t{i}", [128, D], F32) for i in range(2)]
    bn6 = b.sb("bn6", [128, 24], F32)
    mv = b.sb("mv", [128, 2], F32)
    scc = b.sb("scc", [128, 8], F32)
    nb = b.sb("nb", [128, 1], F32)
    xt_free = [(0, 0, 0), (0, 0, 0)]
    u_free = [0, 0]
    xnt_free = [0, 0]
    x2_free = [0, 0]
    xb_free = [0, 0]
    pmm = [pa, pb]
    pmm_free = [0, 0]
    ptr = [pta, ptb]
    ptr_free = [0, 0]
    last_uv = 0

    for tt in range(ntiles):
        ib = tt % 2
        b.wait("sync", sG, xt_free[ib][0])
        b.wait("sync", sV, xt_free[ib][1])
        b.wait("sync", sS, xt_free[ib][2])
        dv = b.dma("sync", xt[ib][:], xin[tt * 128:(tt + 1) * 128, :], sDX)
        b.wait("vector", sDX, dv)
        b.wait("vector", sS, last_uv)
        coeff_v = emit_ln_stats(b, c, xt[ib], bn6, mv, scc, nb, double_ln=True)
        b.wait("scalar", sDX, dv)
        b.wait("scalar", sV, coeff_v)
        b.wait("scalar", sT, u_free[ib])
        uv = b.op("scalar", lambda e, ib=ib: e.activation(
            ub[ib][:], xt[ib][:], AF.Identity, bias=nb[:], scale=scc[:, 0:1]), sS)
        last_uv = uv
        b.wait("gpsimd", sDX, dv)
        b.wait("gpsimd", sV, xb_free[ib])
        gv = b.op("gpsimd", lambda e, ib=ib: e.tensor_tensor(
            xb[ib][:], xt[ib][:], bias_b[:], OP.add), sG)
        xt_free[ib] = (gv, coeff_v, uv)
        b.wait("tensor", sS, uv)
        b.wait("vector", sT, xnt_free[ib])
        for grp in range(4):
            pg = grp % 2
            b.wait("tensor", sV, ptr_free[pg])
            for j in range(4):
                cc = grp * 4 + j
                tv = b.op("tensor", lambda e, cc=cc, pg=pg, j=j, ib=ib: e.transpose(
                    ptr[pg][:, j * 128:(j + 1) * 128], ub[ib][:, cc * 128:(cc + 1) * 128], c["idb"][:]),
                    sT if j == 3 else None)
            if grp == 3:
                u_free[ib] = sT.n
            b.wait("vector", sT, sT.n)
            for j in range(4):
                cc = grp * 4 + j
                ev = b.op("vector", lambda e, cc=cc, pg=pg, j=j, ib=ib: e.tensor_scalar(
                    xnt[ib][:, cc * 128:(cc + 1) * 128], ptr[pg][:, j * 128:(j + 1) * 128],
                    gam[:, cc:cc + 1], bet[:, cc:cc + 1], OP.mult, OP.add), sV)
            ptr_free[pg] = sV.n
        xnt_ready = sV.n
        b.wait("tensor", sV, xnt_ready)
        for ob in range(4):
            pg = ob % 2
            b.wait("tensor", sV, pmm_free[pg])
            for cc in range(NC):
                mmv = b.op("tensor", lambda e, cc=cc, ob=ob, pg=pg, ib=ib: e.matmul(
                    pmm[pg][:], xnt[ib][:, cc * 128:(cc + 1) * 128],
                    wq[:, cc * D + ob * 512: cc * D + (ob + 1) * 512],
                    start=(cc == 0), stop=(cc == NC - 1)), sT if cc == NC - 1 else None)
            b.wait("vector", sT, mmv)
            b.wait("vector", sG, xt_free[ib][0])
            b.wait("vector", sDO, x2_free[ib])
            pmm_free[pg] = b.op("vector", lambda e, ob=ob, pg=pg, ib=ib: e.tensor_tensor(
                x2t[ib][:, ob * 512:(ob + 1) * 512], pmm[pg][:], xb[ib][:, ob * 512:(ob + 1) * 512], OP.add), sV)
        xb_free[ib] = sV.n
        xnt_free[ib] = sT.n
        b.wait("scalar", sV, sV.n)
        x2_free[ib] = b.dma("scalar", xout[tt * 128:(tt + 1) * 128, :], x2t[ib][:], sDO)
        if dbg is not None and tt == 0:
            b.dma("scalar", dbg["u0"], ub[0][:], sDO)
            b.dma("scalar", dbg["xnt0"], xnt[0][:], sDO)
            b.dma("scalar", dbg["biasb"], bias_b[:], sDO)
            b.dma("scalar", dbg["coef"], scc[:], sDO)
            b.dma("scalar", dbg["nb"], nb[:], sDO)
    b.wait("scalar", sDO, sDO.n)
    b.flush()


def emit_ln_phase(b, c, xin_ap, col_of, xnt_sb, gamcol, betcol, double_ln, tiles, les, ptpair):
    sV, sS, sT, sDX = b.sem("v"), b.sem("s"), b.sem("t"), b.sem("dx")
    sS2 = b.sem("v2")
    u_ = b.fresh()
    xt = [b.sb(f"lnxt{i}_{u_}", [128, D], F32, les) for i in range(2)]
    ub = [b.sb(f"lnu{i}_{u_}", [128, D], BF16, les) for i in range(2)]
    bn6 = b.sb(f"lnbn6_{u_}", [128, 24], F32, les)
    mv = b.sb(f"lnmv_{u_}", [128, 2], F32, les)
    scc = b.sb(f"lnscc_{u_}", [128, 8], F32, les)
    nb = b.sb(f"lnnb_{u_}", [128, 1], F32, les)
    ptr = ptpair
    ptr_free = [0, 0]
    xt_free = [(0, 0), (0, 0)]
    u_free = [0, 0]
    last_uv = 0
    for ti, tt in enumerate(tiles):
        ib = ti % 2
        b.wait("sync", sV, xt_free[ib][0])
        b.wait("sync", sS, xt_free[ib][1])
        dv = b.dma("sync", xt[ib][:], xin_ap[tt * 128:(tt + 1) * 128, :], sDX)
        b.wait("vector", sDX, dv)
        b.wait("vector", sS, last_uv)
        coeff_v = emit_ln_stats(b, c, xt[ib], bn6, mv, scc, nb, double_ln=double_ln)
        b.wait("scalar", sDX, dv)
        b.wait("scalar", sV, coeff_v)
        b.wait("scalar", sT, u_free[ib])
        uv = b.op("scalar", lambda e, ib=ib: e.activation(
            ub[ib][:], xt[ib][:], AF.Identity, bias=nb[:], scale=scc[:, 0:1]), sS)
        last_uv = uv
        xt_free[ib] = (coeff_v, uv)
        b.wait("tensor", sS, uv)
        for grp in range(4):
            pg = grp % 2
            b.wait("tensor", sS2, ptr_free[pg])
            for j in range(4):
                cc = grp * 4 + j
                b.op("tensor", lambda e, cc=cc, pg=pg, j=j, ib=ib: e.transpose(
                    ptr[pg][:, j * 128:(j + 1) * 128], ub[ib][:, cc * 128:(cc + 1) * 128], c["idb"][:]),
                    sT if j == 3 else None)
            if grp == 3:
                u_free[ib] = sT.n
            b.wait("vector", sT, sT.n)
            for j in range(4):
                cc = grp * 4 + j
                if gamcol is not None:
                    b.op("vector", lambda e, cc=cc, pg=pg, j=j, tt=tt: e.tensor_scalar(
                        xnt_sb[:, col_of(cc, tt):col_of(cc, tt) + 128], ptr[pg][:, j * 128:(j + 1) * 128],
                        gamcol(cc), betcol(cc), OP.mult, OP.add), sS2)
                else:
                    b.op("vector", lambda e, cc=cc, pg=pg, j=j, tt=tt: e.tensor_copy(
                        xnt_sb[:, col_of(cc, tt):col_of(cc, tt) + 128],
                        ptr[pg][:, j * 128:(j + 1) * 128]), sS2)
            ptr_free[pg] = sS2.n
    return sS2.n


def emit_head_bound(b, c, st, qh, kh, sqpair, nmx, mcol, bufs, ready):
    sV, sS, sT = b.sem("v"), b.sem("s"), b.sem("t")
    psr, pav, pbc, pbb = bufs["psr"], bufs["pav"], bufs["pbc"], bufs["pbb"]
    sqa, sqb_ = sqpair
    regions = [p[0:1, 0:512] for p in psr] + [pav[0:1, 0:512], pbc[0:1, 0:512], pbb[0:1, 0:512]]
    b.wait("scalar", sS, ready)
    b.wait("scalar", sT, st["nmm"])
    b.op("scalar", lambda e: e.activation(sqa[:], qh[:], AF.Square))
    sqv = b.op("scalar", lambda e: e.activation(sqb_[:], kh[:], AF.Square), sS)
    b.wait("tensor", sS, sqv)
    b.wait("tensor", sV, st["pav_ev"])
    b.wait("tensor", sV, st["lrow_ev"])
    b.wait("tensor", sV, st["linv_ev"])
    for bk in range(len(psr)):
        b.wait("tensor", sS, st["psfree"][bk])
    nreg = len(regions)
    done = 0
    rv = 0
    for i in range(8):
        sq_ = sqa if i < 4 else sqb_
        tb = i % 4
        reg = regions[i % nreg]
        if i >= nreg:
            b.wait("tensor", sV, rv)
        st["nmm"] = b.op("tensor", lambda e, sq_=sq_, tb=tb, reg=reg: e.matmul(
            reg, c["onescol"][:], sq_[:, tb * 512:(tb + 1) * 512],
            start=True, stop=True), sT)
        if i == min(nreg, 8) - 1 or i == 7:
            b.wait("vector", sT, st["nmm"])
            for j in range(done, i + 1):
                rv = b.op("vector", lambda e, j=j, reg=regions[j % nreg]: e.reduce_max(
                    nmx[:, j:j + 1], reg, axis=AX), sV)
            done = i + 1
    st["normv"] = rv
    b.wait("vector", sV, rv)
    b.op("vector", lambda e: e.reduce_max(nmx[:, 8:9], nmx[:, 0:4], axis=AX))
    b.drain("vector")
    b.op("vector", lambda e: e.reduce_max(nmx[:, 9:10], nmx[:, 4:8], axis=AX))
    b.drain("vector")
    b.op("vector", lambda e: e.tensor_tensor(nmx[:, 10:11], nmx[:, 8:9], nmx[:, 9:10], OP.mult))
    b.drain("vector")
    vv = b.op("vector", lambda e: e.engine_nop(), sV)
    b.wait("scalar", sV, vv)
    sv = b.op("scalar", lambda e: e.activation(nmx[:, 11:12], nmx[:, 10:11], AF.Sqrt), sS)
    b.wait("vector", sS, sv)
    vv = b.op("vector", lambda e: e.tensor_scalar(nmx[:, 12:13], nmx[:, 11:12], -ASCALE, None, OP.mult), sV)
    b.wait("tensor", sV, vv)
    tv = b.op("tensor", lambda e: e.matmul(pbb[:, 0:1], c["ones1"][:], nmx[:, 12:13],
                                           start=True, stop=True), sT)
    b.wait("vector", sT, tv)
    mv = b.op("vector", lambda e: e.tensor_copy(mcol[:], pbb[:, 0:1]), sV)
    st["linv_ev"] = max(st["linv_ev"], mv)
    return mv


def emit_attn_head(b, c, st, qh, kh, vtile, avt_slice, bufs, ready, mcol, mrdy):
    sV, sS, sT = b.sem("v"), b.sem("s"), b.sem("t")
    psr, pav, pbc, pbb = bufs["psr"], bufs["pav"], bufs["pbc"], bufs["pbb"]
    nb_ = len(psr)
    PT, mskT = bufs["PT"], bufs["mskT"]
    lrow, ltmp, invl, invlb = bufs["lrow"], bufs["ltmp"], bufs["invl"], bufs["invlb"]
    b.wait("tensor", sS, ready)
    first_exp = True
    for qb in range(4):
        nk = 4 * (qb + 1)
        mmv = {}
        for kk in range(nk + 2):
            if kk < nk:
                bk = kk % nb_
                b.wait("tensor", sS, st["psfree"][bk])
                if kk == 0:
                    b.wait("tensor", sV, st["lrow_ev"])
                    b.wait("tensor", sV, st["pav_ev"])
                    b.wait("tensor", sV, st["normv"])
                v = b.op("tensor", lambda e, bk=bk, kk=kk, qb=qb: e.matmul(
                    psr[bk][:], kh[:, kk * 128:(kk + 1) * 128], qh[:, qb * 512:(qb + 1) * 512],
                    start=True, stop=True), sT)
                mmv[kk] = v
                diag = kk >= 4 * qb
                if diag:
                    r = kk - 4 * qb
                    b.wait("vector", sT, v)
                    mmv[kk] = ("v", b.op("vector", lambda e, bk=bk, r=r: e.tensor_tensor(
                        psr[bk][:], psr[bk][:], mskT[:, (3 - r) * 128:(3 - r) * 128 + 512], OP.add), sV))
                if first_exp:
                    b.wait("scalar", sV, mrdy)
                    first_exp = False
                m = mmv[kk]
                if isinstance(m, tuple):
                    b.wait("scalar", sV, m[1])
                else:
                    b.wait("scalar", sT, m)
                b.wait("scalar", sT, st["PTmm"][kk % 3])
                st["psfree"][bk] = b.op("scalar", lambda e, bk=bk, kk=kk: e.activation(
                    PT[kk % 3][:], psr[bk][:], AF.Exp, bias=mcol[:], scale=ASCALE), sS)
                st["pexp"][kk % 3] = st["psfree"][bk]
            j = kk - 2
            if 0 <= j < nk:
                b.wait("tensor", sS, st["pexp"][j % 3])
                b.op("tensor", lambda e, j=j, nk=nk: e.matmul(
                    pav[:], vtile(j), PT[j % 3][:], start=(j == 0), stop=(j == nk - 1)))
                st["PTmm"][j % 3] = b.op("tensor", lambda e, j=j, nk=nk: e.matmul(
                    pbc[0:1, 0:512], c["onescol"][:], PT[j % 3][:],
                    start=(j == 0), stop=(j == nk - 1)), sT)
        lv = st["PTmm"][(nk - 1) % 3]
        b.wait("vector", sT, lv)
        st["lrow_ev"] = b.op("vector", lambda e: e.tensor_copy(lrow[:], pbc[0:1, 0:512]), sV)
        b.wait("scalar", sV, st["lrow_ev"])
        b.op("scalar", lambda e: e.activation(ltmp[:], lrow[:], AF.Ln))
        lx = b.op("scalar", lambda e: e.activation(invl[:], ltmp[:], AF.Exp, scale=-1.0), sS)
        b.wait("tensor", sS, lx)
        b.wait("tensor", sV, st["linv_ev"])
        bcv = b.op("tensor", lambda e: e.matmul(pbb[:], c["ones1"][:], invl[:],
                                                start=True, stop=True), sT)
        b.wait("vector", sT, bcv)
        st["linv_ev"] = b.op("vector", lambda e: e.tensor_copy(invlb[:], pbb[:]), sV)
        st["pav_ev"] = b.op("vector", lambda e, qb=qb: e.tensor_tensor(
            avt_slice(qb), pav[:], invlb[:], OP.mult), sV)


def emit_p1_l0attn(nc, es, A, xin, xout, tag="p1", ntiles=NT, sems=None, sems_es=None):
    b = Bld(nc, es, tag, sems=sems, sems_es=sems_es)
    sV, sS, sT, sG = b.sem("v"), b.sem("s"), b.sem("t"), b.sem("g")
    sDC, sDX, sDO = b.sem("dc"), b.sem("dx"), b.sem("do")
    sWA = b.sem("wa")

    c = emit_consts(b, A)
    lng = b.sb("lng", [128, 16], F32)
    lnb = b.sb("lnb", [128, 16], F32)
    bqc = b.sb("bqc", [128, 16], F32)
    bkc = b.sb("bkc", [128, 16], F32)
    bvb = b.sb("bvb", [128, 16], BF16)
    msk = b.sb("msk", [128, 896], F32)
    for dst, src_ in [(lng, A["l0_lng"]), (lnb, A["l0_lnb"]), (bqc, A["l0_bq"]),
                      (bkc, A["l0_bk"]), (msk, A["masksT"])]:
        b.dma("sync", dst[:], src_, sDC)
    b.dma("gpsimd", bvb[:], A["l0_bv"], sDC)
    b.wait("sync", sDC, sDC.n)
    b.flush()

    pbc = b.ps("pbc", [128, 512], F32)
    pav = b.ps("pav", [128, 512], F32)

    AVT = b.sb("avt", [128, NC * T], BF16)

    st = dict(psfree=[0] * 6, PTmm=[0, 0, 0], pexp=[0, 0, 0], pav_ev=0,
              lrow_ev=0, linv_ev=0, nmm=0, nred=0, normv=0, ptevac=[0, 0])

    with ExitStack() as xes:
        xnt = b.sb("xnt", [128, NC * T], BF16, xes)
        with ExitStack() as les:
            pta = b.ps("lnpta", [128, 512], BF16, les)
            ptb = b.ps("lnptb", [128, 512], BF16, les)
            xnt_done = emit_ln_phase(
                b, c, xin, lambda cc, tt: cc * T + tt * 128, xnt,
                lambda cc: lng[:, cc:cc + 1], lambda cc: lnb[:, cc:cc + 1],
                True, range(ntiles), les, (pta, ptb))
            b.flush()

        with ExitStack() as aes:
            psr = [b.ps(f"psr{i}", [128, 512], F32, aes) for i in range(5)]
            ps = psr
            pbb = b.ps("pbb", [128, 512], F32, aes)
            nmx = b.sb("nmx", [1, 16], F32, aes)
            mcol = b.sb("mcol", [128, 1], F32, aes)
            cmx = b.sb("cmx", [128, 2], BF16, aes)
            sqa = b.sb("sqa", [128, T], BF16, aes)
            sqb_ = b.sb("sqb", [128, T], BF16, aes)
            lrowb = b.sb("lrowb", [1, 1536], F32, aes)
            lrow, ltmp, invl = lrowb[:, 0:512], lrowb[:, 512:1024], lrowb[:, 1024:1536]
            invlb = b.sb("invlb", [128, 512], F32, aes)
            PT = [b.sb(f"PT{i}", [128, 512], BF16, aes) for i in range(3)]
            V_sb = b.sb("vsb", [128, 16 * 512], BF16, aes)
            qh = b.sb("qh", [128, T], BF16, aes)
            kh = b.sb("kh", [128, T], BF16, aes)
            wq_b = b.sb("wqb", [128, D], BF16, aes)
            wk_b = b.sb("wkb", [128, D], BF16, aes)
            bufs = dict(psr=psr, pbc=pbc, pav=pav, pbb=pbb, PT=PT, mskT=msk, cmx=cmx,
                        lrow=lrow, ltmp=ltmp, invl=invl, invlb=invlb)

            qk_done = 0
            for g in range(4):
                with ExitStack() as ges:
                    wvg = b.sb(f"wvg{g}", [128, 8192], BF16, ges)
                    dvv = b.dma("gpsimd", wvg[:], A["l0_wv"][g * 128:(g + 1) * 128, :], sWA)
                    b.wait("tensor", sWA, dvv)
                    b.wait("tensor", b.sem("v2"), xnt_done)
                    vev = [st["psfree"][0], st["psfree"][1]]
                    for tt in range(16):
                        pg = tt % 2
                        b.wait("tensor", sS, vev[pg])
                        mmv = None
                        for cc in range(NC):
                            mmv = b.op("tensor", lambda e, cc=cc, tt=tt, pg=pg, wvg=wvg: e.matmul(
                                ps[pg][:], xnt[:, cc * T + tt * 128: cc * T + (tt + 1) * 128],
                                wvg[:, cc * 512:(cc + 1) * 512],
                                start=(cc == 0), stop=(cc == NC - 1)), sT if cc == NC - 1 else None)
                        b.wait("scalar", sT, mmv)
                        vev[pg] = b.op("scalar", lambda e, tt=tt, pg=pg: e.activation(
                            V_sb[:, tt * 512:(tt + 1) * 512], ps[pg][:], AF.Copy), sS)
                    st["psfree"][0], st["psfree"][1] = vev
                    vprod_ev = sS.n
                    for hh in range(4):
                        h = g * 4 + hh
                        b.wait("gpsimd", sT, qk_done)
                        dq = b.dma("gpsimd", wq_b[:], A["l0_wq"][h * 128:(h + 1) * 128, :], sWA)
                        dk = b.dma("gpsimd", wk_b[:], A["l0_wk"][h * 128:(h + 1) * 128, :], sWA)
                        b.wait("tensor", sWA, dq)
                        kv = None
                        for (which, wbuf, dst, bias) in (("q", wq_b, qh, bqc), ("k", wk_b, kh, bkc)):
                            if which == "k":
                                b.wait("tensor", sWA, dk)
                            for tb in range(4):
                                pg = tb % 2
                                b.wait("tensor", sS, st["psfree"][pg])
                                mmv = None
                                for cc in range(NC):
                                    mmv = b.op("tensor", lambda e, cc=cc, tb=tb, pg=pg, wbuf=wbuf: e.matmul(
                                        ps[pg][:], wbuf[:, cc * 128:(cc + 1) * 128],
                                        xnt[:, cc * T + tb * 512: cc * T + tb * 512 + 512],
                                        start=(cc == 0), stop=(cc == NC - 1)), sT if cc == NC - 1 else None)
                                b.wait("scalar", sT, mmv)
                                st["psfree"][pg] = b.op("scalar", lambda e, tb=tb, pg=pg, dst=dst, bias=bias, h=h:
                                    e.activation(dst[:, tb * 512:(tb + 1) * 512], ps[pg][:], AF.Identity,
                                                 bias=bias[:, h:h + 1], scale=1.0), sS)
                            if which == "k":
                                kv = sS.n
                        qk_done = sT.n
                        mrdy = emit_head_bound(b, c, st, qh, kh, (sqa, sqb_), nmx, mcol, bufs, max(kv, vprod_ev))
                        emit_attn_head(
                            b, c, st, qh, kh,
                            lambda kk, hh=hh, V_sb=V_sb: V_sb[:, kk * 512 + hh * 128: kk * 512 + hh * 128 + 128],
                            lambda qb, h=h: AVT[:, h * T + qb * 512: h * T + (qb + 1) * 512],
                            bufs, max(kv, vprod_ev), mcol, mrdy)
                    b.flush()

    with ExitStack() as oes:
        ps = [b.ps(f"pso{i}", [128, 512], F32, oes) for i in range(2)]
        wo = b.sb("wo", [128, NC * D], BF16, oes)
        borow = b.sb("borow", [1, D], F32, oes)
        boeff = b.sb("boeff", [1, D], F32, oes)
        bob = b.sb("bob", [128, D], F32, oes)
        xt = [b.sb(f"oxt{i}", [128, D], F32, oes) for i in range(2)]
        xbb = [b.sb(f"oxb{i}", [128, D], F32, oes) for i in range(2)]
        x1t = [b.sb(f"ox1{i}", [128, D], F32, oes) for i in range(2)]
        b.dma("sync", borow[:], A["l0_bo"], sDC)
        b.wait("sync", sDC, sDC.n)
        wov = []
        for cc in range(NC):
            wov.append(b.dma("gpsimd", wo[:, cc * D:(cc + 1) * D],
                             A["l0_wo"][cc * 128:(cc + 1) * 128, :], sWA))
        b.flush()
        for ob in range(4):
            mmv = None
            for cc in range(NC):
                if ob == 0:
                    b.wait("tensor", sWA, wov[cc])
                mmv = b.op("tensor", lambda e, cc=cc, ob=ob: e.matmul(
                    pbc[0:1, 0:512], bvb[:, cc:cc + 1], wo[:, cc * D + ob * 512: cc * D + (ob + 1) * 512],
                    start=(cc == 0), stop=(cc == NC - 1)), sT if cc == NC - 1 else None)
            b.wait("vector", sT, mmv)
            ev = b.op("vector", lambda e, ob=ob: e.tensor_tensor(
                boeff[:, ob * 512:(ob + 1) * 512], pbc[0:1, 0:512], borow[:, ob * 512:(ob + 1) * 512],
                OP.add), sV)
            b.wait("tensor", sV, ev)
        b.flush()
        emit_bcast_row(b, c, boeff, bob, pbc)
        b.flush()
        xt_free = [0, 0]
        x1_free = [0, 0]
        xb_free = [0, 0]
        pmm_free = [0, 0]
        for tt in range(ntiles):
            ib = tt % 2
            b.wait("sync", sG, xt_free[ib])
            dv = b.dma("sync", xt[ib][:], xin[tt * 128:(tt + 1) * 128, :], sDX)
            b.wait("gpsimd", sDX, dv)
            b.wait("gpsimd", sV, xb_free[ib])
            gv = b.op("gpsimd", lambda e, ib=ib: e.tensor_tensor(
                xbb[ib][:], xt[ib][:], bob[:], OP.add), sG)
            xt_free[ib] = gv
            for ob in range(4):
                pg = ob % 2
                b.wait("tensor", sV, pmm_free[pg])
                mmv = None
                for cc in range(NC):
                    mmv = b.op("tensor", lambda e, cc=cc, tt=tt, ob=ob, pg=pg: e.matmul(
                        ps[pg][:], AVT[:, cc * T + tt * 128: cc * T + (tt + 1) * 128],
                        wo[:, cc * D + ob * 512: cc * D + (ob + 1) * 512],
                        start=(cc == 0), stop=(cc == NC - 1)), sT if cc == NC - 1 else None)
                b.wait("vector", sT, mmv)
                b.wait("vector", sG, gv)
                b.wait("vector", sDO, x1_free[ib])
                pmm_free[pg] = b.op("vector", lambda e, ob=ob, pg=pg, ib=ib: e.tensor_tensor(
                    x1t[ib][:, ob * 512:(ob + 1) * 512], ps[pg][:], xbb[ib][:, ob * 512:(ob + 1) * 512],
                    OP.add), sV)
            xb_free[ib] = sV.n
            b.wait("scalar", sV, sV.n)
            x1_free[ib] = b.dma("scalar", xout[tt * 128:(tt + 1) * 128, :], x1t[ib][:], sDO)
        b.wait("scalar", sDO, sDO.n)
        b.flush()


def emit_p3_l1attn(nc, es, A, xin, xout, tag="p3", ntiles=NT, sems=None, sems_es=None):
    b = Bld(nc, es, tag, sems=sems, sems_es=sems_es)
    sV, sS, sT, sG = b.sem("v"), b.sem("s"), b.sem("t"), b.sem("g")
    sDC, sDX, sDO = b.sem("dc"), b.sem("dx"), b.sem("do")
    sWA = b.sem("wa")

    c = emit_consts(b, A)
    msk = b.sb("msk", [128, 896], F32)
    b.dma("sync", msk[:], A["masksT"], sDC)
    b.wait("sync", sDC, sDC.n)
    b.flush()

    pbc = b.ps("pbc", [128, 512], F32)
    pav = b.ps("pav", [128, 512], F32)

    AVT = b.sb("avt", [128, NC * T], BF16)

    st = dict(psfree=[0] * 4, PTmm=[0, 0, 0], pexp=[0, 0, 0], pav_ev=0,
              lrow_ev=0, linv_ev=0, nmm=0, nred=0, normv=0, ptevac=[0, 0])

    with ExitStack() as aes:
        psr = [b.ps(f"psr{i}", [128, 512], F32, aes) for i in range(3)]
        ps = psr
        pbb = b.ps("pbb", [128, 512], F32, aes)
        pta = b.ps("pta", [128, 512], BF16, aes)
        ptb = b.ps("ptb", [128, 512], BF16, aes)
        nmx = b.sb("nmx", [1, 16], F32, aes)
        mcol = b.sb("mcol", [128, 1], F32, aes)
        cmx = b.sb("cmx", [128, 2], BF16, aes)
        sqa = b.sb("sqa", [128, T], BF16, aes)
        sqb_ = b.sb("sqb", [128, T], BF16, aes)
        lrowb = b.sb("lrowb", [1, 1536], F32, aes)
        lrow, ltmp, invl = lrowb[:, 0:512], lrowb[:, 512:1024], lrowb[:, 1024:1536]
        invlb = b.sb("invlb", [128, 512], F32, aes)
        PT = [b.sb(f"PT{i}", [128, 512], BF16, aes) for i in range(3)]
        bufs = dict(psr=psr, pbc=pbc, pav=pav, pbb=pbb, PT=PT, mskT=msk, cmx=cmx,
                    lrow=lrow, ltmp=ltmp, invl=invl, invlb=invlb)
        with ExitStack() as xes:
            xnt = b.sb("xnt", [128, NC * T], BF16, xes)
            with ExitStack() as les:
                xnt_done = emit_ln_phase(
                    b, c, xin, lambda cc, tt: cc * T + tt * 128, xnt,
                    None, None, False, range(ntiles), les, (pta, ptb))
                b.flush()

            wdone = 0
            for g in range(4):
                with ExitStack() as ges:
                    Qg = b.sb(f"qg{g}", [128, 4 * T], BF16, ges)
                    Kg = b.sb(f"kg{g}", [128, T], BF16, ges)
                    Vt = b.sb(f"vt{g}", [128, T], BF16, ges)
                    Vg = b.sb(f"vg{g}", [128, T], BF16, ges)
                    wb = [b.sb(f"wb{g}{i}", [128, D], BF16, ges) for i in range(2)]
                    vtv = 0
                    for r in range(6):
                        pc = 6 * g + r
                        ib = r % 2
                        b.wait("gpsimd", sT, wdone if r < 2 else wdone)
                        dq = b.dma("gpsimd", wb[ib][:], A["l1_wqkv"][pc * 128:(pc + 1) * 128, :], sWA)
                        b.wait("tensor", sWA, dq)
                        dst = (Qg[:, r * T:(r + 1) * T] if r < 4 else (Kg[:] if r == 4 else Vt[:]))
                        for tb in range(4):
                            pg = tb % 2
                            b.wait("tensor", sS, st["psfree"][pg])
                            mmv = None
                            for cc in range(NC):
                                mmv = b.op("tensor", lambda e, cc=cc, tb=tb, pg=pg, ib=ib: e.matmul(
                                    ps[pg][:], wb[ib][:, cc * 128:(cc + 1) * 128],
                                    xnt[:, cc * T + tb * 512: cc * T + tb * 512 + 512],
                                    start=(cc == 0), stop=(cc == NC - 1)), sT if cc == NC - 1 else None)
                            wdone = mmv
                            b.wait("scalar", sT, mmv)
                            st["psfree"][pg] = b.op("scalar", lambda e, tb=tb, pg=pg, dst=dst: e.activation(
                                dst[:, tb * 512:(tb + 1) * 512], ps[pg][:], AF.Copy), sS)
                        if r == 5:
                            vtv = sS.n
                    b.wait("tensor", sS, vtv)
                    ptp = [pta, ptb]
                    for g4 in range(4):
                        pg = g4 % 2
                        b.wait("tensor", sS, st["ptevac"][pg])
                        for j in range(4):
                            kk = g4 * 4 + j
                            b.op("tensor", lambda e, kk=kk, j=j, pg=pg: e.transpose(
                                ptp[pg][:, j * 128:(j + 1) * 128], Vt[:, kk * 128:(kk + 1) * 128],
                                c["idb"][:]), sT if j == 3 else None)
                        b.wait("scalar", sT, sT.n)
                        st["ptevac"][pg] = b.op("scalar", lambda e, g4=g4, pg=pg: e.activation(
                            Vg[:, g4 * 512:(g4 + 1) * 512], ptp[pg][:], AF.Copy), sS)
                    vgv = sS.n
                    for j in range(4):
                        h = g * 4 + j
                        mrdy = emit_head_bound(b, c, st, Qg[:, j * T:(j + 1) * T], Kg,
                                               (sqa, sqb_), nmx, mcol, bufs, vgv)
                        emit_attn_head(
                            b, c, st, Qg[:, j * T:(j + 1) * T], Kg,
                            lambda kk, Vg=Vg: Vg[:, kk * 128:(kk + 1) * 128],
                            lambda qb, h=h: AVT[:, h * T + qb * 512: h * T + (qb + 1) * 512],
                            bufs, vgv, mcol, mrdy)
                    b.flush()

    with ExitStack() as oes:
        ps = [b.ps(f"pso{i}", [128, 512], F32, oes) for i in range(2)]
        wo = b.sb("wo", [128, NC * D], BF16, oes)
        xt = [b.sb(f"oxt{i}", [128, D], F32, oes) for i in range(2)]
        x1t = [b.sb(f"ox1{i}", [128, D], F32, oes) for i in range(2)]
        wov = []
        for cc in range(NC):
            wov.append(b.dma("gpsimd", wo[:, cc * D:(cc + 1) * D],
                             A["l1_wproj"][cc * 128:(cc + 1) * 128, :], sWA))
        b.flush()
        xt_free = [0, 0]
        x1_free = [0, 0]
        pmm_free = [0, 0]
        for tt in range(ntiles):
            ib = tt % 2
            b.wait("sync", sV, xt_free[ib])
            dv = b.dma("sync", xt[ib][:], xin[tt * 128:(tt + 1) * 128, :], sDX)
            for ob in range(4):
                pg = ob % 2
                b.wait("tensor", sV, pmm_free[pg])
                mmv = None
                for cc in range(NC):
                    if tt == 0 and ob == 0:
                        b.wait("tensor", sWA, wov[cc])
                    mmv = b.op("tensor", lambda e, cc=cc, tt=tt, ob=ob, pg=pg: e.matmul(
                        ps[pg][:], AVT[:, cc * T + tt * 128: cc * T + (tt + 1) * 128],
                        wo[:, cc * D + ob * 512: cc * D + (ob + 1) * 512],
                        start=(cc == 0), stop=(cc == NC - 1)), sT if cc == NC - 1 else None)
                b.wait("vector", sT, mmv)
                b.wait("vector", sDX, dv)
                b.wait("vector", sDO, x1_free[ib])
                pmm_free[pg] = b.op("vector", lambda e, ob=ob, pg=pg, ib=ib: e.tensor_tensor(
                    x1t[ib][:, ob * 512:(ob + 1) * 512], ps[pg][:], xt[ib][:, ob * 512:(ob + 1) * 512],
                    OP.add), sV)
            xt_free[ib] = sV.n
            b.wait("scalar", sV, sV.n)
            x1_free[ib] = b.dma("scalar", xout[tt * 128:(tt + 1) * 128, :], x1t[ib][:], sDO)
        b.wait("scalar", sDO, sDO.n)
        b.flush()


def emit_p4_l1ffn(nc, es, A, xin, xout, ddram, tag="p4", sems=None, sems_es=None):
    b = Bld(nc, es, tag, sems=sems, sems_es=sems_es)
    sV, sS, sT, sG = b.sem("v"), b.sem("s"), b.sem("t"), b.sem("g")
    sDC, sDX, sDO, sWA, sWB = b.sem("dc"), b.sem("dx"), b.sem("do"), b.sem("wa"), b.sem("wb")
    TH = 1024

    c = emit_consts(b, A)
    sgc = b.sb("sgc", [128, 44], F32)
    suc = b.sb("suc", [128, 44], F32)
    sdrow = b.sb("sdrow", [1, D], F32)
    sdb = b.sb("sdb", [128, D], F32)
    onesc = b.sb("onesc", [128, 1], BF16)
    b.dma("sync", sgc[:], A["l1_sg"], sDC)
    b.dma("sync", suc[:], A["l1_su"], sDC)
    b.dma("sync", sdrow[:], A["l1_sd"], sDC)
    b.op("vector", lambda e: e.memset(onesc[:], 1.0))
    b.wait("sync", sDC, sDC.n)
    b.flush()

    pta = b.ps("pta", [128, 512], BF16)
    ptb = b.ps("ptb", [128, 512], BF16)
    ps = [b.ps(f"ps{i}", [128, 512], F32) for i in range(4)]
    pgu = [b.ps(f"pgu{i}", [1, 512], F32) for i in range(2)]

    emit_bcast_row(b, c, sdrow, sdb, ps[0])
    b.flush()

    for half in range(2):
        off = half * 8
        with ExitStack() as hes:
            gu = b.sb(f"gu{half}", [128, 44 * TH], BF16, hes)
            gurow = b.sb(f"gurow{half}", [1, TH], F32, hes)
            m128g = b.sb(f"m128g{half}", [128, 8], F32, hes)
            with ExitStack() as xes:
                xnt = b.sb(f"xnt{half}", [128, NC * TH], BF16, xes)
                with ExitStack() as les:
                    emit_ln_phase(
                        b, c, xin, lambda cc, tt: cc * TH + (tt - off) * 128, xnt,
                        None, None, False, range(off, off + 8), les, (pta, ptb))
                    b.flush()
                with ExitStack() as ges:
                    wstg = [b.sb(f"wsg{half}{i}", [128, D], BF16, ges) for i in range(2)]
                    wstu = [b.sb(f"wsu{half}{i}", [128, D], BF16, ges) for i in range(2)]
                    wgb = [b.sb(f"wgb{half}{i}", [128, D], BF16, ges) for i in range(2)]
                    wub = [b.sb(f"wub{half}{i}", [128, D], BF16, ges) for i in range(2)]
                    grow = b.sb(f"grow{half}", [128, TH], BF16, ges)
                    urow = b.sb(f"urow{half}", [128, TH], BF16, ges)
                    stg_free = [(0, 0), (0, 0)]
                    wgb_free = [0, 0]
                    ev_free = [0, 0, 0, 0]
                    psfree = [0, 0, 0, 0]
                    growv = [0, 0]
                    for hc in range(NHC):
                        ib = hc % 2
                        b.wait("gpsimd", sV, stg_free[ib][0])
                        dg = b.dma("gpsimd", wstg[ib][:], A["l1_wg"][hc * 128:(hc + 1) * 128, :], sWA)
                        du = b.dma("gpsimd", wstu[ib][:], A["l1_wu"][hc * 128:(hc + 1) * 128, :], sWA)
                        b.wait("vector", sWA, dg)
                        b.wait("vector", sT, wgb_free[ib])
                        b.op("vector", lambda e, ib=ib: e.tensor_scalar(
                            wgb[ib][:], wstg[ib][:], 128.0, None, OP.subtract))
                        b.drain("vector")
                        b.wait("vector", sWA, du)
                        uq = b.op("vector", lambda e, ib=ib: e.tensor_scalar(
                            wub[ib][:], wstu[ib][:], 128.0, None, OP.subtract), sV)
                        stg_free[ib] = (uq, uq)
                        b.wait("tensor", sV, uq)
                        for which, base, wsrc, scol, func in (
                                ("g", 0, wgb[ib], sgc, AF.Silu), ("u", 2, wub[ib], suc, AF.Identity)):
                            for tb in range(2):
                                pg = base + tb
                                b.wait("tensor", sS, psfree[pg])
                                mmv = None
                                for cc in range(NC):
                                    mmv = b.op("tensor", lambda e, cc=cc, tb=tb, pg=pg, wsrc=wsrc: e.matmul(
                                        ps[pg][:], wsrc[:, cc * 128:(cc + 1) * 128],
                                        xnt[:, cc * TH + tb * 512: cc * TH + tb * 512 + 512],
                                        start=(cc == 0), stop=(cc == NC - 1)), sT if cc == NC - 1 else None)
                                b.wait("scalar", sT, mmv)
                                dst = grow if which == "g" else urow
                                b.wait("scalar", sV, growv[tb])
                                psfree[pg] = b.op("scalar", lambda e, tb=tb, pg=pg, dst=dst, scol=scol, hc=hc, func=func:
                                    e.activation(dst[:, tb * 512:(tb + 1) * 512], ps[pg][:], func,
                                                 bias=0.0, scale=scol[:, hc:hc + 1]), sS)
                        wgb_free[ib] = sT.n
                        for tb in range(2):
                            b.wait("vector", sS, psfree[2 + tb])
                            growv[tb] = b.op("vector", lambda e, hc=hc, tb=tb: e.tensor_tensor(
                                gu[:, hc * TH + tb * 512: hc * TH + tb * 512 + 512],
                                grow[:, tb * 512:(tb + 1) * 512], urow[:, tb * 512:(tb + 1) * 512],
                                OP.mult), sV)
                            b.wait("tensor", sV, growv[tb])
                            b.op("tensor", lambda e, hc=hc, tb=tb: e.matmul(
                                pgu[tb][:], onesc[:], gu[:, hc * TH + tb * 512: hc * TH + tb * 512 + 512],
                                start=(hc == 0), stop=(hc == NHC - 1)), sT if hc == NHC - 1 else None)
                    gusum_mm = sT.n
                    b.flush()
            b.wait("vector", sT, gusum_mm)
            for tb in range(2):
                b.op("vector", lambda e, tb=tb: e.tensor_copy(gurow[:, tb * 512:(tb + 1) * 512], pgu[tb][:]))
            b.drain("vector")
            grv = b.op("vector", lambda e: e.engine_nop(), sV)
            b.wait("tensor", sV, grv)
            for j in range(8):
                b.op("tensor", lambda e, j=j: e.matmul(
                    ps[0][:, j:j + 1], gurow[:, j * 128:(j + 1) * 128], c["ones1"][0:1, 0:1],
                    start=True, stop=True), sT if j == 7 else None)
            gtv = sT.n
            b.wait("vector", sT, gtv)
            b.op("vector", lambda e: e.tensor_scalar(m128g[:], ps[0][:, 0:8], -128.0, None, OP.mult))
            b.drain("vector")
            m128v = b.op("vector", lambda e: e.engine_nop(), sV)
            b.flush()

            with ExitStack() as des:
                wdb = [b.sb(f"wdb{half}{i}", [128, 44 * 512], BF16, des) for i in range(2)]
                tmp = [b.sb(f"dtmp{half}{i}", [128, 512], F32, des) for i in range(2)]
                wd_used = [0, 0]
                tmp_free = [0, 0]
                dmad = []
                for db in range(4):
                    ibw = db % 2
                    b.wait("gpsimd", sT, wd_used[ibw])
                    dws = []
                    for q in range(4):
                        lo = q * 11
                        dws.append(b.dma(
                            "gpsimd",
                            wdb[ibw][:].rearrange("p (hc j) -> p hc j", j=512)[:, lo:lo + 11, :],
                            A["l1_wd"][db * 5632 + lo * 128: db * 5632 + (lo + 11) * 128, :]
                            .rearrange("(hc p) j -> p hc j", p=128),
                            sWB))
                    for tt in range(off, off + 8):
                        ti = tt - off
                        pg = 2 + ti % 2
                        b.wait("tensor", sWB, dws[-1])
                        if len(dmad) >= 2:
                            b.wait("tensor", sV, dmad[-2][1])
                        b.wait("tensor", sV, grv)
                        mmv = None
                        for hc in range(NHC):
                            mmv = b.op("tensor", lambda e, hc=hc, ti=ti, pg=pg, ibw=ibw: e.matmul(
                                ps[pg][:], gu[:, hc * TH + ti * 128: hc * TH + (ti + 1) * 128],
                                wdb[ibw][:, hc * 512:(hc + 1) * 512],
                                start=(hc == 0), stop=(hc == NHC - 1)), sT if hc == NHC - 1 else None)
                        wd_used[ibw] = mmv
                        ibt = ti % 2
                        b.wait("vector", sT, mmv)
                        b.wait("vector", sV, m128v)
                        b.wait("vector", sDO, tmp_free[ibt])
                        vv = b.op("vector", lambda e, ti=ti, pg=pg, ibt=ibt, db=db: e.scalar_tensor_tensor(
                            tmp[ibt][:], ps[pg][:], m128g[:, ti:ti + 1], sdb[:, db * 512:(db + 1) * 512],
                            OP.add, OP.mult), sV)
                        dmad.append((None, vv))
                        b.wait("scalar", sV, vv)
                        tmp_free[ibt] = b.dma(
                            "scalar", ddram[tt * 128:(tt + 1) * 128, db * 512:(db + 1) * 512],
                            tmp[ibt][:], sDO)
                b.flush()

        with ExitStack() as fes:
            dxt = [b.sb(f"fx{half}{i}", [128, D], F32, fes) for i in range(2)]
            ddt = [b.sb(f"fd{half}{i}", [128, D], F32, fes) for i in range(2)]
            x4t = [b.sb(f"fo{half}{i}", [128, D], F32, fes) for i in range(2)]
            bfree = [0, 0]
            ofree = [0, 0]
            b.wait("sync", sDO, sDO.n)
            for tt in range(off, off + 8):
                ib = tt % 2
                b.wait("sync", sV, bfree[ib])
                d1 = b.dma("sync", dxt[ib][:], xin[tt * 128:(tt + 1) * 128, :], sDX)
                d2 = b.dma("sync", ddt[ib][:], ddram[tt * 128:(tt + 1) * 128, :], sDX)
                b.wait("vector", sDX, d2)
                b.wait("vector", sDO, ofree[ib])
                bfree[ib] = b.op("vector", lambda e, ib=ib: e.tensor_tensor(
                    x4t[ib][:], dxt[ib][:], ddt[ib][:], OP.add), sV)
                b.wait("scalar", sV, bfree[ib])
                ofree[ib] = b.dma("scalar", xout[tt * 128:(tt + 1) * 128, :], x4t[ib][:], sDO)
            b.wait("scalar", sDO, sDO.n)
            b.flush()


P4_INPUTS = ["ident_bf", "ident_f32", "l1_wg", "l1_wu", "l1_wd", "l1_sg", "l1_su", "l1_sd"]


def build_p4_standalone():
    nc = bass.Bass("TRN2", target_bir_lowering=False)
    A = declare_shared(nc, P4_INPUTS)
    xin = nc.dram_tensor("xin", [T, D], F32, kind="ExternalInput").ap()
    xout = nc.dram_tensor("xout", [T, D], F32, kind="ExternalOutput").ap()
    ddram = nc.dram_tensor("ddram", [T, D], F32, kind="Internal").ap()
    with ExitStack() as es:
        emit_p4_l1ffn(nc, es, A, xin, xout, ddram)
    return nc


P3_INPUTS = ["ident_bf", "ident_f32", "masksT", "l1_wqkv", "l1_wproj"]


def build_p3_standalone(ntiles=NT):
    nc = bass.Bass("TRN2", target_bir_lowering=False)
    A = declare_shared(nc, P3_INPUTS)
    xin = nc.dram_tensor("xin", [T, D], F32, kind="ExternalInput").ap()
    xout = nc.dram_tensor("xout", [T, D], F32, kind="ExternalOutput").ap()
    with ExitStack() as es:
        emit_p3_l1attn(nc, es, A, xin, xout, ntiles=ntiles)
    return nc


P1_INPUTS = ["ident_bf", "ident_f32", "masksT", "l0_wq", "l0_wk", "l0_wv", "l0_wo",
             "l0_lng", "l0_lnb", "l0_bq", "l0_bk", "l0_bv", "l0_bo"]


def build_p1_standalone(ntiles=NT):
    nc = bass.Bass("TRN2", target_bir_lowering=False)
    A = declare_shared(nc, P1_INPUTS)
    xin = nc.dram_tensor("xin", [T, D], F32, kind="ExternalInput").ap()
    xout = nc.dram_tensor("xout", [T, D], F32, kind="ExternalOutput").ap()
    with ExitStack() as es:
        emit_p1_l0attn(nc, es, A, xin, xout, ntiles=ntiles)
    return nc



def _prep_shared(inp):
    A = {}
    A["ident_bf"] = np.eye(128, dtype=BFNP)
    A["ident_f32"] = np.eye(128, dtype=np.float32)
    p = np.arange(128)[:, None]
    mcol = np.arange(896)[None, :]
    A["masks"] = np.where(mcol <= p + 384, 0.0, NEG).astype(np.float32)
    A["masksT"] = np.where(mcol >= p + 384, 0.0, NEG).astype(np.float32)

    def colchunks(v):
        return np.ascontiguousarray(v.reshape(-1, 128).T)

    A["l0_wq"] = np.ascontiguousarray(
        inp["L0_attn_Wq"].reshape(16, 128, 16, 128).transpose(2, 1, 0, 3).reshape(2048, 2048))
    A["l0_wk"] = np.ascontiguousarray(
        inp["L0_attn_Wk"].reshape(16, 128, 16, 128).transpose(2, 1, 0, 3).reshape(2048, 2048))
    A["l0_wv"] = np.ascontiguousarray(
        inp["L0_attn_Wv"].reshape(16, 128, 4, 512).transpose(2, 1, 0, 3).reshape(512, 8192))
    A["l0_wo"] = np.ascontiguousarray(inp["L0_attn_Wo"])
    A["l0_lng"] = colchunks(inp["L0_attn_ln_g"])
    A["l0_lnb"] = colchunks(inp["L0_attn_ln_b"])
    A["l0_bq"] = colchunks(inp["L0_attn_bq"])
    A["l0_bk"] = colchunks(inp["L0_attn_bk"])
    A["l0_bv"] = colchunks(inp["L0_attn_bv"])
    A["l0_bo"] = np.ascontiguousarray(inp["L0_attn_bo"].reshape(1, 2048))
    A["l0f_w"] = np.ascontiguousarray(inp["L0_ffn_weight"].T)
    A["l0f_gam"] = colchunks(inp["L0_ffn_ln_gamma"])
    A["l0f_bet"] = colchunks(inp["L0_ffn_ln_beta"])
    A["l0f_bias"] = np.ascontiguousarray(inp["L0_ffn_bias"].reshape(1, 2048))
    A["l1_wqkv"] = np.ascontiguousarray(
        inp["L1_attn_weight_qkv"].T.reshape(16, 128, 24, 128).transpose(2, 1, 0, 3).reshape(3072, 2048))
    A["l1_wproj"] = np.ascontiguousarray(inp["L1_attn_weight_proj"].T)
    A["l1_wg"] = np.ascontiguousarray(
        inp["L1_ffn_wq_gate"].T.reshape(16, 128, 44, 128).transpose(2, 1, 0, 3).reshape(5632, 2048))
    A["l1_wu"] = np.ascontiguousarray(
        inp["L1_ffn_wq_up"].T.reshape(16, 128, 44, 128).transpose(2, 1, 0, 3).reshape(5632, 2048))
    A["l1_wd"] = np.ascontiguousarray(
        inp["L1_ffn_wq_down"].T.reshape(44, 128, 4, 512).transpose(2, 0, 1, 3).reshape(22528, 512))
    A["l1_sg"] = colchunks(inp["L1_ffn_s_gate"])
    A["l1_su"] = colchunks(inp["L1_ffn_s_up"])
    A["l1_sd"] = np.ascontiguousarray(inp["L1_ffn_s_down"].reshape(1, 2048))
    return A


SHARED_SPECS = {
    "ident_bf": ([128, 128], BF16), "ident_f32": ([128, 128], F32),
    "masks": ([128, 896], F32), "masksT": ([128, 896], F32),
    "l0_wq": ([2048, 2048], F32), "l0_wk": ([2048, 2048], F32),
    "l0_wv": ([512, 8192], F32), "l0_wo": ([2048, 2048], F32),
    "l0_lng": ([128, 16], F32), "l0_lnb": ([128, 16], F32),
    "l0_bq": ([128, 16], F32), "l0_bk": ([128, 16], F32), "l0_bv": ([128, 16], F32),
    "l0_bo": ([1, 2048], F32),
    "l0f_w": ([2048, 2048], F32), "l0f_gam": ([128, 16], F32),
    "l0f_bet": ([128, 16], F32), "l0f_bias": ([1, 2048], F32),
    "l1_wqkv": ([3072, 2048], F32), "l1_wproj": ([2048, 2048], F32),
    "l1_wg": ([5632, 2048], I32), "l1_wu": ([5632, 2048], I32), "l1_wd": ([22528, 512], I32),
    "l1_sg": ([128, 44], F32), "l1_su": ([128, 44], F32), "l1_sd": ([1, 2048], F32),
}


def declare_shared(nc, names):
    aps = {}
    for n in names:
        shape, dt = SHARED_SPECS[n]
        aps[n] = nc.dram_tensor(n, shape, dt, kind="ExternalInput").ap()
    return aps


P2_INPUTS = ["ident_bf", "ident_f32", "l0f_w", "l0f_gam", "l0f_bet", "l0f_bias"]


def build_p2_standalone(debug=False, ntiles=NT):
    nc = bass.Bass("TRN2", target_bir_lowering=False)
    A = declare_shared(nc, P2_INPUTS)
    xin = nc.dram_tensor("xin", [T, D], F32, kind="ExternalInput").ap()
    xout = nc.dram_tensor("xout", [T, D], F32, kind="ExternalOutput").ap()
    dbg = None
    if debug:
        dbg = {
            "rinv": nc.dram_tensor("d_rinv", [128, 1], F32, kind="ExternalOutput").ap(),
            "wq0": nc.dram_tensor("d_wq0", [128, D], BF16, kind="ExternalOutput").ap(),
            "u0": nc.dram_tensor("d_u0", [128, D], BF16, kind="ExternalOutput").ap(),
            "xnt0": nc.dram_tensor("d_xnt0", [128, D], BF16, kind="ExternalOutput").ap(),
            "biasb": nc.dram_tensor("d_biasb", [128, D], F32, kind="ExternalOutput").ap(),
            "coef": nc.dram_tensor("d_coef", [128, 8], F32, kind="ExternalOutput").ap(),
            "nb": nc.dram_tensor("d_nb", [128, 1], F32, kind="ExternalOutput").ap(),
        }
    with ExitStack() as es:
        emit_p2_l0ffn(nc, es, A, xin, xout, dbg=dbg, ntiles=ntiles)
    return nc


def build_fused():
    nc = bass.Bass("TRN2", target_bir_lowering=False)
    names = sorted(set(P1_INPUTS) | set(P2_INPUTS) | set(P3_INPUTS) | set(P4_INPUTS))
    A = declare_shared(nc, names)
    x0 = nc.dram_tensor("xin", [T, D], F32, kind="ExternalInput").ap()
    x4 = nc.dram_tensor("xout", [T, D], F32, kind="ExternalOutput").ap()
    x1 = nc.dram_tensor("x1s", [T, D], F32, kind="Internal").ap()
    x2 = nc.dram_tensor("x2s", [T, D], F32, kind="Internal").ap()
    x3 = nc.dram_tensor("x3s", [T, D], F32, kind="Internal").ap()
    dd = nc.dram_tensor("ddram", [T, D], F32, kind="Internal").ap()
    sems = {}
    with ExitStack() as ses:
        with ExitStack() as es:
            emit_p1_l0attn(nc, es, A, x0, x1, tag="p1", sems=sems, sems_es=ses)
        with ExitStack() as es:
            emit_p2_l0ffn(nc, es, A, x1, x2, tag="p2", sems=sems, sems_es=ses)
        with ExitStack() as es:
            emit_p3_l1attn(nc, es, A, x2, x3, tag="p3", sems=sems, sems_es=ses)
        with ExitStack() as es:
            emit_p4_l1ffn(nc, es, A, x3, x4, dd, tag="p4", sems=sems, sems_es=ses)
    return nc, names


_BUILD_CACHE = {}


def _get_phase_programs():
    if "phases" not in _BUILD_CACHE:
        _BUILD_CACHE["phases"] = [
            (build_p1_standalone(), P1_INPUTS),
            (build_p2_standalone(), P2_INPUTS),
            (build_p3_standalone(), P3_INPUTS),
            (build_p4_standalone(), P4_INPUTS),
        ]
    return _BUILD_CACHE["phases"]


def kernel(**inputs):
    A = _prep_shared(inputs)
    if "fused" not in _BUILD_CACHE:
        _BUILD_CACHE["fused"] = build_fused()
    nc, names = _BUILD_CACHE["fused"]
    in_maps = []
    for i in range(NCORES):
        m = {k: A[k] for k in names}
        m["xin"] = np.ascontiguousarray(inputs["L0_attn_X"][i])
        in_maps.append(m)
    res = run_bass_kernel_spmd(nc, in_maps, core_ids=list(range(NCORES)))
    return np.stack([res.results[i]["xout"] for i in range(NCORES)], axis=0)


# version 09
# speedup vs baseline: 1.4563x; 1.0291x over previous
import numpy as np
import ml_dtypes
from contextlib import ExitStack
import concourse.bass as bass
import concourse.mybir as mybir
from concourse.bass_utils import run_bass_kernel_spmd
from concourse.alu_op_type import AluOpType as OP

F32 = mybir.dt.float32
BF16 = mybir.dt.bfloat16
I32 = mybir.dt.int32
AF = mybir.ActivationFunctionType
AX = mybir.AxisListType.X
BFNP = ml_dtypes.bfloat16

ENGINES = ("sync", "tensor", "vector", "scalar", "gpsimd")
NCORES = 8
T = 2048
D = 2048
NT = T // 128
NC = D // 128
HFF = 5632
NHC = HFF // 128
INVD = 1.0 / D
EPS = 1e-5
ASCALE = 0.08838834764831845
NEG = -1.0e30


class Sem:
    __slots__ = ("h", "n")

    def __init__(self, h):
        self.h = h
        self.n = 0


class Bld:

    def __init__(self, nc, es, tag, sems=None, sems_es=None):
        self.nc, self.es, self.tag = nc, es, tag
        self.q = {e: [] for e in ENGINES}
        self._sems = sems if sems is not None else {}
        self._sems_es = sems_es
        self._uid = 0

    def fresh(self):
        self._uid += 1
        return self._uid

    def sem(self, name):
        if name not in self._sems:
            es = self._sems_es or self.es
            self._sems[name] = Sem(es.enter_context(self.nc.semaphore(f"sem_{name}_{self.tag}")))
        return self._sems[name]

    def sb(self, name, shape, dt, es=None):
        return (es or self.es).enter_context(self.nc.sbuf_tensor(f"{self.tag}_{name}", shape, dt))

    def ps(self, name, shape, dt, es=None):
        return (es or self.es).enter_context(self.nc.psum_tensor(f"{self.tag}_{name}", shape, dt))

    def op(self, eng, fn, sem=None, k=1):
        if sem is None:
            self.q[eng].append(fn)
            return None
        sem.n += k
        v = sem.n
        self.q[eng].append(lambda e, fn=fn, s=sem.h, k=k: fn(e).then_inc(s, k))
        return v

    def dma(self, eng, dst, src, sem):
        return self.op(eng, lambda e, d=dst, s=src: e.dma_start(d, s), sem, 16)

    def wait(self, eng, sem, v):
        if v:
            self.q[eng].append(lambda e, s=sem.h, v=v: e.wait_ge(s, v))

    def drain(self, eng):
        self.q[eng].append(lambda e: e.drain())

    def flush(self):
        q = self.q
        self.q = {e: [] for e in ENGINES}
        with self.nc.Block() as blk:
            for name in ENGINES:
                ops = q[name]
                if ops:
                    getattr(blk, name)(lambda eng, ops=ops: [f(eng) for f in ops])


def emit_consts(b, A):
    c = {}
    c["idb"] = b.sb("idb", [128, 128], BF16)
    c["idf"] = b.sb("idf", [128, 128], F32)
    c["ones1"] = b.sb("ones1", [1, 128], F32)
    c["epsc"] = b.sb("epsc", [128, 1], F32)
    c["onescol"] = b.sb("onescol", [128, 1], BF16)
    sDC = b.sem("dc")
    b.dma("sync", c["idb"][:], A["ident_bf"], sDC)
    b.dma("sync", c["idf"][:], A["ident_f32"], sDC)
    b.op("vector", lambda e: e.memset(c["ones1"][:], 1.0))
    b.op("vector", lambda e: e.memset(c["epsc"][:], EPS))
    b.op("vector", lambda e: e.memset(c["onescol"][:], 1.0))
    b.wait("sync", sDC, sDC.n)
    return c


def emit_bcast_row(b, c, row_sb, dst_sb, pa, width=D):
    sT, sV = b.sem("t"), b.sem("v")
    for ob in range(width // 512):
        vprev = sV.n
        b.wait("tensor", sV, vprev)
        b.op("tensor", lambda e, ob=ob: e.matmul(
            pa[:], c["ones1"][:], row_sb[:, ob * 512:(ob + 1) * 512], start=True, stop=True), sT)
        b.wait("vector", sT, sT.n)
        b.op("vector", lambda e, ob=ob: e.tensor_copy(dst_sb[:, ob * 512:(ob + 1) * 512], pa[:]), sV)


def emit_ln_stats(b, c, x_ap, bn6, mv, sc, nb, double_ln):
    sV, sS = b.sem("v"), b.sem("s")
    for i in range(4):
        b.op("vector", lambda e, i=i: e.bn_stats(bn6[:, i * 6:(i + 1) * 6], x_ap[:, i * 512:(i + 1) * 512]))
    b.drain("vector")
    vs = b.op("vector", lambda e: e.bn_aggr(mv[:], bn6[:]), sV)
    b.wait("scalar", sV, vs)
    ss = b.op("scalar", lambda e: e.activation(sc[:, 2:3], mv[:, 1:2], AF.Sqrt, bias=c["epsc"][:], scale=1.0), sS)
    b.wait("vector", sS, ss)
    b.op("vector", lambda e: e.reciprocal(sc[:, 3:4], sc[:, 2:3]))
    b.drain("vector")
    if double_ln:
        b.op("vector", lambda e: e.tensor_scalar(sc[:, 4:5], sc[:, 3:4], sc[:, 3:4], None, OP.mult))
        b.drain("vector")
        vv = b.op("vector", lambda e: e.tensor_scalar(sc[:, 5:6], mv[:, 1:2], sc[:, 4:5], None, OP.mult), sV)
        b.wait("scalar", sV, vv)
        ss = b.op("scalar", lambda e: e.activation(sc[:, 6:7], sc[:, 5:6], AF.Sqrt, bias=c["epsc"][:], scale=1.0), sS)
        b.wait("vector", sS, ss)
        b.op("vector", lambda e: e.reciprocal(sc[:, 7:8], sc[:, 6:7]))
        b.drain("vector")
        b.op("vector", lambda e: e.tensor_scalar(sc[:, 0:1], sc[:, 3:4], sc[:, 7:8], None, OP.mult))
    else:
        b.op("vector", lambda e: e.tensor_copy(sc[:, 0:1], sc[:, 3:4]))
    b.drain("vector")
    vv = b.op("vector", lambda e: e.scalar_tensor_tensor(nb[:], mv[:, 0:1], -1.0, sc[:, 0:1], OP.mult, OP.mult), sV)
    return vv


def emit_p2_l0ffn(nc, es, A, xin, xout, tag="p2", dbg=None, ntiles=NT, sems=None, sems_es=None):
    b = Bld(nc, es, tag, sems=sems, sems_es=sems_es)
    sV, sS, sT, sG = b.sem("v"), b.sem("s"), b.sem("t"), b.sem("g")
    sDC, sDW, sDX, sDO = b.sem("dc"), b.sem("dw"), b.sem("dx"), b.sem("do")

    c = emit_consts(b, A)
    gam = b.sb("gam", [128, 16], F32)
    bet = b.sb("bet", [128, 16], F32)
    brow = b.sb("brow", [1, D], F32)
    bias_b = b.sb("bias_b", [128, D], F32)
    wq = b.sb("wq", [128, NC * D], BF16)
    pa = b.ps("pa", [128, 512], F32)
    pb = b.ps("pb", [128, 512], F32)
    pta = b.ps("pta", [128, 512], BF16)
    ptb = b.ps("ptb", [128, 512], BF16)

    b.dma("sync", gam[:], A["l0f_gam"], sDC)
    b.dma("sync", bet[:], A["l0f_bet"], sDC)
    b.dma("sync", brow[:], A["l0f_bias"], sDC)
    b.wait("sync", sDC, sDC.n)
    b.flush()

    emit_bcast_row(b, c, brow, bias_b, pa)
    b.flush()

    with ExitStack() as wes:
        stg = [b.sb(f"stg{i}", [128, D], F32, wes) for i in range(2)]
        acc = b.sb("acc", [128, 16], F32, wes)
        dump = b.sb("dump", [128, D], F32, wes)
        small = b.sb("small", [1, 4], F32, wes)
        rinv = b.sb("rinv", [128, 1], F32, wes)
        tmp_col = b.sb("tmpcol", [128, 1], F32, wes)
        t1 = b.sb("t1", [128, D], BF16, wes)
        t2 = b.sb("t2", [128, D], BF16, wes)
        sDW2 = b.sem("dw2")
        for cc in range(NC):
            qe = "sync" if cc % 2 == 0 else "scalar"
            qs = sDW if cc % 2 == 0 else sDW2
            if cc >= 2:
                b.wait(qe, sS, sS.n)
            dv = b.dma(qe, stg[cc % 2][:], A["l0f_w"][cc * 128:(cc + 1) * 128, :], qs)
            b.wait("scalar", qs, dv)
            b.op("scalar", lambda e, cc=cc: e.activation(
                dump[:], stg[cc % 2][:], AF.Abs, accum_out=acc[:, cc:cc + 1]), sS)
        b.flush()
        b.op("vector", lambda e: e.reduce_sum(tmp_col[:], acc[:], axis=AX))
        b.drain("vector")
        vv = b.op("vector", lambda e: e.engine_nop(), sV)
        b.wait("tensor", sV, vv)
        tv = b.op("tensor", lambda e: e.transpose(pa[0:1, 0:128], tmp_col[:], c["idf"][:]), sT)
        b.wait("vector", sT, tv)
        b.op("vector", lambda e: e.reduce_sum(small[:, 0:1], pa[0:1, 0:128], axis=AX))
        b.drain("vector")
        b.op("vector", lambda e: e.tensor_scalar(small[:, 1:2], small[:, 0:1], 1.0 / (D * D), EPS, OP.mult, OP.add))
        b.drain("vector")
        vv = b.op("vector", lambda e: e.reciprocal(small[:, 2:3], small[:, 1:2]), sV)
        b.wait("tensor", sV, vv)
        tv = b.op("tensor", lambda e: e.matmul(pb[:, 0:1], c["ones1"][:], small[:, 2:3], start=True, stop=True), sT)
        b.wait("vector", sT, tv)
        b.op("vector", lambda e: e.tensor_copy(rinv[:], pb[:, 0:1]))
        b.drain("vector")
        b.flush()
        for cc in range(NC):
            qe = "sync" if cc % 2 == 0 else "scalar"
            qs = sDW if cc % 2 == 0 else sDW2
            if cc >= 2:
                b.wait(qe, sV, sV.n)
            dv = b.dma(qe, stg[cc % 2][:], A["l0f_w"][cc * 128:(cc + 1) * 128, :], qs)
            b.wait("vector", qs, dv)
            b.op("vector", lambda e, cc=cc: e.tensor_scalar(
                t1[:], stg[cc % 2][:], rinv[:], 0.5, OP.mult, OP.is_gt))
            b.drain("vector")
            b.op("vector", lambda e, cc=cc: e.tensor_scalar(
                t2[:], stg[cc % 2][:], rinv[:], -0.5, OP.mult, OP.is_lt))
            b.drain("vector")
            b.op("vector", lambda e, cc=cc: e.tensor_tensor(
                wq[:, cc * D:(cc + 1) * D], t1[:], t2[:], OP.subtract), sV)
        b.flush()
        if dbg is not None:
            b.dma("scalar", dbg["rinv"], rinv[:], sDO)
            b.dma("scalar", dbg["wq0"], wq[:, 0:D], sDO)
            b.wait("scalar", sDO, sDO.n)
            b.flush()

    xt = [b.sb(f"xt{i}", [128, D], F32) for i in range(2)]
    ub = [b.sb(f"u{i}", [128, D], BF16) for i in range(2)]
    xnt = [b.sb(f"xnt{i}", [128, D], BF16) for i in range(2)]
    xb = [b.sb(f"xb{i}", [128, D], F32) for i in range(2)]
    x2t = [b.sb(f"x2t{i}", [128, D], F32) for i in range(2)]
    bn6 = b.sb("bn6", [128, 24], F32)
    mv = b.sb("mv", [128, 2], F32)
    scc = b.sb("scc", [128, 8], F32)
    nb = b.sb("nb", [128, 1], F32)
    xt_free = [(0, 0, 0), (0, 0, 0)]
    u_free = [0, 0]
    xnt_free = [0, 0]
    x2_free = [0, 0]
    xb_free = [0, 0]
    pmm = [pa, pb]
    pmm_free = [0, 0]
    ptr = [pta, ptb]
    ptr_free = [0, 0]
    last_uv = 0

    for tt in range(ntiles):
        ib = tt % 2
        b.wait("sync", sG, xt_free[ib][0])
        b.wait("sync", sV, xt_free[ib][1])
        b.wait("sync", sS, xt_free[ib][2])
        dv = b.dma("sync", xt[ib][:], xin[tt * 128:(tt + 1) * 128, :], sDX)
        b.wait("vector", sDX, dv)
        b.wait("vector", sS, last_uv)
        coeff_v = emit_ln_stats(b, c, xt[ib], bn6, mv, scc, nb, double_ln=True)
        b.wait("scalar", sDX, dv)
        b.wait("scalar", sV, coeff_v)
        b.wait("scalar", sT, u_free[ib])
        uv = b.op("scalar", lambda e, ib=ib: e.activation(
            ub[ib][:], xt[ib][:], AF.Identity, bias=nb[:], scale=scc[:, 0:1]), sS)
        last_uv = uv
        b.wait("gpsimd", sDX, dv)
        b.wait("gpsimd", sV, xb_free[ib])
        gv = b.op("gpsimd", lambda e, ib=ib: e.tensor_tensor(
            xb[ib][:], xt[ib][:], bias_b[:], OP.add), sG)
        xt_free[ib] = (gv, coeff_v, uv)
        b.wait("tensor", sS, uv)
        b.wait("vector", sT, xnt_free[ib])
        for grp in range(4):
            pg = grp % 2
            b.wait("tensor", sV, ptr_free[pg])
            for j in range(4):
                cc = grp * 4 + j
                tv = b.op("tensor", lambda e, cc=cc, pg=pg, j=j, ib=ib: e.transpose(
                    ptr[pg][:, j * 128:(j + 1) * 128], ub[ib][:, cc * 128:(cc + 1) * 128], c["idb"][:]),
                    sT if j == 3 else None)
            if grp == 3:
                u_free[ib] = sT.n
            b.wait("vector", sT, sT.n)
            for j in range(4):
                cc = grp * 4 + j
                ev = b.op("vector", lambda e, cc=cc, pg=pg, j=j, ib=ib: e.tensor_scalar(
                    xnt[ib][:, cc * 128:(cc + 1) * 128], ptr[pg][:, j * 128:(j + 1) * 128],
                    gam[:, cc:cc + 1], bet[:, cc:cc + 1], OP.mult, OP.add), sV)
            ptr_free[pg] = sV.n
        xnt_ready = sV.n
        b.wait("tensor", sV, xnt_ready)
        for ob in range(4):
            pg = ob % 2
            b.wait("tensor", sV, pmm_free[pg])
            for cc in range(NC):
                mmv = b.op("tensor", lambda e, cc=cc, ob=ob, pg=pg, ib=ib: e.matmul(
                    pmm[pg][:], xnt[ib][:, cc * 128:(cc + 1) * 128],
                    wq[:, cc * D + ob * 512: cc * D + (ob + 1) * 512],
                    start=(cc == 0), stop=(cc == NC - 1)), sT if cc == NC - 1 else None)
            b.wait("vector", sT, mmv)
            b.wait("vector", sG, xt_free[ib][0])
            b.wait("vector", sDO, x2_free[ib])
            pmm_free[pg] = b.op("vector", lambda e, ob=ob, pg=pg, ib=ib: e.tensor_tensor(
                x2t[ib][:, ob * 512:(ob + 1) * 512], pmm[pg][:], xb[ib][:, ob * 512:(ob + 1) * 512], OP.add), sV)
        xb_free[ib] = sV.n
        xnt_free[ib] = sT.n
        b.wait("scalar", sV, sV.n)
        x2_free[ib] = b.dma("scalar", xout[tt * 128:(tt + 1) * 128, :], x2t[ib][:], sDO)
        if dbg is not None and tt == 0:
            b.dma("scalar", dbg["u0"], ub[0][:], sDO)
            b.dma("scalar", dbg["xnt0"], xnt[0][:], sDO)
            b.dma("scalar", dbg["biasb"], bias_b[:], sDO)
            b.dma("scalar", dbg["coef"], scc[:], sDO)
            b.dma("scalar", dbg["nb"], nb[:], sDO)
    b.wait("scalar", sDO, sDO.n)
    b.flush()


def emit_ln_phase(b, c, xin_ap, col_of, xnt_sb, gamcol, betcol, double_ln, tiles, les, ptpair):
    sV, sS, sT, sDX = b.sem("v"), b.sem("s"), b.sem("t"), b.sem("dx")
    sS2 = b.sem("v2")
    u_ = b.fresh()
    xt = [b.sb(f"lnxt{i}_{u_}", [128, D], F32, les) for i in range(2)]
    ub = [b.sb(f"lnu{i}_{u_}", [128, D], BF16, les) for i in range(2)]
    bn6 = b.sb(f"lnbn6_{u_}", [128, 24], F32, les)
    mv = b.sb(f"lnmv_{u_}", [128, 2], F32, les)
    scc = b.sb(f"lnscc_{u_}", [128, 8], F32, les)
    nb = b.sb(f"lnnb_{u_}", [128, 1], F32, les)
    ptr = ptpair
    ptr_free = [0, 0]
    xt_free = [(0, 0), (0, 0)]
    u_free = [0, 0]
    last_uv = 0
    tile_done = {}
    for ti, tt in enumerate(tiles):
        ib = ti % 2
        b.wait("sync", sV, xt_free[ib][0])
        b.wait("sync", sS, xt_free[ib][1])
        dv = b.dma("sync", xt[ib][:], xin_ap[tt * 128:(tt + 1) * 128, :], sDX)
        b.wait("vector", sDX, dv)
        b.wait("vector", sS, last_uv)
        coeff_v = emit_ln_stats(b, c, xt[ib], bn6, mv, scc, nb, double_ln=double_ln)
        b.wait("scalar", sDX, dv)
        b.wait("scalar", sV, coeff_v)
        b.wait("scalar", sT, u_free[ib])
        uv = b.op("scalar", lambda e, ib=ib: e.activation(
            ub[ib][:], xt[ib][:], AF.Identity, bias=nb[:], scale=scc[:, 0:1]), sS)
        last_uv = uv
        xt_free[ib] = (coeff_v, uv)
        b.wait("tensor", sS, uv)
        for grp in range(4):
            pg = grp % 2
            b.wait("tensor", sS2, ptr_free[pg])
            for j in range(4):
                cc = grp * 4 + j
                b.op("tensor", lambda e, cc=cc, pg=pg, j=j, ib=ib: e.transpose(
                    ptr[pg][:, j * 128:(j + 1) * 128], ub[ib][:, cc * 128:(cc + 1) * 128], c["idb"][:]),
                    sT if j == 3 else None)
            if grp == 3:
                u_free[ib] = sT.n
            b.wait("vector", sT, sT.n)
            for j in range(4):
                cc = grp * 4 + j
                if gamcol is not None:
                    b.op("vector", lambda e, cc=cc, pg=pg, j=j, tt=tt: e.tensor_scalar(
                        xnt_sb[:, col_of(cc, tt):col_of(cc, tt) + 128], ptr[pg][:, j * 128:(j + 1) * 128],
                        gamcol(cc), betcol(cc), OP.mult, OP.add), sS2)
                else:
                    b.op("vector", lambda e, cc=cc, pg=pg, j=j, tt=tt: e.tensor_copy(
                        xnt_sb[:, col_of(cc, tt):col_of(cc, tt) + 128],
                        ptr[pg][:, j * 128:(j + 1) * 128]), sS2)
            ptr_free[pg] = sS2.n
        tile_done[tt] = sS2.n
    return tile_done


def emit_head_bound(b, c, st, qh, kh, sqpair, nmx, mcol, bufs, ready):
    sV, sS, sT = b.sem("v"), b.sem("s"), b.sem("t")
    psr, pav, pbc, pbb = bufs["psr"], bufs["pav"], bufs["pbc"], bufs["pbb"]
    sqa, sqb_ = sqpair
    regions = [p[0:1, 0:512] for p in psr] + [pav[0:1, 0:512], pbc[0:1, 0:512], pbb[0:1, 0:512]]
    b.wait("scalar", sS, ready)
    b.wait("scalar", sT, st["nmm"])
    b.op("scalar", lambda e: e.activation(sqa[:], qh[:], AF.Square))
    sqv = b.op("scalar", lambda e: e.activation(sqb_[:], kh[:], AF.Square), sS)
    b.wait("tensor", sS, sqv)
    b.wait("tensor", sV, st["pav_ev"])
    b.wait("tensor", sV, st["lrow_ev"])
    b.wait("tensor", sV, st["linv_ev"])
    for bk in range(len(psr)):
        b.wait("tensor", sS, st["psfree"][bk])
    nreg = len(regions)
    done = 0
    rv = 0
    for i in range(8):
        sq_ = sqa if i < 4 else sqb_
        tb = i % 4
        reg = regions[i % nreg]
        if i >= nreg:
            b.wait("tensor", sV, rv)
        st["nmm"] = b.op("tensor", lambda e, sq_=sq_, tb=tb, reg=reg: e.matmul(
            reg, c["onescol"][:], sq_[:, tb * 512:(tb + 1) * 512],
            start=True, stop=True), sT)
        if i == min(nreg, 8) - 1 or i == 7:
            b.wait("vector", sT, st["nmm"])
            for j in range(done, i + 1):
                rv = b.op("vector", lambda e, j=j, reg=regions[j % nreg]: e.reduce_max(
                    nmx[:, j:j + 1], reg, axis=AX), sV)
            done = i + 1
    st["normv"] = rv
    b.wait("vector", sV, rv)
    b.op("vector", lambda e: e.reduce_max(nmx[:, 8:9], nmx[:, 0:4], axis=AX))
    b.drain("vector")
    b.op("vector", lambda e: e.reduce_max(nmx[:, 9:10], nmx[:, 4:8], axis=AX))
    b.drain("vector")
    b.op("vector", lambda e: e.tensor_tensor(nmx[:, 10:11], nmx[:, 8:9], nmx[:, 9:10], OP.mult))
    b.drain("vector")
    vv = b.op("vector", lambda e: e.engine_nop(), sV)
    b.wait("scalar", sV, vv)
    sv = b.op("scalar", lambda e: e.activation(nmx[:, 11:12], nmx[:, 10:11], AF.Sqrt), sS)
    b.wait("vector", sS, sv)
    vv = b.op("vector", lambda e: e.tensor_scalar(nmx[:, 12:13], nmx[:, 11:12], -ASCALE, None, OP.mult), sV)
    b.wait("tensor", sV, vv)
    tv = b.op("tensor", lambda e: e.matmul(pbb[:, 0:1], c["ones1"][:], nmx[:, 12:13],
                                           start=True, stop=True), sT)
    b.wait("vector", sT, tv)
    mv = b.op("vector", lambda e: e.tensor_copy(mcol[:], pbb[:, 0:1]), sV)
    st["linv_ev"] = max(st["linv_ev"], mv)
    return mv


def emit_attn_head(b, c, st, qh, kh, vtile, avt_slice, bufs, ready, mcol, mrdy, filler=None):
    sV, sS, sT = b.sem("v"), b.sem("s"), b.sem("t")
    psr, pav, pbc, pbb = bufs["psr"], bufs["pav"], bufs["pbc"], bufs["pbb"]
    nb_ = len(psr)
    PT, mskT = bufs["PT"], bufs["mskT"]
    lrow, ltmp, invl, invlb = bufs["lrow"], bufs["ltmp"], bufs["invl"], bufs["invlb"]
    b.wait("tensor", sS, ready)
    first_exp = True
    for qb in range(4):
        if filler is not None:
            filler(2)
        nk = 4 * (qb + 1)
        mmv = {}
        for kk in range(nk + 2):
            if kk < nk:
                bk = kk % nb_
                b.wait("tensor", sS, st["psfree"][bk])
                if kk == 0:
                    b.wait("tensor", sV, st["lrow_ev"])
                    b.wait("tensor", sV, st["pav_ev"])
                    b.wait("tensor", sV, st["normv"])
                v = b.op("tensor", lambda e, bk=bk, kk=kk, qb=qb: e.matmul(
                    psr[bk][:], kh[:, kk * 128:(kk + 1) * 128], qh[:, qb * 512:(qb + 1) * 512],
                    start=True, stop=True), sT)
                mmv[kk] = v
                diag = kk >= 4 * qb
                if diag:
                    r = kk - 4 * qb
                    b.wait("vector", sT, v)
                    mmv[kk] = ("v", b.op("vector", lambda e, bk=bk, r=r: e.tensor_tensor(
                        psr[bk][:], psr[bk][:], mskT[:, (3 - r) * 128:(3 - r) * 128 + 512], OP.add), sV))
                if first_exp:
                    b.wait("scalar", sV, mrdy)
                    first_exp = False
                m = mmv[kk]
                if isinstance(m, tuple):
                    b.wait("scalar", sV, m[1])
                else:
                    b.wait("scalar", sT, m)
                b.wait("scalar", sT, st["PTmm"][kk % 3])
                st["psfree"][bk] = b.op("scalar", lambda e, bk=bk, kk=kk: e.activation(
                    PT[kk % 3][:], psr[bk][:], AF.Exp, bias=mcol[:], scale=ASCALE), sS)
                st["pexp"][kk % 3] = st["psfree"][bk]
            j = kk - 2
            if 0 <= j < nk:
                b.wait("tensor", sS, st["pexp"][j % 3])
                b.op("tensor", lambda e, j=j, nk=nk: e.matmul(
                    pav[:], vtile(j), PT[j % 3][:], start=(j == 0), stop=(j == nk - 1)))
                st["PTmm"][j % 3] = b.op("tensor", lambda e, j=j, nk=nk: e.matmul(
                    pbc[0:1, 0:512], c["onescol"][:], PT[j % 3][:],
                    start=(j == 0), stop=(j == nk - 1)), sT)
        lv = st["PTmm"][(nk - 1) % 3]
        b.wait("vector", sT, lv)
        st["lrow_ev"] = b.op("vector", lambda e: e.tensor_copy(lrow[:], pbc[0:1, 0:512]), sV)
        b.wait("scalar", sV, st["lrow_ev"])
        b.op("scalar", lambda e: e.activation(ltmp[:], lrow[:], AF.Ln))
        lx = b.op("scalar", lambda e: e.activation(invl[:], ltmp[:], AF.Exp, scale=-1.0), sS)
        b.wait("tensor", sS, lx)
        b.wait("tensor", sV, st["linv_ev"])
        bcv = b.op("tensor", lambda e: e.matmul(pbb[:], c["ones1"][:], invl[:],
                                                start=True, stop=True), sT)
        b.wait("vector", sT, bcv)
        st["linv_ev"] = b.op("vector", lambda e: e.tensor_copy(invlb[:], pbb[:]), sV)
        st["pav_ev"] = b.op("vector", lambda e, qb=qb: e.tensor_tensor(
            avt_slice(qb), pav[:], invlb[:], OP.mult), sV)


def emit_p1_l0attn(nc, es, A, xin, xout, tag="p1", ntiles=NT, sems=None, sems_es=None):
    b = Bld(nc, es, tag, sems=sems, sems_es=sems_es)
    sV, sS, sT, sG = b.sem("v"), b.sem("s"), b.sem("t"), b.sem("g")
    sDC, sDX, sDO = b.sem("dc"), b.sem("dx"), b.sem("do")
    sWA = b.sem("wa")

    c = emit_consts(b, A)
    lng = b.sb("lng", [128, 16], F32)
    lnb = b.sb("lnb", [128, 16], F32)
    bqc = b.sb("bqc", [128, 16], F32)
    bkc = b.sb("bkc", [128, 16], F32)
    bvb = b.sb("bvb", [128, 16], BF16)
    msk = b.sb("msk", [128, 896], F32)
    for dst, src_ in [(lng, A["l0_lng"]), (lnb, A["l0_lnb"]), (bqc, A["l0_bq"]),
                      (bkc, A["l0_bk"]), (msk, A["masksT"])]:
        b.dma("sync", dst[:], src_, sDC)
    b.dma("gpsimd", bvb[:], A["l0_bv"], sDC)
    b.wait("sync", sDC, sDC.n)
    b.flush()

    pbc = b.ps("pbc", [128, 512], F32)
    pav = b.ps("pav", [128, 512], F32)

    AVT = b.sb("avt", [128, NC * T], BF16)

    st = dict(psfree=[0] * 6, PTmm=[0, 0, 0], pexp=[0, 0, 0], pav_ev=0,
              lrow_ev=0, linv_ev=0, nmm=0, nred=0, normv=0, ptevac=[0, 0])

    with ExitStack() as xes:
        xnt = b.sb("xnt", [128, NC * T], BF16, xes)
        with ExitStack() as les:
            pta = b.ps("lnpta", [128, 512], BF16, les)
            ptb = b.ps("lnptb", [128, 512], BF16, les)
            tile_done = emit_ln_phase(
                b, c, xin, lambda cc, tt: cc * T + tt * 128, xnt,
                lambda cc: lng[:, cc:cc + 1], lambda cc: lnb[:, cc:cc + 1],
                True, range(ntiles), les, (pta, ptb))
            xnt_done = max(tile_done.values())
            b.flush()

        with ExitStack() as aes:
            psr = [b.ps(f"psr{i}", [128, 512], F32, aes) for i in range(5)]
            ps = psr
            pbb = b.ps("pbb", [128, 512], F32, aes)
            nmx = b.sb("nmx", [1, 16], F32, aes)
            mcol = b.sb("mcol", [128, 1], F32, aes)
            cmx = b.sb("cmx", [128, 2], BF16, aes)
            lrowb = b.sb("lrowb", [1, 1536], F32, aes)
            lrow, ltmp, invl = lrowb[:, 0:512], lrowb[:, 512:1024], lrowb[:, 1024:1536]
            invlb = b.sb("invlb", [128, 512], F32, aes)
            PT = [b.sb(f"PT{i}", [128, 512], BF16, aes) for i in range(3)]
            V_sb = b.sb("vsb", [128, 16 * 512], BF16, aes)
            qh = [b.sb(f"qh{i}", [128, T], BF16, aes) for i in range(2)]
            kh = [b.sb(f"kh{i}", [128, T], BF16, aes) for i in range(2)]
            wq_b = [b.sb(f"wqb{i}", [128, D], BF16, aes) for i in range(2)]
            wk_b = [b.sb(f"wkb{i}", [128, D], BF16, aes) for i in range(2)]
            wvg = b.sb("wvg", [128, 4096], BF16, aes)
            bufs = dict(psr=psr, pbc=pbc, pav=pav, pbb=pbb, PT=PT, mskT=msk, cmx=cmx,
                        lrow=lrow, ltmp=ltmp, invl=invl, invlb=invlb)

            kv_of = {}
            mms_of = {}
            sqread_of = {}
            pending = []

            def emit_qk_dmas(h):
                ib = h % 2
                gate = max(mms_of.get(h - 2, 0), sqread_of.get(h - 2, 0))
                b.wait("gpsimd", sT, gate)
                dq = b.dma("gpsimd", wq_b[ib][:], A["l0_wq"][h * 128:(h + 1) * 128, :], sWA)
                dk = b.dma("gpsimd", wk_b[ib][:], A["l0_wk"][h * 128:(h + 1) * 128, :], sWA)
                return dq, dk

            def make_block(h, which, tb, dv):
                ib = h % 2
                wbuf = wq_b[ib] if which == "q" else wk_b[ib]
                dst = qh[ib] if which == "q" else kh[ib]
                bias = bqc if which == "q" else bkc

                def blk():
                    pg = tb % 2
                    b.wait("tensor", sWA, dv)
                    b.wait("tensor", sS, st["psfree"][pg])
                    mmv = None
                    for cc in range(NC):
                        mmv = b.op("tensor", lambda e, cc=cc, tb=tb, pg=pg, wbuf=wbuf: e.matmul(
                            ps[pg][:], wbuf[:, cc * 128:(cc + 1) * 128],
                            xnt[:, cc * T + tb * 512: cc * T + tb * 512 + 512],
                            start=(cc == 0), stop=(cc == NC - 1)), sT if cc == NC - 1 else None)
                    mms_of[h] = max(mms_of.get(h, 0), mmv)
                    b.wait("scalar", sT, mmv)
                    st["psfree"][pg] = b.op("scalar",
                        lambda e, tb=tb, pg=pg, dst=dst, bias=bias, h=h: e.activation(
                            dst[:, tb * 512:(tb + 1) * 512], ps[pg][:], AF.Identity,
                            bias=bias[:, h:h + 1], scale=1.0), sS)
                    kv_of[h] = max(kv_of.get(h, 0), st["psfree"][pg])
                return blk

            def queue_head(h):
                dq, dk = emit_qk_dmas(h)
                for which, dv in (("q", dq), ("k", dk)):
                    for tb in range(4):
                        pending.append(make_block(h, which, tb, dv))

            def fill(n):
                for _ in range(min(n, len(pending))):
                    pending.pop(0)()

            queue_head(0)
            fill(8)
            for g in range(4):
                vev = [st["psfree"][0], st["psfree"][1]]
                for h2 in range(2):
                    dvv = b.dma("gpsimd", wvg[:],
                                A["l0_wv"][(g * 2 + h2) * 128:(g * 2 + h2 + 1) * 128, :], sWA)
                    b.wait("tensor", sWA, dvv)
                    for tt in range(16):
                        pg = tt % 2
                        b.wait("tensor", sS, vev[pg])
                        mmv = None
                        for cc in range(NC):
                            mmv = b.op("tensor", lambda e, cc=cc, tt=tt, pg=pg: e.matmul(
                                ps[pg][:, 0:256], xnt[:, cc * T + tt * 128: cc * T + (tt + 1) * 128],
                                wvg[:, cc * 256:(cc + 1) * 256],
                                start=(cc == 0), stop=(cc == NC - 1)), sT if cc == NC - 1 else None)
                        b.wait("scalar", sT, mmv)
                        vev[pg] = b.op("scalar", lambda e, tt=tt, pg=pg, h2=h2: e.activation(
                            V_sb[:, tt * 512 + h2 * 256: tt * 512 + h2 * 256 + 256],
                            ps[pg][:, 0:256], AF.Copy), sS)
                    if h2 == 0:
                        b.wait("gpsimd", sT, sT.n)
                st["psfree"][0], st["psfree"][1] = vev
                vprod_ev = sS.n
                for hh in range(4):
                    h = g * 4 + hh
                    fill(8)
                    if h + 1 < 16:
                        queue_head(h + 1)
                    ib = h % 2
                    mrdy = emit_head_bound(b, c, st, qh[ib], kh[ib],
                                           (wq_b[ib], wk_b[ib]), nmx, mcol, bufs,
                                           max(kv_of[h], vprod_ev))
                    sqread_of[h] = st["nmm"]
                    emit_attn_head(
                        b, c, st, qh[ib], kh[ib],
                        lambda kk, hh=hh, V_sb=V_sb: V_sb[:, kk * 512 + hh * 128: kk * 512 + hh * 128 + 128],
                        lambda qb, h=h: AVT[:, h * T + qb * 512: h * T + (qb + 1) * 512],
                        bufs, max(kv_of[h], vprod_ev), mcol, mrdy, filler=fill)
                b.flush()

    with ExitStack() as oes:
        ps = [b.ps(f"pso{i}", [128, 512], F32, oes) for i in range(2)]
        wo = b.sb("wo", [128, NC * D], BF16, oes)
        borow = b.sb("borow", [1, D], F32, oes)
        boeff = b.sb("boeff", [1, D], F32, oes)
        bob = b.sb("bob", [128, D], F32, oes)
        xt = [b.sb(f"oxt{i}", [128, D], F32, oes) for i in range(2)]
        xbb = [b.sb(f"oxb{i}", [128, D], F32, oes) for i in range(2)]
        x1t = [b.sb(f"ox1{i}", [128, D], F32, oes) for i in range(2)]
        b.dma("sync", borow[:], A["l0_bo"], sDC)
        b.wait("sync", sDC, sDC.n)
        wov = []
        for cc in range(NC):
            wov.append(b.dma("gpsimd", wo[:, cc * D:(cc + 1) * D],
                             A["l0_wo"][cc * 128:(cc + 1) * 128, :], sWA))
        b.flush()
        for ob in range(4):
            mmv = None
            for cc in range(NC):
                if ob == 0:
                    b.wait("tensor", sWA, wov[cc])
                mmv = b.op("tensor", lambda e, cc=cc, ob=ob: e.matmul(
                    pbc[0:1, 0:512], bvb[:, cc:cc + 1], wo[:, cc * D + ob * 512: cc * D + (ob + 1) * 512],
                    start=(cc == 0), stop=(cc == NC - 1)), sT if cc == NC - 1 else None)
            b.wait("vector", sT, mmv)
            ev = b.op("vector", lambda e, ob=ob: e.tensor_tensor(
                boeff[:, ob * 512:(ob + 1) * 512], pbc[0:1, 0:512], borow[:, ob * 512:(ob + 1) * 512],
                OP.add), sV)
            b.wait("tensor", sV, ev)
        b.flush()
        emit_bcast_row(b, c, boeff, bob, pbc)
        b.flush()
        xt_free = [0, 0]
        x1_free = [0, 0]
        xb_free = [0, 0]
        pmm_free = [0, 0]
        for tt in range(ntiles):
            ib = tt % 2
            b.wait("sync", sG, xt_free[ib])
            dv = b.dma("sync", xt[ib][:], xin[tt * 128:(tt + 1) * 128, :], sDX)
            b.wait("gpsimd", sDX, dv)
            b.wait("gpsimd", sV, xb_free[ib])
            gv = b.op("gpsimd", lambda e, ib=ib: e.tensor_tensor(
                xbb[ib][:], xt[ib][:], bob[:], OP.add), sG)
            xt_free[ib] = gv
            for ob in range(4):
                pg = ob % 2
                b.wait("tensor", sV, pmm_free[pg])
                mmv = None
                for cc in range(NC):
                    mmv = b.op("tensor", lambda e, cc=cc, tt=tt, ob=ob, pg=pg: e.matmul(
                        ps[pg][:], AVT[:, cc * T + tt * 128: cc * T + (tt + 1) * 128],
                        wo[:, cc * D + ob * 512: cc * D + (ob + 1) * 512],
                        start=(cc == 0), stop=(cc == NC - 1)), sT if cc == NC - 1 else None)
                b.wait("vector", sT, mmv)
                b.wait("vector", sG, gv)
                b.wait("vector", sDO, x1_free[ib])
                pmm_free[pg] = b.op("vector", lambda e, ob=ob, pg=pg, ib=ib: e.tensor_tensor(
                    x1t[ib][:, ob * 512:(ob + 1) * 512], ps[pg][:], xbb[ib][:, ob * 512:(ob + 1) * 512],
                    OP.add), sV)
            xb_free[ib] = sV.n
            b.wait("scalar", sV, sV.n)
            x1_free[ib] = b.dma("scalar", xout[tt * 128:(tt + 1) * 128, :], x1t[ib][:], sDO)
        b.wait("scalar", sDO, sDO.n)
        b.flush()


def emit_p3_l1attn(nc, es, A, xin, xout, tag="p3", ntiles=NT, sems=None, sems_es=None):
    b = Bld(nc, es, tag, sems=sems, sems_es=sems_es)
    sV, sS, sT, sG = b.sem("v"), b.sem("s"), b.sem("t"), b.sem("g")
    sDC, sDX, sDO = b.sem("dc"), b.sem("dx"), b.sem("do")
    sWA = b.sem("wa")

    c = emit_consts(b, A)
    msk = b.sb("msk", [128, 896], F32)
    b.dma("sync", msk[:], A["masksT"], sDC)
    b.wait("sync", sDC, sDC.n)
    b.flush()

    pbc = b.ps("pbc", [128, 512], F32)
    pav = b.ps("pav", [128, 512], F32)

    AVT = b.sb("avt", [128, NC * T], BF16)

    st = dict(psfree=[0] * 4, PTmm=[0, 0, 0], pexp=[0, 0, 0], pav_ev=0,
              lrow_ev=0, linv_ev=0, nmm=0, nred=0, normv=0, ptevac=[0, 0])

    with ExitStack() as aes:
        psr = [b.ps(f"psr{i}", [128, 512], F32, aes) for i in range(3)]
        ps = psr
        pbb = b.ps("pbb", [128, 512], F32, aes)
        pta = b.ps("pta", [128, 512], BF16, aes)
        ptb = b.ps("ptb", [128, 512], BF16, aes)
        nmx = b.sb("nmx", [1, 16], F32, aes)
        mcol = b.sb("mcol", [128, 1], F32, aes)
        cmx = b.sb("cmx", [128, 2], BF16, aes)
        sqa = b.sb("sqa", [128, T], BF16, aes)
        sqb_ = b.sb("sqb", [128, T], BF16, aes)
        lrowb = b.sb("lrowb", [1, 1536], F32, aes)
        lrow, ltmp, invl = lrowb[:, 0:512], lrowb[:, 512:1024], lrowb[:, 1024:1536]
        invlb = b.sb("invlb", [128, 512], F32, aes)
        PT = [b.sb(f"PT{i}", [128, 512], BF16, aes) for i in range(3)]
        bufs = dict(psr=psr, pbc=pbc, pav=pav, pbb=pbb, PT=PT, mskT=msk, cmx=cmx,
                    lrow=lrow, ltmp=ltmp, invl=invl, invlb=invlb)
        with ExitStack() as xes:
            xnt = b.sb("xnt", [128, NC * T], BF16, xes)
            with ExitStack() as les:
                tile_done = emit_ln_phase(
                    b, c, xin, lambda cc, tt: cc * T + tt * 128, xnt,
                    None, None, False, range(ntiles), les, (pta, ptb))
                xnt_done = max(tile_done.values())
                b.flush()

            wdone = 0
            for g in range(4):
                with ExitStack() as ges:
                    Qg = b.sb(f"qg{g}", [128, 4 * T], BF16, ges)
                    Kg = b.sb(f"kg{g}", [128, T], BF16, ges)
                    Vt = b.sb(f"vt{g}", [128, T], BF16, ges)
                    Vg = b.sb(f"vg{g}", [128, T], BF16, ges)
                    wb = [b.sb(f"wb{g}{i}", [128, D], BF16, ges) for i in range(2)]
                    vtv = 0
                    for r in range(6):
                        pc = 6 * g + r
                        ib = r % 2
                        b.wait("gpsimd", sT, wdone if r < 2 else wdone)
                        dq = b.dma("gpsimd", wb[ib][:], A["l1_wqkv"][pc * 128:(pc + 1) * 128, :], sWA)
                        b.wait("tensor", sWA, dq)
                        dst = (Qg[:, r * T:(r + 1) * T] if r < 4 else (Kg[:] if r == 4 else Vt[:]))
                        for tb in range(4):
                            pg = tb % 2
                            b.wait("tensor", sS, st["psfree"][pg])
                            mmv = None
                            for cc in range(NC):
                                mmv = b.op("tensor", lambda e, cc=cc, tb=tb, pg=pg, ib=ib: e.matmul(
                                    ps[pg][:], wb[ib][:, cc * 128:(cc + 1) * 128],
                                    xnt[:, cc * T + tb * 512: cc * T + tb * 512 + 512],
                                    start=(cc == 0), stop=(cc == NC - 1)), sT if cc == NC - 1 else None)
                            wdone = mmv
                            b.wait("scalar", sT, mmv)
                            st["psfree"][pg] = b.op("scalar", lambda e, tb=tb, pg=pg, dst=dst: e.activation(
                                dst[:, tb * 512:(tb + 1) * 512], ps[pg][:], AF.Copy), sS)
                        if r == 5:
                            vtv = sS.n
                    b.wait("tensor", sS, vtv)
                    ptp = [pta, ptb]
                    for g4 in range(4):
                        pg = g4 % 2
                        b.wait("tensor", sS, st["ptevac"][pg])
                        for j in range(4):
                            kk = g4 * 4 + j
                            b.op("tensor", lambda e, kk=kk, j=j, pg=pg: e.transpose(
                                ptp[pg][:, j * 128:(j + 1) * 128], Vt[:, kk * 128:(kk + 1) * 128],
                                c["idb"][:]), sT if j == 3 else None)
                        b.wait("scalar", sT, sT.n)
                        st["ptevac"][pg] = b.op("scalar", lambda e, g4=g4, pg=pg: e.activation(
                            Vg[:, g4 * 512:(g4 + 1) * 512], ptp[pg][:], AF.Copy), sS)
                    vgv = sS.n
                    for j in range(4):
                        h = g * 4 + j
                        mrdy = emit_head_bound(b, c, st, Qg[:, j * T:(j + 1) * T], Kg,
                                               (sqa, sqb_), nmx, mcol, bufs, vgv)
                        emit_attn_head(
                            b, c, st, Qg[:, j * T:(j + 1) * T], Kg,
                            lambda kk, Vg=Vg: Vg[:, kk * 128:(kk + 1) * 128],
                            lambda qb, h=h: AVT[:, h * T + qb * 512: h * T + (qb + 1) * 512],
                            bufs, vgv, mcol, mrdy)
                    b.flush()

    with ExitStack() as oes:
        ps = [b.ps(f"pso{i}", [128, 512], F32, oes) for i in range(2)]
        wo = b.sb("wo", [128, NC * D], BF16, oes)
        xt = [b.sb(f"oxt{i}", [128, D], F32, oes) for i in range(2)]
        x1t = [b.sb(f"ox1{i}", [128, D], F32, oes) for i in range(2)]
        wov = []
        for cc in range(NC):
            wov.append(b.dma("gpsimd", wo[:, cc * D:(cc + 1) * D],
                             A["l1_wproj"][cc * 128:(cc + 1) * 128, :], sWA))
        b.flush()
        xt_free = [0, 0]
        x1_free = [0, 0]
        pmm_free = [0, 0]
        for tt in range(ntiles):
            ib = tt % 2
            b.wait("sync", sV, xt_free[ib])
            dv = b.dma("sync", xt[ib][:], xin[tt * 128:(tt + 1) * 128, :], sDX)
            for ob in range(4):
                pg = ob % 2
                b.wait("tensor", sV, pmm_free[pg])
                mmv = None
                for cc in range(NC):
                    if tt == 0 and ob == 0:
                        b.wait("tensor", sWA, wov[cc])
                    mmv = b.op("tensor", lambda e, cc=cc, tt=tt, ob=ob, pg=pg: e.matmul(
                        ps[pg][:], AVT[:, cc * T + tt * 128: cc * T + (tt + 1) * 128],
                        wo[:, cc * D + ob * 512: cc * D + (ob + 1) * 512],
                        start=(cc == 0), stop=(cc == NC - 1)), sT if cc == NC - 1 else None)
                b.wait("vector", sT, mmv)
                b.wait("vector", sDX, dv)
                b.wait("vector", sDO, x1_free[ib])
                pmm_free[pg] = b.op("vector", lambda e, ob=ob, pg=pg, ib=ib: e.tensor_tensor(
                    x1t[ib][:, ob * 512:(ob + 1) * 512], ps[pg][:], xt[ib][:, ob * 512:(ob + 1) * 512],
                    OP.add), sV)
            xt_free[ib] = sV.n
            b.wait("scalar", sV, sV.n)
            x1_free[ib] = b.dma("scalar", xout[tt * 128:(tt + 1) * 128, :], x1t[ib][:], sDO)
        b.wait("scalar", sDO, sDO.n)
        b.flush()


def emit_p4_l1ffn(nc, es, A, xin, xout, ddram, tag="p4", sems=None, sems_es=None):
    b = Bld(nc, es, tag, sems=sems, sems_es=sems_es)
    sV, sS, sT, sG = b.sem("v"), b.sem("s"), b.sem("t"), b.sem("g")
    sDC, sDX, sDO, sWA, sWB = b.sem("dc"), b.sem("dx"), b.sem("do"), b.sem("wa"), b.sem("wb")
    TH = 1024

    c = emit_consts(b, A)
    sgc = b.sb("sgc", [128, 44], F32)
    suc = b.sb("suc", [128, 44], F32)
    sdrow = b.sb("sdrow", [1, D], F32)
    sdb = b.sb("sdb", [128, D], F32)
    onesc = b.sb("onesc", [128, 1], BF16)
    b.dma("sync", sgc[:], A["l1_sg"], sDC)
    b.dma("sync", suc[:], A["l1_su"], sDC)
    b.dma("sync", sdrow[:], A["l1_sd"], sDC)
    b.op("vector", lambda e: e.memset(onesc[:], 1.0))
    b.wait("sync", sDC, sDC.n)
    b.flush()

    pta = b.ps("pta", [128, 512], BF16)
    ptb = b.ps("ptb", [128, 512], BF16)
    ps = [b.ps(f"ps{i}", [128, 512], F32) for i in range(4)]
    pgu = [b.ps(f"pgu{i}", [1, 512], F32) for i in range(2)]

    emit_bcast_row(b, c, sdrow, sdb, ps[0])
    b.flush()

    for half in range(2):
        off = half * 8
        with ExitStack() as hes:
            gu = b.sb(f"gu{half}", [128, 44 * TH], BF16, hes)
            gurow = b.sb(f"gurow{half}", [1, TH], F32, hes)
            m128g = b.sb(f"m128g{half}", [128, 8], F32, hes)
            with ExitStack() as xes:
                xnt = b.sb(f"xnt{half}", [128, NC * TH], BF16, xes)
                with ExitStack() as les:
                    emit_ln_phase(
                        b, c, xin, lambda cc, tt: cc * TH + (tt - off) * 128, xnt,
                        None, None, False, range(off, off + 8), les, (pta, ptb))
                    b.flush()
                with ExitStack() as ges:
                    wstg = [b.sb(f"wsg{half}{i}", [128, D], BF16, ges) for i in range(2)]
                    wstu = [b.sb(f"wsu{half}{i}", [128, D], BF16, ges) for i in range(2)]
                    wgb = [b.sb(f"wgb{half}{i}", [128, D], BF16, ges) for i in range(2)]
                    wub = [b.sb(f"wub{half}{i}", [128, D], BF16, ges) for i in range(2)]
                    grow = b.sb(f"grow{half}", [128, TH], BF16, ges)
                    urow = b.sb(f"urow{half}", [128, TH], BF16, ges)
                    stg_free = [(0, 0), (0, 0)]
                    wgb_free = [0, 0]
                    ev_free = [0, 0, 0, 0]
                    psfree = [0, 0, 0, 0]
                    growv = [0, 0]
                    for hc in range(NHC):
                        ib = hc % 2
                        b.wait("gpsimd", sV, stg_free[ib][0])
                        dg = b.dma("gpsimd", wstg[ib][:], A["l1_wg"][hc * 128:(hc + 1) * 128, :], sWA)
                        du = b.dma("gpsimd", wstu[ib][:], A["l1_wu"][hc * 128:(hc + 1) * 128, :], sWA)
                        b.wait("vector", sWA, dg)
                        b.wait("vector", sT, wgb_free[ib])
                        b.op("vector", lambda e, ib=ib: e.tensor_scalar(
                            wgb[ib][:], wstg[ib][:], 128.0, None, OP.subtract))
                        b.drain("vector")
                        b.wait("vector", sWA, du)
                        uq = b.op("vector", lambda e, ib=ib: e.tensor_scalar(
                            wub[ib][:], wstu[ib][:], 128.0, None, OP.subtract), sV)
                        stg_free[ib] = (uq, uq)
                        b.wait("tensor", sV, uq)
                        for which, base, wsrc, scol, func in (
                                ("g", 0, wgb[ib], sgc, AF.Silu), ("u", 2, wub[ib], suc, AF.Identity)):
                            for tb in range(2):
                                pg = base + tb
                                b.wait("tensor", sS, psfree[pg])
                                mmv = None
                                for cc in range(NC):
                                    mmv = b.op("tensor", lambda e, cc=cc, tb=tb, pg=pg, wsrc=wsrc: e.matmul(
                                        ps[pg][:], wsrc[:, cc * 128:(cc + 1) * 128],
                                        xnt[:, cc * TH + tb * 512: cc * TH + tb * 512 + 512],
                                        start=(cc == 0), stop=(cc == NC - 1)), sT if cc == NC - 1 else None)
                                b.wait("scalar", sT, mmv)
                                dst = grow if which == "g" else urow
                                b.wait("scalar", sV, growv[tb])
                                psfree[pg] = b.op("scalar", lambda e, tb=tb, pg=pg, dst=dst, scol=scol, hc=hc, func=func:
                                    e.activation(dst[:, tb * 512:(tb + 1) * 512], ps[pg][:], func,
                                                 bias=0.0, scale=scol[:, hc:hc + 1]), sS)
                        wgb_free[ib] = sT.n
                        for tb in range(2):
                            b.wait("vector", sS, psfree[2 + tb])
                            growv[tb] = b.op("vector", lambda e, hc=hc, tb=tb: e.tensor_tensor(
                                gu[:, hc * TH + tb * 512: hc * TH + tb * 512 + 512],
                                grow[:, tb * 512:(tb + 1) * 512], urow[:, tb * 512:(tb + 1) * 512],
                                OP.mult), sV)
                            b.wait("tensor", sV, growv[tb])
                            b.op("tensor", lambda e, hc=hc, tb=tb: e.matmul(
                                pgu[tb][:], onesc[:], gu[:, hc * TH + tb * 512: hc * TH + tb * 512 + 512],
                                start=(hc == 0), stop=(hc == NHC - 1)), sT if hc == NHC - 1 else None)
                    gusum_mm = sT.n
                    b.flush()
            b.wait("vector", sT, gusum_mm)
            for tb in range(2):
                b.op("vector", lambda e, tb=tb: e.tensor_copy(gurow[:, tb * 512:(tb + 1) * 512], pgu[tb][:]))
            b.drain("vector")
            grv = b.op("vector", lambda e: e.engine_nop(), sV)
            b.wait("tensor", sV, grv)
            for j in range(8):
                b.op("tensor", lambda e, j=j: e.matmul(
                    ps[0][:, j:j + 1], gurow[:, j * 128:(j + 1) * 128], c["ones1"][0:1, 0:1],
                    start=True, stop=True), sT if j == 7 else None)
            gtv = sT.n
            b.wait("vector", sT, gtv)
            b.op("vector", lambda e: e.tensor_scalar(m128g[:], ps[0][:, 0:8], -128.0, None, OP.mult))
            b.drain("vector")
            m128v = b.op("vector", lambda e: e.engine_nop(), sV)
            b.flush()

            with ExitStack() as des:
                wdb = [b.sb(f"wdb{half}{i}", [128, 44 * 512], BF16, des) for i in range(2)]
                tmp = [b.sb(f"dtmp{half}{i}", [128, 512], F32, des) for i in range(2)]
                wd_used = [0, 0]
                tmp_free = [0, 0]
                dmad = []
                for db in range(4):
                    ibw = db % 2
                    b.wait("gpsimd", sT, wd_used[ibw])
                    dws = []
                    for q in range(4):
                        lo = q * 11
                        dws.append(b.dma(
                            "gpsimd",
                            wdb[ibw][:].rearrange("p (hc j) -> p hc j", j=512)[:, lo:lo + 11, :],
                            A["l1_wd"][db * 5632 + lo * 128: db * 5632 + (lo + 11) * 128, :]
                            .rearrange("(hc p) j -> p hc j", p=128),
                            sWB))
                    for tt in range(off, off + 8):
                        ti = tt - off
                        pg = 2 + ti % 2
                        b.wait("tensor", sWB, dws[-1])
                        if len(dmad) >= 2:
                            b.wait("tensor", sV, dmad[-2][1])
                        b.wait("tensor", sV, grv)
                        mmv = None
                        for hc in range(NHC):
                            mmv = b.op("tensor", lambda e, hc=hc, ti=ti, pg=pg, ibw=ibw: e.matmul(
                                ps[pg][:], gu[:, hc * TH + ti * 128: hc * TH + (ti + 1) * 128],
                                wdb[ibw][:, hc * 512:(hc + 1) * 512],
                                start=(hc == 0), stop=(hc == NHC - 1)), sT if hc == NHC - 1 else None)
                        wd_used[ibw] = mmv
                        ibt = ti % 2
                        b.wait("vector", sT, mmv)
                        b.wait("vector", sV, m128v)
                        b.wait("vector", sDO, tmp_free[ibt])
                        vv = b.op("vector", lambda e, ti=ti, pg=pg, ibt=ibt, db=db: e.scalar_tensor_tensor(
                            tmp[ibt][:], ps[pg][:], m128g[:, ti:ti + 1], sdb[:, db * 512:(db + 1) * 512],
                            OP.add, OP.mult), sV)
                        dmad.append((None, vv))
                        b.wait("scalar", sV, vv)
                        tmp_free[ibt] = b.dma(
                            "scalar", ddram[tt * 128:(tt + 1) * 128, db * 512:(db + 1) * 512],
                            tmp[ibt][:], sDO)
                b.flush()

        with ExitStack() as fes:
            dxt = [b.sb(f"fx{half}{i}", [128, D], F32, fes) for i in range(2)]
            ddt = [b.sb(f"fd{half}{i}", [128, D], F32, fes) for i in range(2)]
            x4t = [b.sb(f"fo{half}{i}", [128, D], F32, fes) for i in range(2)]
            bfree = [0, 0]
            ofree = [0, 0]
            b.wait("sync", sDO, sDO.n)
            for tt in range(off, off + 8):
                ib = tt % 2
                b.wait("sync", sV, bfree[ib])
                d1 = b.dma("sync", dxt[ib][:], xin[tt * 128:(tt + 1) * 128, :], sDX)
                d2 = b.dma("sync", ddt[ib][:], ddram[tt * 128:(tt + 1) * 128, :], sDX)
                b.wait("vector", sDX, d2)
                b.wait("vector", sDO, ofree[ib])
                bfree[ib] = b.op("vector", lambda e, ib=ib: e.tensor_tensor(
                    x4t[ib][:], dxt[ib][:], ddt[ib][:], OP.add), sV)
                b.wait("scalar", sV, bfree[ib])
                ofree[ib] = b.dma("scalar", xout[tt * 128:(tt + 1) * 128, :], x4t[ib][:], sDO)
            b.wait("scalar", sDO, sDO.n)
            b.flush()


P4_INPUTS = ["ident_bf", "ident_f32", "l1_wg", "l1_wu", "l1_wd", "l1_sg", "l1_su", "l1_sd"]


def build_p4_standalone():
    nc = bass.Bass("TRN2", target_bir_lowering=False)
    A = declare_shared(nc, P4_INPUTS)
    xin = nc.dram_tensor("xin", [T, D], F32, kind="ExternalInput").ap()
    xout = nc.dram_tensor("xout", [T, D], F32, kind="ExternalOutput").ap()
    ddram = nc.dram_tensor("ddram", [T, D], F32, kind="Internal").ap()
    with ExitStack() as es:
        emit_p4_l1ffn(nc, es, A, xin, xout, ddram)
    return nc


P3_INPUTS = ["ident_bf", "ident_f32", "masksT", "l1_wqkv", "l1_wproj"]


def build_p3_standalone(ntiles=NT):
    nc = bass.Bass("TRN2", target_bir_lowering=False)
    A = declare_shared(nc, P3_INPUTS)
    xin = nc.dram_tensor("xin", [T, D], F32, kind="ExternalInput").ap()
    xout = nc.dram_tensor("xout", [T, D], F32, kind="ExternalOutput").ap()
    with ExitStack() as es:
        emit_p3_l1attn(nc, es, A, xin, xout, ntiles=ntiles)
    return nc


P1_INPUTS = ["ident_bf", "ident_f32", "masksT", "l0_wq", "l0_wk", "l0_wv", "l0_wo",
             "l0_lng", "l0_lnb", "l0_bq", "l0_bk", "l0_bv", "l0_bo"]


def build_p1_standalone(ntiles=NT):
    nc = bass.Bass("TRN2", target_bir_lowering=False)
    A = declare_shared(nc, P1_INPUTS)
    xin = nc.dram_tensor("xin", [T, D], F32, kind="ExternalInput").ap()
    xout = nc.dram_tensor("xout", [T, D], F32, kind="ExternalOutput").ap()
    with ExitStack() as es:
        emit_p1_l0attn(nc, es, A, xin, xout, ntiles=ntiles)
    return nc



def _prep_shared(inp):
    A = {}
    A["ident_bf"] = np.eye(128, dtype=BFNP)
    A["ident_f32"] = np.eye(128, dtype=np.float32)
    p = np.arange(128)[:, None]
    mcol = np.arange(896)[None, :]
    A["masks"] = np.where(mcol <= p + 384, 0.0, NEG).astype(np.float32)
    A["masksT"] = np.where(mcol >= p + 384, 0.0, NEG).astype(np.float32)

    def colchunks(v):
        return np.ascontiguousarray(v.reshape(-1, 128).T)

    A["l0_wq"] = np.ascontiguousarray(
        inp["L0_attn_Wq"].reshape(16, 128, 16, 128).transpose(2, 1, 0, 3).reshape(2048, 2048))
    A["l0_wk"] = np.ascontiguousarray(
        inp["L0_attn_Wk"].reshape(16, 128, 16, 128).transpose(2, 1, 0, 3).reshape(2048, 2048))
    A["l0_wv"] = np.ascontiguousarray(
        inp["L0_attn_Wv"].reshape(16, 128, 4, 2, 256).transpose(2, 3, 1, 0, 4).reshape(1024, 4096))
    A["l0_wo"] = np.ascontiguousarray(inp["L0_attn_Wo"])
    A["l0_lng"] = colchunks(inp["L0_attn_ln_g"])
    A["l0_lnb"] = colchunks(inp["L0_attn_ln_b"])
    A["l0_bq"] = colchunks(inp["L0_attn_bq"])
    A["l0_bk"] = colchunks(inp["L0_attn_bk"])
    A["l0_bv"] = colchunks(inp["L0_attn_bv"])
    A["l0_bo"] = np.ascontiguousarray(inp["L0_attn_bo"].reshape(1, 2048))
    A["l0f_w"] = np.ascontiguousarray(inp["L0_ffn_weight"].T)
    A["l0f_gam"] = colchunks(inp["L0_ffn_ln_gamma"])
    A["l0f_bet"] = colchunks(inp["L0_ffn_ln_beta"])
    A["l0f_bias"] = np.ascontiguousarray(inp["L0_ffn_bias"].reshape(1, 2048))
    A["l1_wqkv"] = np.ascontiguousarray(
        inp["L1_attn_weight_qkv"].T.reshape(16, 128, 24, 128).transpose(2, 1, 0, 3).reshape(3072, 2048))
    A["l1_wproj"] = np.ascontiguousarray(inp["L1_attn_weight_proj"].T)
    A["l1_wg"] = np.ascontiguousarray(
        inp["L1_ffn_wq_gate"].T.reshape(16, 128, 44, 128).transpose(2, 1, 0, 3).reshape(5632, 2048))
    A["l1_wu"] = np.ascontiguousarray(
        inp["L1_ffn_wq_up"].T.reshape(16, 128, 44, 128).transpose(2, 1, 0, 3).reshape(5632, 2048))
    A["l1_wd"] = np.ascontiguousarray(
        inp["L1_ffn_wq_down"].T.reshape(44, 128, 4, 512).transpose(2, 0, 1, 3).reshape(22528, 512))
    A["l1_sg"] = colchunks(inp["L1_ffn_s_gate"])
    A["l1_su"] = colchunks(inp["L1_ffn_s_up"])
    A["l1_sd"] = np.ascontiguousarray(inp["L1_ffn_s_down"].reshape(1, 2048))
    return A


SHARED_SPECS = {
    "ident_bf": ([128, 128], BF16), "ident_f32": ([128, 128], F32),
    "masks": ([128, 896], F32), "masksT": ([128, 896], F32),
    "l0_wq": ([2048, 2048], F32), "l0_wk": ([2048, 2048], F32),
    "l0_wv": ([1024, 4096], F32), "l0_wo": ([2048, 2048], F32),
    "l0_lng": ([128, 16], F32), "l0_lnb": ([128, 16], F32),
    "l0_bq": ([128, 16], F32), "l0_bk": ([128, 16], F32), "l0_bv": ([128, 16], F32),
    "l0_bo": ([1, 2048], F32),
    "l0f_w": ([2048, 2048], F32), "l0f_gam": ([128, 16], F32),
    "l0f_bet": ([128, 16], F32), "l0f_bias": ([1, 2048], F32),
    "l1_wqkv": ([3072, 2048], F32), "l1_wproj": ([2048, 2048], F32),
    "l1_wg": ([5632, 2048], I32), "l1_wu": ([5632, 2048], I32), "l1_wd": ([22528, 512], I32),
    "l1_sg": ([128, 44], F32), "l1_su": ([128, 44], F32), "l1_sd": ([1, 2048], F32),
}


def declare_shared(nc, names):
    aps = {}
    for n in names:
        shape, dt = SHARED_SPECS[n]
        aps[n] = nc.dram_tensor(n, shape, dt, kind="ExternalInput").ap()
    return aps


P2_INPUTS = ["ident_bf", "ident_f32", "l0f_w", "l0f_gam", "l0f_bet", "l0f_bias"]


def build_p2_standalone(debug=False, ntiles=NT):
    nc = bass.Bass("TRN2", target_bir_lowering=False)
    A = declare_shared(nc, P2_INPUTS)
    xin = nc.dram_tensor("xin", [T, D], F32, kind="ExternalInput").ap()
    xout = nc.dram_tensor("xout", [T, D], F32, kind="ExternalOutput").ap()
    dbg = None
    if debug:
        dbg = {
            "rinv": nc.dram_tensor("d_rinv", [128, 1], F32, kind="ExternalOutput").ap(),
            "wq0": nc.dram_tensor("d_wq0", [128, D], BF16, kind="ExternalOutput").ap(),
            "u0": nc.dram_tensor("d_u0", [128, D], BF16, kind="ExternalOutput").ap(),
            "xnt0": nc.dram_tensor("d_xnt0", [128, D], BF16, kind="ExternalOutput").ap(),
            "biasb": nc.dram_tensor("d_biasb", [128, D], F32, kind="ExternalOutput").ap(),
            "coef": nc.dram_tensor("d_coef", [128, 8], F32, kind="ExternalOutput").ap(),
            "nb": nc.dram_tensor("d_nb", [128, 1], F32, kind="ExternalOutput").ap(),
        }
    with ExitStack() as es:
        emit_p2_l0ffn(nc, es, A, xin, xout, dbg=dbg, ntiles=ntiles)
    return nc


def build_fused():
    nc = bass.Bass("TRN2", target_bir_lowering=False)
    names = sorted(set(P1_INPUTS) | set(P2_INPUTS) | set(P3_INPUTS) | set(P4_INPUTS))
    A = declare_shared(nc, names)
    x0 = nc.dram_tensor("xin", [T, D], F32, kind="ExternalInput").ap()
    x4 = nc.dram_tensor("xout", [T, D], F32, kind="ExternalOutput").ap()
    x1 = nc.dram_tensor("x1s", [T, D], F32, kind="Internal").ap()
    x2 = nc.dram_tensor("x2s", [T, D], F32, kind="Internal").ap()
    x3 = nc.dram_tensor("x3s", [T, D], F32, kind="Internal").ap()
    dd = nc.dram_tensor("ddram", [T, D], F32, kind="Internal").ap()
    sems = {}
    with ExitStack() as ses:
        with ExitStack() as es:
            emit_p1_l0attn(nc, es, A, x0, x1, tag="p1", sems=sems, sems_es=ses)
        with ExitStack() as es:
            emit_p2_l0ffn(nc, es, A, x1, x2, tag="p2", sems=sems, sems_es=ses)
        with ExitStack() as es:
            emit_p3_l1attn(nc, es, A, x2, x3, tag="p3", sems=sems, sems_es=ses)
        with ExitStack() as es:
            emit_p4_l1ffn(nc, es, A, x3, x4, dd, tag="p4", sems=sems, sems_es=ses)
    return nc, names


_BUILD_CACHE = {}


def _get_phase_programs():
    if "phases" not in _BUILD_CACHE:
        _BUILD_CACHE["phases"] = [
            (build_p1_standalone(), P1_INPUTS),
            (build_p2_standalone(), P2_INPUTS),
            (build_p3_standalone(), P3_INPUTS),
            (build_p4_standalone(), P4_INPUTS),
        ]
    return _BUILD_CACHE["phases"]


def kernel(**inputs):
    A = _prep_shared(inputs)
    if "fused" not in _BUILD_CACHE:
        _BUILD_CACHE["fused"] = build_fused()
    nc, names = _BUILD_CACHE["fused"]
    in_maps = []
    for i in range(NCORES):
        m = {k: A[k] for k in names}
        m["xin"] = np.ascontiguousarray(inputs["L0_attn_X"][i])
        in_maps.append(m)
    res = run_bass_kernel_spmd(nc, in_maps, core_ids=list(range(NCORES)))
    return np.stack([res.results[i]["xout"] for i in range(NCORES)], axis=0)
